# version 02
# baseline (speedup 1.0000x reference)
import math
import numpy as np
import ml_dtypes
import concourse.bass as bass
import concourse.mybir as mybir
from concourse.bass_utils import run_bass_kernel_spmd

F32 = mybir.dt.float32
BF16 = mybir.dt.bfloat16
AF = mybir.ActivationFunctionType
ALU = mybir.AluOpType
AXL = mybir.AxisListType

SEQ = 2
T = 2048
HD = 2048
I = 4096
NH = 64
P = 64
NS = 64
KC = 4
CONV = I + 2 * NS
PROJ = I + CONV + NH
PROJ_PAD = 8448
NT_GATE = 32
NT_XBC = 33
CH = 128
G = 512
NCHUNK = G // CH
NGRP_FULL = SEQ * T // G
INTER = 8192
EPS = 1e-5

bfnp = ml_dtypes.bfloat16


def _hadamard(n):
    h = np.array([[1.0]], dtype=np.float64)
    while h.shape[0] < n:
        h = np.block([[h, h], [h, -h]])
    return np.ascontiguousarray(h.astype(np.float32))


class Sem:
    __slots__ = ("name", "n", "h")

    def __init__(self, name):
        self.name = name
        self.n = 0
        self.h = None


class Op:
    __slots__ = ("eng", "fn", "waits", "incs")

    def __init__(self, eng, fn, waits, incs):
        self.eng = eng
        self.fn = fn
        self.waits = waits
        self.incs = incs


class Builder:
    def __init__(self, cfg=None):
        self.cfg = cfg or {}
        self.ops = []
        self.sems = []

    def sem(self, name):
        s = Sem(name)
        self.sems.append(s)
        return s

    def op(self, eng, fn, waits=(), incs=()):
        winc = []
        for s, k in incs:
            s.n += k
            winc.append((s, k))
        self.ops.append(Op(eng, fn, [(s, v) for (s, v) in waits], winc))

    def dma(self, eng, out, in_, sem, waits=(), **kw):
        def fn(e, o=out, i=in_, kw2=dict(kw)):
            return e.dma_start(out=o, in_=i, **kw2)
        self.op(eng, fn, waits=waits, incs=[(sem, 16)])

    def emit(self, block):
        for eng_name in ["sync", "tensor", "vector", "scalar", "gpsimd"]:
            ops = [o for o in self.ops if o.eng == eng_name]

            def make(ops_):
                def run(eng):
                    waited = {}
                    for o in ops_:
                        for (s, v) in o.waits:
                            if waited.get(id(s), -1) < v:
                                eng.wait_ge(s.h, v)
                                waited[id(s)] = v
                        inst = o.fn(eng)
                        for (s, k) in o.incs:
                            inst.then_inc(s.h, k)
                return run
            getattr(block, eng_name)(make(ops))


def build_program(cfg):
    b = Builder(cfg)
    nc = bass.Bass("TRN2", target_bir_lowering=False, debug=False)
    NGRP = cfg.get("groups", NGRP_FULL)

    d_x = nc.dram_tensor("x", [SEQ, T, HD], F32, kind="ExternalInput").ap()
    d_win = nc.dram_tensor("w_in_T", [HD, PROJ_PAD], F32, kind="ExternalInput").ap()
    d_wop = nc.dram_tensor("w_op_T", [I, HD], F32, kind="ExternalInput").ap()
    d_wup = nc.dram_tensor("w_up_T", [HD, INTER], F32, kind="ExternalInput").ap()
    d_wgt = nc.dram_tensor("w_gt_T", [HD, INTER], F32, kind="ExternalInput").ap()
    d_wdn = nc.dram_tensor("w_dn_T", [INTER, HD], F32, kind="ExternalInput").ap()
    d_convw = nc.dram_tensor("convw_in", [128, NT_XBC * KC], F32, kind="ExternalInput").ap()
    d_convb = nc.dram_tensor("convb_in", [128, NT_XBC], F32, kind="ExternalInput").ap()
    d_lnw = nc.dram_tensor("lnw_cols", [128, 16], F32, kind="ExternalInput").ap()
    d_normw = nc.dram_tensor("normw_cols", [128, 32], F32, kind="ExternalInput").ap()
    d_su = nc.dram_tensor("su_cols", [128, 16], F32, kind="ExternalInput").ap()
    d_sudp = nc.dram_tensor("sudp_cols", [128, 64], F32, kind="ExternalInput").ap()
    d_wsu_u = nc.dram_tensor("wsu_up", [128, 64], F32, kind="ExternalInput").ap()
    d_wsu_g = nc.dram_tensor("wsu_gt", [128, 64], F32, kind="ExternalInput").ap()
    d_wsdp = nc.dram_tensor("wsdp_rep", [64, HD], F32, kind="ExternalInput").ap()
    d_dcols = nc.dram_tensor("d_cols_in", [128, 32], F32, kind="ExternalInput").ap()
    d_dtb = nc.dram_tensor("dt_bias", [NH, 1], F32, kind="ExternalInput").ap()
    d_alog = nc.dram_tensor("a_log", [NH, 1], F32, kind="ExternalInput").ap()
    d_h128 = nc.dram_tensor("h128", [128, 128], F32, kind="ExternalInput").ap()
    d_h64 = nc.dram_tensor("h64", [64, 64], F32, kind="ExternalInput").ap()
    d_causal = nc.dram_tensor("causalT", [128, 128], F32, kind="ExternalInput").ap()
    d_out = nc.dram_tensor("out", [SEQ, T, HD], F32, kind="ExternalOutput").ap()

    s_win = nc.dram_tensor("s_win", [HD, PROJ_PAD], BF16, kind="Internal").ap()
    s_wop = nc.dram_tensor("s_wop", [I, HD], BF16, kind="Internal").ap()
    s_wup = nc.dram_tensor("s_wup", [HD, INTER], BF16, kind="Internal").ap()
    s_wgt = nc.dram_tensor("s_wgt", [HD, INTER], BF16, kind="Internal").ap()
    s_wdn1 = nc.dram_tensor("s_wdn1", [INTER, HD], BF16, kind="Internal").ap()
    s_wdn = nc.dram_tensor("s_wdn", [INTER, HD], BF16, kind="Internal").ap()
    s_acum = nc.dram_tensor("s_acum", [NH, CH], F32, kind="Internal").ap()
    s_row = nc.dram_tensor("s_row", [1, 512], F32, kind="Internal").ap()

    SBUF_BASE = 16384 + 2048
    cur = [SBUF_BASE]
    AL = 64

    def salloc(name, shape, dtype, at=None):
        per = 1
        for d in shape[1:]:
            per *= d
        per *= {F32: 4, BF16: 2}[dtype]
        if at is None:
            off = cur[0]
            cur[0] = (off + per + AL - 1) // AL * AL
        else:
            off = at
            assert at >= SBUF_BASE
        return nc.alloc_sbuf_tensor_at(name, shape, dtype, offset=off).ap(), off

    x_tm, _ = salloc("x_tm", [128, 4 * HD], BF16)
    exitst, exit_off = salloc("exitst", [128, HD], F32)
    xn_stage, _ = salloc("xn_stage", [128, HD], BF16, at=exit_off)
    x1_tm, _ = salloc("x1_tm", [128, 4 * HD], BF16)
    a5, a5_off = salloc("a5", [128, 16 * 512], BF16)
    a6, _ = salloc("a6", [128, 1024], BF16)
    proj, proj_off = salloc("proj", [128, NT_GATE * 512 + NT_XBC * 515 + 512], BF16)
    GATE_COLS = NT_GATE * 512
    XBC_COLS = NT_XBC * 515
    xbcwin = proj_off + GATE_COLS * 2
    replo = [salloc("replo0", [128, 1024], F32, at=xbcwin)[0],
             salloc("replo1", [128, 1024], F32, at=xbcwin + 4096)[0]]
    scoreso = [salloc("scores0", [128, 1024], BF16, at=xbcwin + 8192)[0],
               salloc("scores1", [128, 1024], BF16, at=xbcwin + 10240)[0]]
    bdeco = [salloc("bdec0", [128, 512], BF16, at=xbcwin + 12288)[0],
             salloc("bdec1", [128, 512], BF16, at=xbcwin + 13312)[0]]
    xct, _ = salloc("xct", [128, 4096], BF16, at=xbcwin + 14336)
    cdeco = [salloc("cdec0", [64, 1024], BF16, at=xbcwin + 22528)[0],
             salloc("cdec1", [64, 1024], BF16, at=xbcwin + 24576)[0]]
    y2win, _ = salloc("y2win", [128, NT_GATE * 512], BF16, at=xbcwin)
    pp_in = [salloc("pp_in0", [128, 8192], BF16, at=proj_off)[0],
             salloc("pp_in1", [128, 8192], BF16, at=proj_off + 16384)[0]]
    pp_inf = [salloc("pp_inf0", [128, 4096], F32, at=proj_off)[0],
              salloc("pp_inf1", [128, 4096], F32, at=proj_off + 16384)[0]]
    pp_out = [salloc("pp_out0", [128, 8192], BF16, at=proj_off + 32768)[0],
              salloc("pp_out1", [128, 8192], BF16, at=proj_off + 49152)[0]]
    wsdp, _ = salloc("wsdp", [64, HD], F32, at=proj_off + 65536)

    xsy, xsy_off = salloc("xsy", [128, NT_XBC * 512], BF16)
    ub1, _ = salloc("ub1", [128, 8192], BF16, at=xsy_off)
    zfull, _ = salloc("zfull", [128, 8192], BF16, at=xsy_off)
    usg, _ = salloc("usg", [128, 2048], BF16, at=xsy_off + 16384)

    carry, _ = salloc("carry", [64, 4096], F32)
    carryb = [salloc("carryb0", [64, 512], BF16)[0],
              salloc("carryb1", [64, 512], BF16)[0]]
    dt_sp, _ = salloc("dt_sp", [64, G], F32)
    dt_spb, _ = salloc("dt_spb", [64, G], BF16)
    acum, _ = salloc("acum", [64, G], F32)
    dA_t, _ = salloc("dA_t", [64, G], F32)
    dshm, _ = salloc("dshm", [64, CH], BF16)
    dssub, _ = salloc("dssub", [64, CH], F32)
    wbuf = [salloc("wbufA", [128, 2048], BF16)[0],
            salloc("wbufB", [128, 2048], BF16)[0]]
    sg = [salloc("sgA", [128, 512], BF16)[0],
          salloc("sgB", [128, 512], BF16)[0]]
    tmpg, _ = salloc("tmpg", [128, 512], BF16)
    halo, _ = salloc("halo", [128, NT_XBC * 3], BF16)
    cols, _ = salloc("cols", [128, 32], F32)
    consts, _ = salloc("consts", [128, 16], F32)
    lnw, _ = salloc("lnw", [128, 16], F32)
    normw, _ = salloc("normw", [128, 32], F32)
    su, _ = salloc("su", [128, 16], F32)
    cu, _ = salloc("cu", [128, 64], F32)
    cg, _ = salloc("cg", [128, 64], F32)
    sudp_t, _ = salloc("sudp_t", [128, 64], F32)
    dcol, _ = salloc("dcol", [128, 32], F32)
    convw, _ = salloc("convw", [128, NT_XBC * KC], F32)
    convb, _ = salloc("convb", [128, NT_XBC], F32)
    idb, _ = salloc("idb", [128, 128], BF16)
    idf, _ = salloc("idf", [128, 128], F32)
    h128b, _ = salloc("h128b", [128, 128], BF16)
    h64b, _ = salloc("h64b", [64, 64], BF16)
    causal, _ = salloc("causal", [128, 128], BF16)
    onesb, _ = salloc("onesb", [128, 1], BF16)
    onesf, _ = salloc("onesf", [1, 128], F32)
    ones64, _ = salloc("ones64", [64, CH], F32)
    abias, _ = salloc("abias", [64, 1], F32)
    acol, _ = salloc("acol", [64, 1], F32)
    cc_buf, _ = salloc("cc_buf", [64, 512], BF16)
    bct, _ = salloc("bct", [128, 64], BF16)
    dtt, _ = salloc("dtt", [128, 64], BF16)
    act_t, _ = salloc("act_t", [128, 64], F32)
    dst, _ = salloc("dst", [128, 64], BF16)
    cbm, _ = salloc("cbm", [128, 128], BF16)
    cdecr, _ = salloc("cdecr", [64, 64], F32)
    rowst, _ = salloc("rowst", [2, 512], F32)

    assert cur[0] <= 228000, f"SBUF over budget: {cur[0]} bytes/partition"

    psB = [nc.alloc_psum_tensor(f"psb{i}", [128, 1024], BF16).ap() for i in range(2)]
    psF = [nc.alloc_psum_tensor(f"psf{i}", [128, 512], F32).ap() for i in range(6)]

    CONSTS = {"inv2048": (0, 1.0 / 2048.0), "eps": (1, EPS), "mhalf": (2, -0.5),
              "invs2048": (3, 1.0 / math.sqrt(2048.0)), "one": (4, 1.0),
              "invs8192": (5, 1.0 / math.sqrt(8192.0)), "neg1": (6, -1.0),
              "inv4096": (7, 1.0 / 4096.0), "ln05": (8, -0.6931471805599453)}

    def ccol(name, lo=0, hi=128):
        return consts[lo:hi, CONSTS[name][0]:CONSTS[name][0] + 1]

    def proj_gate(t):
        return proj[:, t * 512:(t + 1) * 512]

    def proj_xbc(t, lo=0, span=512):
        base = GATE_COLS + t * 515 + 3
        return proj[:, base + lo:base + lo + span]

    def proj_xbc_halo(t):
        base = GATE_COLS + t * 515
        return proj[:, base:base + 515]

    def proj_dtile():
        base = GATE_COLS + XBC_COLS
        return proj[0:64, base:base + 512]

    def xs_tile(t):
        return xsy[:, t * 512:(t + 1) * 512]

    s_io = b.sem("io")
    s_w = b.sem("w")
    s_d2 = b.sem("d2")
    s_pe = b.sem("pe")
    s_v = b.sem("v")
    s_sc = b.sem("sc")
    s_g = b.sem("g")
    s_out = b.sem("out")

    psf_free = [[] for _ in range(6)]
    psb_free = [[] for _ in range(2)]

    for dst_, src_ in [(convw, d_convw), (convb, d_convb), (lnw, d_lnw), (normw, d_normw),
                       (su, d_su), (dcol, d_dcols), (wsdp, d_wsdp), (abias, d_dtb),
                       (acol, d_alog), (cu, d_wsu_u), (cg, d_wsu_g), (sudp_t, d_sudp)]:
        b.dma("sync", dst_, src_, s_w)
    b.dma("gpsimd", h128b, d_h128, s_io)
    b.dma("gpsimd", h64b, d_h64, s_io)
    b.dma("gpsimd", causal, d_causal, s_io)
    W0 = s_w.n
    IO0 = s_io.n

    b.op("vector", lambda e: e.memset(onesb, 1.0))
    b.op("vector", lambda e: e.memset(onesf, 1.0))
    b.op("vector", lambda e: e.memset(ones64, 1.0))
    for nm, (idx, val) in CONSTS.items():
        b.op("vector", (lambda i_, v_: (lambda e: e.memset(consts[:, i_:i_ + 1], v_)))(idx, val))
    b.op("gpsimd", lambda e: e.memset(idb, 0.0))
    b.op("gpsimd", lambda e: e.affine_select(out=idb, in_=idb, compare_op=ALU.not_equal,
                                             fill=1.0, base=0, pattern=[[-1, 128]], channel_multiplier=1))
    b.op("gpsimd", lambda e: e.memset(idf, 0.0))
    b.op("gpsimd", lambda e: e.affine_select(out=idf, in_=idf, compare_op=ALU.not_equal,
                                             fill=1.0, base=0, pattern=[[-1, 128]], channel_multiplier=1))
    b.op("gpsimd", lambda e: e.memset(halo, 0.0))
    b.op("gpsimd", lambda e: e.memset(carry, 0.0), incs=[(s_g, 1)])
    G_INIT = s_g.n

    b.op("scalar", lambda e: e.activation(out=acol, in_=acol, func=AF.Exp),
         waits=[(s_w, W0)], incs=[(s_sc, 1)])
    SC_A = s_sc.n
    b.op("vector", lambda e: e.tensor_scalar(out=acol, in0=acol, scalar1=-1.0,
                                             scalar2=None, op0=ALU.mult),
         waits=[(s_sc, SC_A), (s_w, W0)])
    b.op("vector", lambda e: e.tensor_tensor(out=cu, in0=cu, in1=sudp_t, op=ALU.mult),
         waits=[(s_w, W0)])
    b.op("vector", lambda e: e.tensor_scalar(out=cu, in0=cu, scalar1=ccol("invs2048"),
                                             scalar2=None, op0=ALU.mult))
    b.op("vector", lambda e: e.tensor_scalar(out=cg, in0=cg, scalar1=ccol("invs2048"),
                                             scalar2=None, op0=ALU.mult))
    b.op("vector", lambda e: e.tensor_scalar(out=wsdp, in0=wsdp, scalar1=ccol("invs8192", 0, 64),
                                             scalar2=None, op0=ALU.mult), incs=[(s_v, 1)])
    V_CONST = s_v.n

    skip_pre = cfg.get("skip_prepass", False)
    if not skip_pre:
        for i in range(8):
            rows = slice(i * 256, (i + 1) * 256)
            b.dma("gpsimd", s_win[rows, :], d_win[rows, :], s_io)

        mk = {}
        for i in range(32):
            par = i % 2
            fin = pp_inf[par][:, 0:2048]
            fout = pp_out[par][:, 0:2048]
            waits = [(s_v, V_CONST)]
            if i >= 2:
                waits.append((s_w, mk[("p2o", i - 2)]))
            b.dma("sync", fin, d_wop[i * 128:(i + 1) * 128, :], s_w, waits=waits)
            W_IN = s_w.n
            b.op("vector", (lambda fin_, fout_, i_: (lambda e: e.tensor_scalar(
                out=fout_, in0=fin_, scalar1=normw[:, i_:i_ + 1], scalar2=None, op0=ALU.mult)))(fin, fout, i),
                waits=[(s_w, W_IN)], incs=[(s_v, 1)])
            b.dma("sync", s_wop[i * 128:(i + 1) * 128, :], fout, s_w, waits=[(s_v, s_v.n)])
            mk[("p2o", i)] = s_w.n

        P2_DONE = mk[("p2o", 31)]
        for wi, (dsrc, ddst) in enumerate([(d_wup, s_wup), (d_wgt, s_wgt)]):
            for kt in range(16):
                it = wi * 16 + kt
                par = it % 2
                waits = [(s_w, P2_DONE)]
                if it >= 2:
                    waits.append((s_w, mk[("p3o", it - 2)]))
                iow = s_io.n
                b.dma("gpsimd", pp_in[par][:, 0:4096], dsrc[kt * 128:(kt + 1) * 128, 0:4096], s_io, waits=waits)
                b.dma("gpsimd", pp_in[par][:, 4096:8192], dsrc[kt * 128:(kt + 1) * 128, 4096:8192], s_io, waits=waits)
                IOW = s_io.n
                for j in range(16):
                    bk = j % 2
                    b.op("tensor", (lambda par_, j_, bk_: (lambda e: e.matmul(
                        psF[bk_], h128b, pp_in[par_][:, j_ * 512:(j_ + 1) * 512],
                        start=True, stop=True)))(par, j, bk),
                        waits=[(s_io, IOW), (s_io, IO0)] + psf_free[bk], incs=[(s_pe, 1)])
                    b.op("vector", (lambda par_, j_, bk_: (lambda e: e.tensor_copy(
                        out=pp_out[par_][:, j_ * 512:(j_ + 1) * 512], in_=psF[bk_])))(par, j, bk),
                        waits=[(s_pe, s_pe.n)], incs=[(s_v, 1)])
                    psf_free[bk] = [(s_v, s_v.n)]
                b.dma("sync", ddst[kt * 128:(kt + 1) * 128, :], pp_out[par], s_w, waits=[(s_v, s_v.n)])
                mk[("p3o", it)] = s_w.n

        dwdn_v = d_wdn.rearrange("(i jr) r -> i (jr r)", i=64)
        swdn1_v = s_wdn1.rearrange("(i jr) r -> i (jr r)", i=64)
        P3_DONE = mk[("p3o", 31)]
        for c in range(32):
            par = c % 2
            fin = pp_in[par][0:64, 0:8192]
            fout = pp_out[par][0:64, 0:8192]
            waits = [(s_w, P3_DONE)]
            if c >= 2:
                waits.append((s_w, mk[("p4o", c - 2)]))
            b.dma("gpsimd", fin, dwdn_v[:, c * 8192:(c + 1) * 8192], s_io, waits=waits)
            IOW = s_io.n
            for j in range(16):
                bk = j % 2
                rstart = (c * 8192 + j * 512) % 2048
                b.op("tensor", (lambda fin_, j_, bk_: (lambda e: e.matmul(
                    psF[bk_][0:64, :], h64b, fin_[:, j_ * 512:(j_ + 1) * 512],
                    start=True, stop=True)))(fin, j, bk),
                    waits=[(s_io, IOW), (s_v, V_CONST)] + psf_free[bk], incs=[(s_pe, 1)])
                b.op("vector", (lambda fout_, j_, rs_, bk_: (lambda e: e.tensor_tensor(
                    out=fout_[:, j_ * 512:(j_ + 1) * 512], in0=psF[bk_][0:64, :],
                    in1=wsdp[:, rs_:rs_ + 512], op=ALU.mult)))(fout, j, rstart, bk),
                    waits=[(s_pe, s_pe.n)], incs=[(s_v, 1)])
                psf_free[bk] = [(s_v, s_v.n)]
            b.dma("sync", swdn1_v[:, c * 8192:(c + 1) * 8192], fout, s_w, waits=[(s_v, s_v.n)])
            mk[("p4o", c)] = s_w.n

        for kt in range(64):
            par = kt % 2
            fin = pp_in[par][:, 0:2048]
            fout = pp_out[par][:, 0:2048]
            waits = [(s_w, mk[("p4o", 31)])]
            if kt >= 2:
                waits.append((s_w, mk[("p5o", kt - 2)]))
            b.dma("sync", fin, s_wdn1[kt * 128:(kt + 1) * 128, :], s_w, waits=waits)
            W_IN = s_w.n
            for j in range(4):
                bk = j % 2
                b.op("tensor", (lambda fin_, j_, bk_: (lambda e: e.matmul(
                    psF[bk_], h128b, fin_[:, j_ * 512:(j_ + 1) * 512],
                    start=True, stop=True)))(fin, j, bk),
                    waits=[(s_w, W_IN)] + psf_free[bk], incs=[(s_pe, 1)])
                b.op("vector", (lambda fout_, j_, bk_: (lambda e: e.tensor_copy(
                    out=fout_[:, j_ * 512:(j_ + 1) * 512], in_=psF[bk_])))(fout, j, bk),
                    waits=[(s_pe, s_pe.n)], incs=[(s_v, 1)])
                psf_free[bk] = [(s_v, s_v.n)]
            b.dma("sync", s_wdn[kt * 128:(kt + 1) * 128, :], fout, s_w, waits=[(s_v, s_v.n)])
            mk[("p5o", kt)] = s_w.n

    PRE_DONE = [(s_w, s_w.n), (s_v, s_v.n), (s_io, s_io.n), (s_g, G_INIT), (s_sc, s_sc.n)]

    tb_par = [0]

    def tpose_rounds(blocks, first_waits, evac_emit, group=4):
        nrounds = (len(blocks) + group - 1) // group
        for r in range(nrounds):
            bank = tb_par[0] % 2
            tb_par[0] += 1
            chunk = blocks[r * group:(r + 1) * group]
            for si, (in_ap, key) in enumerate(chunk):
                b.op("tensor", (lambda a_, bk_, s_: (lambda e: e.transpose(
                    psB[bk_][:, s_ * 128:(s_ + 1) * 128], a_, idb)))(in_ap, bank, si),
                    waits=list(first_waits) + psb_free[bank], incs=[(s_pe, 1)])
            PE_M = s_pe.n
            last = None
            for si, (in_ap, key) in enumerate(chunk):
                last = evac_emit(key, psB[bank][:, si * 128:(si + 1) * 128], (s_pe, PE_M))
            psb_free[bank] = [last]

    M_out_exit = 0
    M_xtm_free = [(s_v, 0)]
    M_proj_free = list(PRE_DONE)
    M_a5_free = list(PRE_DONE)
    M_xsy_free = [(s_g, G_INIT)]
    M_carry = [(s_g, G_INIT)]
    M_exitst = [(s_out, 0)]

    STOP = cfg.get("stop_after")

    def stop_at(name, _done=[False]):
        order = ["entry", "in_proj", "conv", "dt", "ssd", "op", "ln1", "f1", "f2", "ffn3", "ffn", None]
        if STOP is None:
            return False
        return order.index(STOP) < order.index(name)

    for g in range(NGRP):
        seq = g // (NGRP_FULL // SEQ)
        tok0 = (g % (NGRP_FULL // SEQ)) * G
        first_of_seq = (tok0 == 0)

        src = d_x[seq, tok0:tok0 + G, :].rearrange("(a p) h -> p a h", p=128)
        b.dma("gpsimd", x_tm.rearrange("p (a h) -> p a h", h=HD), src, s_io,
              waits=list(M_xtm_free))
        IO_X = s_io.n

        for a in range(4):
            b.op("scalar", (lambda a_: (lambda e: e.activation(
                out=exitst, in_=x_tm[:, a_ * HD:(a_ + 1) * HD], func=AF.Square,
                accum_out=cols[:, a_:a_ + 1])))(a),
                waits=[(s_io, IO_X)] + list(M_exitst), incs=[(s_sc, 1)])
        b.op("scalar", lambda e: e.activation(out=cols[:, 4:8], in_=cols[:, 0:4], func=AF.Ln,
                                              scale=ccol("inv2048"), bias=ccol("eps")),
             waits=[(s_sc, s_sc.n)])
        b.op("scalar", lambda e: e.activation(out=cols[:, 4:8], in_=cols[:, 4:8], func=AF.Exp,
                                              scale=ccol("mhalf")), incs=[(s_sc, 1)])
        SC_RSTD1 = s_sc.n
        M_exitst = [(s_sc, SC_RSTD1)]

        for a in range(4):
            b.op("vector", (lambda a_: (lambda e: e.tensor_scalar(
                out=xn_stage, in0=x_tm[:, a_ * HD:(a_ + 1) * HD],
                scalar1=cols[:, 4 + a_:5 + a_], scalar2=None, op0=ALU.mult)))(a),
                waits=[(s_sc, SC_RSTD1)] + psb_free[0] + psb_free[1] + list(M_exitst),
                incs=[(s_v, 1)])
            V_STAGE = s_v.n
            M_exitst = [(s_v, V_STAGE)]

            def evac_xn(key, slot_ap, pe_mark, a_=a):
                i_ = key
                b.op("vector", (lambda i2, a2, sl: (lambda e: e.tensor_scalar(
                    out=a5[:, i2 * 512 + a2 * 128: i2 * 512 + (a2 + 1) * 128],
                    in0=sl, scalar1=lnw[:, i2:i2 + 1], scalar2=None, op0=ALU.mult)))(i_, a_, slot_ap),
                    waits=[pe_mark] + list(M_a5_free), incs=[(s_v, 1)])
                return (s_v, s_v.n)

            blocks = [(xn_stage[:, i * 128:(i + 1) * 128], i) for i in range(16)]
            tpose_rounds(blocks, [(s_v, V_STAGE)], evac_xn, group=8)
            M_exitst = [(s_pe, s_pe.n)]
        V_XN_DONE = s_v.n
        M_a5_free = []

        if stop_at("in_proj"):
            continue
        ip_ev = {}
        ip_strip = {}
        for pb in range(33):
            srcv = s_win.rearrange("(kt kp) p -> kp kt p", kp=128)
            for kh in range(2):
                gstrip = pb * 2 + kh
                par = gstrip % 2
                dstv = wbuf[par].rearrange("p (kt c) -> p kt c", c=256)
                waits = list(M_proj_free) + [(s_v, V_XN_DONE)]
                if gstrip >= 2:
                    waits.append(ip_strip[gstrip - 2])
                b.dma("sync", dstv, srcv[:, kh * 8:(kh + 1) * 8, pb * 256:(pb + 1) * 256],
                      s_w, waits=waits)
                W_IP = s_w.n
                for j in range(2):
                    bankF = 2 + (pb % 2) * 2 + j
                    for k8 in range(8):
                        kt = kh * 8 + k8
                        mmw = [(s_w, W_IP)]
                        if kt == 0:
                            mmw += psf_free[bankF]
                        b.op("tensor", (lambda par_, j_, k8_, kt_, bankF_: (lambda e: e.matmul(
                            psF[bankF_], wbuf[par_][:, k8_ * 256 + j_ * 128: k8_ * 256 + j_ * 128 + 128],
                            a5[:, kt_ * 512:(kt_ + 1) * 512],
                            start=(kt_ == 0), stop=(kt_ == 15))))(par, j, k8, kt, bankF),
                            waits=mmw, incs=([(s_pe, 1)] if (k8 == 7 and j == 1) else []))
                ip_strip[gstrip] = (s_pe, s_pe.n)
            PE_MM = s_pe.n
            for j in range(2):
                tI = pb * 2 + j
                bankF = 2 + (pb % 2) * 2 + j
                if tI < NT_GATE:
                    dstap = proj_gate(tI)
                elif tI < NT_GATE + NT_XBC:
                    dstap = proj_xbc(tI - NT_GATE)
                else:
                    dstap = proj[:, GATE_COLS + XBC_COLS: GATE_COLS + XBC_COLS + 512]
                if tI % 2 == 0:
                    b.op("vector", (lambda d_, bf_: (lambda e: e.tensor_copy(out=d_, in_=psF[bf_])))(dstap, bankF),
                         waits=[(s_pe, PE_MM)], incs=[(s_v, 1)])
                    psf_free[bankF] = [(s_v, s_v.n)]
                    ip_ev[tI] = (s_v, s_v.n)
                else:
                    b.op("scalar", (lambda d_, bf_: (lambda e: e.copy(out=d_, in_=psF[bf_])))(dstap, bankF),
                         waits=[(s_pe, PE_MM)], incs=[(s_sc, 1)])
                    psf_free[bankF] = [(s_sc, s_sc.n)]
                    ip_ev[tI] = (s_sc, s_sc.n)
        IP_ALL = [(s_v, s_v.n), (s_sc, s_sc.n)]

        if stop_at("conv"):
            continue
        if first_of_seq and g > 0:
            b.op("gpsimd", lambda e: e.memset(halo, 0.0), waits=list(M_xsy_free), incs=[(s_g, 1)])
        HALO_RDY = (s_g, s_g.n)
        cv_marks = {}
        for t in range(NT_XBC):
            eng = "vector"
            sm = s_v
            w0 = GATE_COLS + t * 515
            b.op(eng, (lambda t_, w0_: (lambda e: e.tensor_copy(
                out=proj[:, w0_:w0_ + 3], in_=halo[:, t_ * 3:t_ * 3 + 3])))(t, w0),
                waits=[ip_ev[NT_GATE + t], HALO_RDY], incs=[(sm, 1)])
            b.op(eng, (lambda t_: (lambda e: e.tensor_copy(
                out=halo[:, t_ * 3:t_ * 3 + 3], in_=proj_xbc(t_, 509, 3))))(t),
                incs=[(sm, 1)])
            b.op(eng, (lambda t_: (lambda e: e.tensor_scalar(
                out=xs_tile(t_), in0=proj_xbc_halo(t_)[:, 0:512],
                scalar1=convw[:, t_ * KC:t_ * KC + 1], scalar2=None, op0=ALU.mult)))(t),
                waits=list(M_xsy_free), incs=[(sm, 1)])
            for k in range(1, KC):
                b.op(eng, (lambda t_, k_: (lambda e: e.scalar_tensor_tensor(
                    out=xs_tile(t_), in0=proj_xbc_halo(t_)[:, k_:k_ + 512],
                    scalar=convw[:, t_ * KC + k_:t_ * KC + k_ + 1], in1=xs_tile(t_),
                    op0=ALU.mult, op1=ALU.add)))(t, k),
                    incs=[(sm, 1)])
            b.op("scalar", (lambda t_: (lambda e: e.activation(
                out=xs_tile(t_), in_=xs_tile(t_), func=AF.Silu, bias=convb[:, t_:t_ + 1])))(t),
                waits=[(sm, sm.n)], incs=[(s_sc, 1)])
            cv_marks[t] = (s_sc, s_sc.n)
        CV_ALL = (s_sc, s_sc.n)
        b.dma("scalar", cc_buf, xs_tile(32)[64:128, :], s_d2, waits=[CV_ALL])
        D2_CC = s_d2.n

        if stop_at("dt"):
            continue
        b.op("vector", lambda e: e.tensor_scalar(out=dt_sp, in0=proj_dtile(),
                                                 scalar1=abias, scalar2=None, op0=ALU.add),
             waits=[ip_ev[65]] + IP_ALL, incs=[(s_v, 1)])
        V_DT0 = s_v.n
        b.op("scalar", lambda e: e.activation(out=dA_t, in_=proj_dtile(), func=AF.Abs,
                                              bias=abias),
             waits=[ip_ev[65]] + IP_ALL, incs=[(s_sc, 1)])
        SC_ABS = s_sc.n
        b.op("scalar", lambda e: e.activation(out=dA_t, in_=dA_t, func=AF.Exp,
                                              scale=ccol("neg1", 0, 64)))
        b.op("scalar", lambda e: e.activation(out=dA_t, in_=dA_t, func=AF.Ln,
                                              bias=ccol("one", 0, 64)), incs=[(s_sc, 1)])
        SC_DT = s_sc.n
        b.op("vector", lambda e: e.tensor_scalar(out=dt_sp, in0=dt_sp, scalar1=0.0,
                                                 scalar2=None, op0=ALU.max))
        b.op("vector", lambda e: e.tensor_tensor(out=dt_sp, in0=dt_sp, in1=dA_t, op=ALU.add),
             waits=[(s_sc, SC_DT)])
        b.op("vector", lambda e: e.tensor_copy(out=dt_spb, in_=dt_sp))
        b.op("vector", lambda e: e.tensor_scalar(out=dA_t, in0=dt_sp, scalar1=acol,
                                                 scalar2=None, op0=ALU.mult), incs=[(s_v, 1)])
        for c in range(NCHUNK):
            b.op("vector", (lambda c_: (lambda e: e.tensor_tensor_scan(
                out=acum[:, c_ * CH:(c_ + 1) * CH], data0=ones64,
                data1=dA_t[:, c_ * CH:(c_ + 1) * CH], initial=0.0,
                op0=ALU.mult, op1=ALU.add)))(c), incs=[(s_v, 1)])
        V_ACUM = s_v.n

        if stop_at("ssd"):
            continue
        prev_repl_reads = None
        gate_done = []
        for c in range(NCHUNK):
            csl = slice(c * CH, (c + 1) * CH)
            fw = [(s_v, V_ACUM)]
            if prev_repl_reads is not None:
                fw += prev_repl_reads
            b.dma("scalar", s_acum, acum[:, csl], s_d2, waits=fw)
            D2_FLAT = s_d2.n

            b.op("vector", (lambda c_: (lambda e: e.tensor_scalar(
                out=dssub, in0=acum[:, c_ * CH:(c_ + 1) * CH],
                scalar1=acum[:, c_ * CH + CH - 1: c_ * CH + CH],
                scalar2=None, op0=ALU.subtract)))(c), waits=[(s_v, V_ACUM)], incs=[(s_v, 1)])
            b.op("scalar", lambda e: e.activation(out=dshm, in_=dssub, func=AF.Exp,
                                                  scale=ccol("neg1", 0, 64)),
                 waits=[(s_v, s_v.n)], incs=[(s_sc, 1)])
            SC_DS = s_sc.n

            bank = tb_par[0] % 2
            tb_par[0] += 1
            tw = psb_free[bank] + [(s_v, s_v.n), (s_sc, SC_DS)]
            b.op("tensor", (lambda c_, bk_: (lambda e: e.transpose(
                psB[bk_][:, 0:64], dt_spb[:, c_ * CH:(c_ + 1) * CH], idb[0:64, 0:64])))(c, bank),
                waits=tw, incs=[(s_pe, 1)])
            b.op("tensor", (lambda bk_: (lambda e: e.transpose(
                psB[bk_][:, 64:128], dshm, idb[0:64, 0:64])))(bank), incs=[(s_pe, 1)])
            b.op("tensor", (lambda c_, bk_: (lambda e: e.transpose(
                psB[bk_][:, 128:192], xs_tile(32)[0:64, c_ * CH:(c_ + 1) * CH], idb[0:64, 0:64])))(c, bank),
                waits=[CV_ALL], incs=[(s_pe, 1)])
            b.op("tensor", (lambda c_: (lambda e: e.transpose(
                psF[0][:, 0:64], acum[:, c_ * CH:(c_ + 1) * CH], idf[0:64, 0:64])))(c),
                waits=psf_free[0], incs=[(s_pe, 1)])
            b.op("tensor", (lambda c_: (lambda e: e.transpose(
                psF[0][0:1, 64:128], acum[:, c_ * CH + CH - 1:c_ * CH + CH], idf[0:64, 0:64])))(c),
                incs=[(s_pe, 1)])
            PE_TR = s_pe.n
            b.op("vector", (lambda bk_: (lambda e: e.tensor_copy(out=dtt, in_=psB[bk_][:, 0:64])))(bank),
                 waits=[(s_pe, PE_TR)], incs=[(s_v, 1)])
            b.op("vector", (lambda bk_: (lambda e: e.tensor_copy(out=dst, in_=psB[bk_][:, 64:128])))(bank),
                 incs=[(s_v, 1)])
            b.op("vector", (lambda bk_: (lambda e: e.tensor_copy(out=bct, in_=psB[bk_][:, 128:192])))(bank),
                 incs=[(s_v, 1)])
            b.op("vector", lambda e: e.tensor_copy(out=act_t, in_=psF[0][:, 0:64]), incs=[(s_v, 1)])
            b.op("vector", lambda e: e.tensor_copy(out=rowst[0:1, 0:64], in_=psF[0][0:1, 64:128]),
                 incs=[(s_v, 1)])
            V_TR = s_v.n
            psb_free[bank] = [(s_v, V_TR)]
            psf_free[0] = [(s_v, V_TR)]
            b.op("tensor", lambda e: e.matmul(psF[1][0:64, 0:64], onesf[:, 0:64], rowst[0:1, 0:64],
                                              start=True, stop=True),
                 waits=[(s_v, V_TR)] + psf_free[1], incs=[(s_pe, 1)])
            b.op("scalar", lambda e: e.activation(out=cdecr, in_=psF[1][0:64, 0:64], func=AF.Exp),
                 waits=[(s_pe, s_pe.n)], incs=[(s_sc, 1)])
            SC_CDECR = s_sc.n
            b.op("tensor", (lambda c_: (lambda e: e.matmul(
                psF[1][:, 128:256], xs_tile(32)[0:64, c_ * CH:(c_ + 1) * CH],
                cc_buf[:, c_ * CH:(c_ + 1) * CH], start=True, stop=True)))(c),
                waits=[(s_d2, D2_CC), CV_ALL, (s_sc, SC_CDECR)], incs=[(s_pe, 1)])
            PE_CB = s_pe.n
            b.op("vector", lambda e: e.tensor_tensor(out=cbm, in0=psF[1][:, 128:256],
                                                     in1=causal, op=ALU.mult),
                 waits=[(s_pe, PE_CB)], incs=[(s_v, 1)])
            V_CBM = s_v.n
            psf_free[1] = [(s_v, V_CBM)]

            xct_ev = []
            for half in range(4):
                bank = tb_par[0] % 2
                tb_par[0] += 1
                for q in range(8):
                    t = half * 8 + q
                    b.op("tensor", (lambda t_, c_, bk_, q_: (lambda e: e.transpose(
                        psB[bk_][:, q_ * 128:(q_ + 1) * 128],
                        xs_tile(t_)[:, c_ * CH:(c_ + 1) * CH], idb)))(t, c, bank, q),
                        waits=[CV_ALL] + psb_free[bank], incs=[(s_pe, 1)])
                PE_T8 = s_pe.n
                b.op("vector", (lambda half_, bk_: (lambda e: e.tensor_tensor(
                    out=xct.rearrange("p (h f) -> p h f", f=64)[:, half_ * 16:(half_ + 1) * 16, :],
                    in0=psB[bk_].rearrange("p (h f) -> p h f", f=64),
                    in1=dtt[:, half_ * 16:(half_ + 1) * 16].unsqueeze(2).broadcast_to([128, 16, 64]),
                    op=ALU.mult)))(half, bank),
                    waits=[(s_pe, PE_T8), (s_v, V_TR)], incs=[(s_v, 1)])
                psb_free[bank] = [(s_v, s_v.n)]
                xct_ev.append((s_v, s_v.n))
            V_XCT = s_v.n

            psY = [psF[2], psF[3]]
            psS = [psF[4], psF[5]]
            repl_reads = []
            oct_marks = []
            for o in range(8):
                par = o % 2
                rw = [(s_d2, D2_FLAT)]
                if len(oct_marks) >= 2:
                    rw += oct_marks[-2]["repl_reads"]
                b.dma("scalar", replo[par],
                      s_acum.rearrange("h s -> (h s)").rearrange("(o2 f) -> o2 f", o2=1)[:, o * 1024:(o + 1) * 1024].broadcast_to([128, 1024]),
                      s_d2, waits=rw)
                D2_R = s_d2.n
                msk_w = [(s_d2, D2_R), (s_v, V_TR)]
                if len(oct_marks) >= 2:
                    msk_w.append(oct_marks[-2]["y_pe"])
                b.op("vector", (lambda par_, o_: (lambda e: e.tensor_tensor(
                    out=scoreso[par_].rearrange("p (h l) -> p h l", l=128),
                    in0=replo[par_].rearrange("p (h l) -> p h l", l=128),
                    in1=act_t[:, o_ * 8:(o_ + 1) * 8].unsqueeze(2).broadcast_to([128, 8, 128]),
                    op=ALU.subtract)))(par, o), waits=msk_w, incs=[(s_v, 1)])
                V_SUB = s_v.n
                b.op("vector", (lambda par_: (lambda e: e.tensor_scalar(
                    out=scoreso[par_], in0=scoreso[par_], scalar1=0.0, scalar2=None,
                    op0=ALU.min)))(par), incs=[(s_v, 1)])
                V_MS = s_v.n
                b.op("scalar", (lambda par_: (lambda e: e.activation(
                    out=scoreso[par_], in_=scoreso[par_], func=AF.Exp)))(par),
                    waits=[(s_v, V_MS)], incs=[(s_sc, 1)])
                SC_EXP = s_sc.n
                b.op("vector", (lambda par_: (lambda e: e.tensor_tensor(
                    out=scoreso[par_].rearrange("p (h l) -> p h l", l=128),
                    in0=scoreso[par_].rearrange("p (h l) -> p h l", l=128),
                    in1=cbm.rearrange("p (h2 l) -> p h2 l", h2=1).broadcast_to([128, 8, 128]),
                    op=ALU.mult)))(par),
                    waits=[(s_sc, SC_EXP), (s_v, V_CBM)], incs=[(s_v, 1)])
                V_SCORE = s_v.n
                cd_w = [(s_d2, D2_R)]
                if len(oct_marks) >= 2:
                    cd_w.append(oct_marks[-2]["y_pe"])
                b.op("scalar", (lambda par_: (lambda e: e.activation(
                    out=cdeco[par_], in_=replo[par_][0:64, :], func=AF.Exp)))(par),
                    waits=cd_w, incs=[(s_sc, 1)])
                SC_CD = s_sc.n
                b.op("gpsimd", (lambda par_, c_: (lambda e: e.tensor_tensor(
                    out=cdeco[par_].rearrange("p (h l) -> p h l", l=128),
                    in0=cdeco[par_].rearrange("p (h l) -> p h l", l=128),
                    in1=cc_buf[:, c_ * CH:(c_ + 1) * CH].rearrange("p (h2 l) -> p h2 l", h2=1).broadcast_to([64, 8, 128]),
                    op=ALU.mult)))(par, c),
                    waits=[(s_sc, SC_CD), (s_d2, D2_CC)], incs=[(s_g, 1)])
                G_CD = s_g.n
                bd_w = [(s_v, V_TR)]
                if len(oct_marks) >= 2:
                    bd_w.append(oct_marks[-2]["st_pe"])
                b.op("gpsimd", (lambda par_, o_: (lambda e: e.tensor_tensor(
                    out=bdeco[par_].rearrange("p (h n) -> p h n", n=64),
                    in0=bct.rearrange("p (h2 n) -> p h2 n", h2=1).broadcast_to([128, 8, 64]),
                    in1=dst[:, o_ * 8:(o_ + 1) * 8].unsqueeze(2).broadcast_to([128, 8, 64]),
                    op=ALU.mult)))(par, o), waits=bd_w, incs=[(s_g, 1)])
                G_BD = s_g.n
                cb_w = list(M_carry)
                if len(oct_marks) >= 2:
                    cb_w.append(oct_marks[-2]["y_pe"])
                b.op("vector", (lambda par_, o_: (lambda e: e.tensor_copy(
                    out=carryb[par_], in_=carry[:, o_ * 512:(o_ + 1) * 512])))(par, o),
                    waits=cb_w, incs=[(s_v, 1)])
                V_CBo = s_v.n

                ymm_w = [(s_v, V_SCORE), (s_v, V_XCT), (s_v, V_CBo), (s_g, G_CD)]
                if len(gate_done) >= 2:
                    ymm_w.append(gate_done[-2])
                for j2 in range(4):
                    h = 2 * j2
                    habs = o * 8 + h
                    b.op("tensor", (lambda par_, j2_, habs_, h_, f_: (lambda e: e.matmul(
                        psY[par_][0:64, j2_ * 128:(j2_ + 1) * 128],
                        xct[:, habs_ * 64:(habs_ + 1) * 64],
                        scoreso[par_][:, h_ * 128:(h_ + 1) * 128],
                        start=f_, stop=False, skip_group_check=True)))(par, j2, habs, h, j2 == 0),
                        waits=ymm_w, incs=[(s_pe, 1)])
                    b.op("tensor", (lambda par_, j2_, h_: (lambda e: e.matmul(
                        psY[par_][0:64, j2_ * 128:(j2_ + 1) * 128],
                        carryb[par_][:, h_ * 64:(h_ + 1) * 64],
                        cdeco[par_][:, h_ * 128:(h_ + 1) * 128],
                        start=False, stop=False, skip_group_check=True)))(par, j2, h),
                        incs=[(s_pe, 1)])
                    b.op("tensor", (lambda par_, j2_, habs_, h_, f_: (lambda e: e.matmul(
                        psY[par_][64:128, j2_ * 128:(j2_ + 1) * 128],
                        xct[:, (habs_ + 1) * 64:(habs_ + 2) * 64],
                        scoreso[par_][:, (h_ + 1) * 128:(h_ + 2) * 128],
                        start=f_, stop=False, skip_group_check=True,
                        tile_position=(0, 64))))(par, j2, habs, h, j2 == 0),
                        incs=[(s_pe, 1)])
                    b.op("tensor", (lambda par_, j2_, h_, l_: (lambda e: e.matmul(
                        psY[par_][64:128, j2_ * 128:(j2_ + 1) * 128],
                        carryb[par_][:, (h_ + 1) * 64:(h_ + 2) * 64],
                        cdeco[par_][:, (h_ + 1) * 128:(h_ + 2) * 128],
                        start=False, stop=l_, skip_group_check=True,
                        tile_position=(0, 64))))(par, j2, h, j2 == 3),
                        incs=[(s_pe, 1)])
                PE_Y = s_pe.n
                st_w = [(s_g, G_BD), (s_v, V_XCT)]
                if len(oct_marks) >= 2:
                    st_w.append(oct_marks[-2]["st_rd"])
                for hh in range(8):
                    habs = o * 8 + hh
                    b.op("tensor", (lambda par_, hh_, habs_: (lambda e: e.matmul(
                        psS[par_][0:64, hh_ * 64:(hh_ + 1) * 64],
                        bdeco[par_][:, hh_ * 64:(hh_ + 1) * 64],
                        xct[:, habs_ * 64:(habs_ + 1) * 64],
                        start=(hh_ == 0), stop=(hh_ == 7), skip_group_check=True)))(par, hh, habs),
                        waits=st_w, incs=([(s_pe, 1)] if hh == 7 else []))
                PE_S = s_pe.n

                sg_w = [(s_pe, PE_Y), CV_ALL] + IP_ALL
                if len(gate_done) >= 2:
                    sg_w.append(gate_done[-2])
                b.op("scalar", (lambda o_, c_, par_: (lambda e: e.activation(
                    out=sg[par_].rearrange("p (t l) -> p t l", l=128),
                    in_=proj[:, 0:GATE_COLS].rearrange("p (t f) -> p t f", f=512)[:, o_ * 4:(o_ + 1) * 4, c_ * CH:(c_ + 1) * CH],
                    func=AF.Silu)))(o, c, par), waits=sg_w, incs=[(s_sc, 1)])
                SC_SG = s_sc.n
                b.op("vector", (lambda o_, c_: (lambda e: e.tensor_tensor(
                    out=tmpg.rearrange("p (t l) -> p t l", l=128),
                    in0=xsy.rearrange("p (t f) -> p t f", f=512)[:, o_ * 4:(o_ + 1) * 4, c_ * CH:(c_ + 1) * CH],
                    in1=dcol[:, o_ * 4:(o_ + 1) * 4].unsqueeze(2).broadcast_to([128, 4, 128]),
                    op=ALU.mult)))(o, c),
                    waits=[(s_v, V_XCT)], incs=[(s_v, 1)])
                b.op("vector", (lambda par_: (lambda e: e.tensor_tensor(
                    out=tmpg, in0=tmpg, in1=psY[par_], op=ALU.add)))(par),
                    waits=[(s_pe, PE_Y)], incs=[(s_v, 1)])
                b.op("vector", (lambda o_, c_, par_: (lambda e: e.tensor_tensor(
                    out=xsy.rearrange("p (t f) -> p t f", f=512)[:, o_ * 4:(o_ + 1) * 4, c_ * CH:(c_ + 1) * CH],
                    in0=tmpg.rearrange("p (t l) -> p t l", l=128),
                    in1=sg[par_].rearrange("p (t l) -> p t l", l=128),
                    op=ALU.mult)))(o, c, par),
                    waits=[(s_sc, SC_SG)], incs=[(s_v, 1)])
                gate_done.append((s_v, s_v.n))

                b.op("gpsimd", (lambda o_: (lambda e: e.tensor_tensor(
                    out=carry.rearrange("p (h f) -> p h f", f=64)[:, o_ * 8:(o_ + 1) * 8, :],
                    in0=carry.rearrange("p (h f) -> p h f", f=64)[:, o_ * 8:(o_ + 1) * 8, :],
                    in1=cdecr[:, o_ * 8:(o_ + 1) * 8].unsqueeze(2).broadcast_to([64, 8, 64]),
                    op=ALU.mult)))(o),
                    waits=[(s_v, V_CBo), (s_sc, SC_CDECR)], incs=[(s_g, 1)])
                G_MUL = s_g.n
                b.op("vector", (lambda o_, par_: (lambda e: e.tensor_tensor(
                    out=carry[:, o_ * 512:(o_ + 1) * 512], in0=carry[:, o_ * 512:(o_ + 1) * 512],
                    in1=psS[par_][0:64, :], op=ALU.add)))(o, par),
                    waits=[(s_g, G_MUL), (s_pe, PE_S)], incs=[(s_v, 1)])
                V_ST = s_v.n
                oct_marks.append({"repl_reads": [(s_v, V_SUB), (s_sc, SC_CD)],
                                  "y_pe": (s_pe, PE_Y), "st_pe": (s_pe, PE_S),
                                  "st_rd": (s_v, V_ST)})
            M_carry = [(s_v, s_v.n)]
            psf_free[2] = [gate_done[-2]]
            psf_free[3] = [gate_done[-1]]
            psf_free[4] = [oct_marks[-2]["st_rd"]]
            psf_free[5] = [oct_marks[-1]["st_rd"]]
            prev_repl_reads = oct_marks[-1]["repl_reads"] + oct_marks[-2]["repl_reads"]
        GATE_ALL = [(s_v, s_v.n)]

        if stop_at("op"):
            continue
        for t in range(NT_GATE):
            b.op("scalar", (lambda t_: (lambda e: e.activation(
                out=y2win[:, t_ * 512:(t_ + 1) * 512], in_=xs_tile(t_), func=AF.Square)))(t),
                waits=GATE_ALL + list(M_carry), incs=[(s_sc, 1)])
        SC_Y2 = s_sc.n
        for t in range(NT_GATE):
            mmw = [(s_sc, SC_Y2)]
            if t == 0:
                mmw += psf_free[0]
            b.op("tensor", (lambda t_: (lambda e: e.matmul(
                psF[0][0:1, :], onesb, y2win[:, t_ * 512:(t_ + 1) * 512],
                start=(t_ == 0), stop=(t_ == NT_GATE - 1))))(t),
                waits=mmw, incs=([(s_pe, 1)] if t == NT_GATE - 1 else []))
        PE_YS = s_pe.n
        b.op("scalar", lambda e: e.activation(out=rowst[0:1, :], in_=psF[0][0:1, :],
                                              func=AF.Ln, scale=ccol("inv4096", 0, 1), bias=ccol("eps", 0, 1)),
             waits=[(s_pe, PE_YS)])
        b.op("scalar", lambda e: e.activation(out=rowst[0:1, :], in_=rowst[0:1, :], func=AF.Exp,
                                              scale=ccol("mhalf", 0, 1), bias=ccol("ln05", 0, 1)),
             incs=[(s_sc, 1)])
        SC_RY = s_sc.n
        psf_free[0] = [(s_sc, SC_RY)]
        for i4 in range(4):
            b.op("tensor", (lambda i_: (lambda e: e.transpose(
                psF[0][:, i_:i_ + 1], rowst[0:1, i_ * 128:(i_ + 1) * 128], idf[0:1, 0:1])))(i4),
                waits=[(s_sc, SC_RY)] + (psf_free[0] if i4 == 0 else []), incs=[(s_pe, 1)])
        b.op("vector", lambda e: e.tensor_copy(out=cols[:, 8:12], in_=psF[0][:, 0:4]),
             waits=[(s_pe, s_pe.n)], incs=[(s_v, 1)])
        psf_free[0] = [(s_v, s_v.n)]
        RY_COLS = (s_v, s_v.n)

        op_mk = {}
        for pb in range(8):
            srcv = s_wop.rearrange("(kt kp) p -> kp kt p", kp=128)
            for kq in range(4):
                gstrip = pb * 4 + kq
                par = gstrip % 2
                dstv = wbuf[par].rearrange("p (kt cc) -> p kt cc", cc=256)
                waits = [(s_sc, SC_Y2)]
                if gstrip >= 2:
                    waits.append(op_mk[gstrip - 2])
                b.dma("sync", dstv, srcv[:, kq * 8:kq * 8 + 8, pb * 256:(pb + 1) * 256],
                      s_w, waits=waits)
                W_OP = s_w.n
                for j in range(2):
                    bankF = 2 + (pb % 2) * 2 + j
                    for k8 in range(8):
                        kt = kq * 8 + k8
                        mmw = [(s_w, W_OP)]
                        if kt == 0:
                            mmw += psf_free[bankF]
                        b.op("tensor", (lambda par_, j_, k8_, kt_, bankF_: (lambda e: e.matmul(
                            psF[bankF_], wbuf[par_][:, k8_ * 256 + j_ * 128: k8_ * 256 + j_ * 128 + 128],
                            xs_tile(kt_), start=(kt_ == 0), stop=(kt_ == 31))))(par, j, k8, kt, bankF),
                            waits=mmw, incs=([(s_pe, 1)] if (k8 == 7 and j == 1) else []))
                op_mk[gstrip] = (s_pe, s_pe.n)
            PE_OP2 = s_pe.n
            for j in range(2):
                tI = pb * 2 + j
                bankF = 2 + (pb % 2) * 2 + j
                stg = a6[:, j * 512:(j + 1) * 512]
                b.op("vector", (lambda stg_, bf_: (lambda e: e.tensor_copy(out=stg_, in_=psF[bf_])))(stg, bankF),
                     waits=[(s_pe, PE_OP2)], incs=[(s_v, 1)])
                V_STG = s_v.n
                psf_free[bankF] = [(s_v, V_STG)]

                def evac_z1(key, slot_ap, pe_mark, tI_=tI):
                    a_ = key
                    b.op("vector", (lambda a2, t2, sl: (lambda e: e.scalar_tensor_tensor(
                        out=x_tm[:, a2 * HD + t2 * 128: a2 * HD + (t2 + 1) * 128],
                        in0=sl, scalar=cols[:, 8 + a2:9 + a2],
                        in1=x_tm[:, a2 * HD + t2 * 128: a2 * HD + (t2 + 1) * 128],
                        op0=ALU.mult, op1=ALU.add)))(a_, tI_, slot_ap),
                        waits=[pe_mark, RY_COLS], incs=[(s_v, 1)])
                    return (s_v, s_v.n)

                blocks = [(stg[:, a * 128:(a + 1) * 128], a) for a in range(4)]
                tpose_rounds(blocks, [(s_v, V_STG)], evac_z1, group=4)
        Z1_DONE = s_v.n

        if stop_at("ln1"):
            continue
        for a in range(4):
            b.op("scalar", (lambda a_: (lambda e: e.activation(
                out=exitst, in_=x_tm[:, a_ * HD:(a_ + 1) * HD], func=AF.Square,
                accum_out=cols[:, 28 + a_:29 + a_])))(a),
                waits=[(s_v, Z1_DONE)] + list(M_exitst), incs=[(s_sc, 1)])
            b.op("scalar", (lambda a_: (lambda e: e.activation(
                out=exitst, in_=x_tm[:, a_ * HD:(a_ + 1) * HD], func=AF.Copy,
                accum_out=cols[:, 16 + a_:17 + a_])))(a), incs=[(s_sc, 1)])
        SC_LN1S = s_sc.n
        M_exitst = [(s_sc, SC_LN1S)]
        b.op("scalar", lambda e: e.mul(out=cols[:, 16:20], in_=cols[:, 16:20], mul=ccol("inv2048")))
        b.op("scalar", lambda e: e.mul(out=cols[:, 28:32], in_=cols[:, 28:32], mul=ccol("inv2048")))
        b.op("scalar", lambda e: e.activation(out=cols[:, 20:24], in_=cols[:, 16:20], func=AF.Square))
        b.op("scalar", lambda e: e.mul(out=cols[:, 20:24], in_=cols[:, 20:24], mul=-1.0))
        for a4 in range(4):
            b.op("scalar", (lambda a_: (lambda e: e.add(
                out=cols[:, 12 + a_:13 + a_], in_=cols[:, 28 + a_:29 + a_],
                add=cols[:, 20 + a_:21 + a_])))(a4))
        b.op("scalar", lambda e: e.activation(out=cols[:, 12:16], in_=cols[:, 12:16], func=AF.Ln,
                                              bias=ccol("eps")))
        b.op("scalar", lambda e: e.activation(out=cols[:, 12:16], in_=cols[:, 12:16], func=AF.Exp,
                                              scale=ccol("mhalf")), incs=[(s_sc, 1)])
        SC_R1 = s_sc.n
        for a in range(4):
            b.op("vector", (lambda a_: (lambda e: e.tensor_scalar(
                out=x1_tm[:, a_ * HD:(a_ + 1) * HD], in0=x_tm[:, a_ * HD:(a_ + 1) * HD],
                scalar1=cols[:, 16 + a_:17 + a_], scalar2=cols[:, 12 + a_:13 + a_],
                op0=ALU.subtract, op1=ALU.mult)))(a),
                waits=[(s_sc, SC_R1)], incs=[(s_v, 1)])
        X1_DONE = s_v.n
        M_xtm_free = [(s_v, X1_DONE)]

        if stop_at("f1"):
            continue
        f1_first = GATE_ALL + [(s_v, X1_DONE), (s_sc, SC_Y2)]
        for a in range(4):
            def evac_ub(key, slot_ap, pe_mark, a_=a):
                i_ = key
                b.op("vector", (lambda i2, a2, sl: (lambda e: e.tensor_scalar(
                    out=a5[:, i2 * 512 + a2 * 128: i2 * 512 + (a2 + 1) * 128],
                    in0=sl, scalar1=su[:, i2:i2 + 1], scalar2=None, op0=ALU.mult)))(i_, a_, slot_ap),
                    waits=[pe_mark], incs=[(s_v, 1)])
                return (s_v, s_v.n)

            blocks = [(x1_tm[:, a * HD + i * 128: a * HD + (i + 1) * 128], i) for i in range(16)]
            tpose_rounds(blocks, f1_first, evac_ub, group=8)
        UB_RAW = s_v.n

        if stop_at("f2"):
            continue
        for st in range(4):
            bit = 1 << st
            srcbuf = a5 if st % 2 == 0 else ub1
            dstbuf = ub1 if st % 2 == 0 else a5
            stm = s_v.n
            gm = s_g.n
            extra = [(s_v, UB_RAW)] if st == 0 else []
            for i in range(16):
                if i & bit:
                    continue
                jp = i | bit
                for (oi, sgn) in ((i, ALU.add), (jp, ALU.subtract)):
                    eng = "vector" if (oi % 2 == 0) else "gpsimd"
                    sm = s_v if eng == "vector" else s_g
                    b.op(eng, (lambda sb_, db_, i_, jp_, oi_, sg2: (lambda e: e.tensor_tensor(
                        out=db_[:, oi_ * 512:(oi_ + 1) * 512],
                        in0=sb_[:, i_ * 512:(i_ + 1) * 512],
                        in1=sb_[:, jp_ * 512:(jp_ + 1) * 512],
                        op=sg2)))(srcbuf, dstbuf, i, jp, oi, sgn),
                        waits=[(s_v, stm), (s_g, gm)] + extra, incs=[(sm, 1)])
        b.op("vector", lambda e: e.engine_nop(), waits=[(s_v, s_v.n), (s_g, s_g.n)], incs=[(s_v, 1)])
        UB_DONE = s_v.n

        if stop_at("ffn3"):
            continue
        f3_mk = {}
        sidx = [0]
        for pb in range(32):
            for wsel, wsrc in enumerate([s_wup, s_wgt]):
                srcv = wsrc.rearrange("(kt kp) p -> kp kt p", kp=128)
                for kh in range(2):
                    gstrip = sidx[0]
                    sidx[0] += 1
                    par = gstrip % 2
                    dstv = wbuf[par].rearrange("p (kt cc) -> p kt cc", cc=256)
                    waits = [(s_v, UB_DONE)]
                    if gstrip >= 2:
                        waits.append(f3_mk[gstrip - 2])
                    b.dma("sync", dstv, srcv[:, kh * 8:(kh + 1) * 8, pb * 256:(pb + 1) * 256],
                          s_w, waits=waits)
                    W_F3 = s_w.n
                    for j in range(2):
                        if wsel == 0:
                            bankF = 2 + (pb % 2) * 2 + j
                        else:
                            bankF = 0 + j
                        for k8 in range(8):
                            kt = kh * 8 + k8
                            mmw = [(s_w, W_F3)]
                            if kt == 0:
                                mmw += psf_free[bankF]
                            b.op("tensor", (lambda par_, j_, k8_, kt_, bankF_: (lambda e: e.matmul(
                                psF[bankF_], wbuf[par_][:, k8_ * 256 + j_ * 128: k8_ * 256 + j_ * 128 + 128],
                                a5[:, kt_ * 512:(kt_ + 1) * 512],
                                start=(kt_ == 0), stop=(kt_ == 15))))(par, j, k8, kt, bankF),
                                waits=mmw, incs=([(s_pe, 1)] if (k8 == 7 and j == 1) else []))
                    f3_mk[gstrip] = (s_pe, s_pe.n)
            PE_UG = s_pe.n
            for j in range(2):
                tI = pb * 2 + j
                bank_u = 2 + (pb % 2) * 2 + j
                bank_g = 0 + j
                b.op("vector", (lambda tI_, bu_, j_: (lambda e: e.tensor_scalar(
                    out=usg[:, j_ * 1024:j_ * 1024 + 512], in0=psF[bu_],
                    scalar1=cu[:, tI_:tI_ + 1], scalar2=None, op0=ALU.mult)))(tI, bank_u, j),
                    waits=[(s_pe, PE_UG)], incs=[(s_v, 1)])
                psf_free[bank_u] = [(s_v, s_v.n)]
                b.op("scalar", (lambda tI_, bg_, j_: (lambda e: e.activation(
                    out=usg[:, j_ * 1024 + 512:j_ * 1024 + 1024], in_=psF[bg_], func=AF.Silu,
                    scale=cg[:, tI_:tI_ + 1])))(tI, bank_g, j),
                    waits=[(s_pe, PE_UG)], incs=[(s_sc, 1)])
                psf_free[bank_g] = [(s_sc, s_sc.n)]
                b.op("vector", (lambda tI_, j_: (lambda e: e.tensor_tensor(
                    out=proj[:, tI_ * 512:(tI_ + 1) * 512], in0=usg[:, j_ * 1024:j_ * 1024 + 512],
                    in1=usg[:, j_ * 1024 + 512:j_ * 1024 + 1024], op=ALU.mult)))(tI, j),
                    waits=[(s_v, s_v.n), (s_sc, s_sc.n)], incs=[(s_v, 1)])
        V_FFNV = s_v.n

        if stop_at("ffn"):
            continue
        f4_mk = {}
        for pb in range(8):
            srcv = s_wdn.rearrange("(kt kp) p -> kp kt p", kp=128)
            for kq in range(8):
                gstrip = pb * 8 + kq
                par = gstrip % 2
                dstv = wbuf[par].rearrange("p (kt cc) -> p kt cc", cc=256)
                waits = [(s_v, V_FFNV)]
                if gstrip >= 2:
                    waits.append(f4_mk[gstrip - 2])
                b.dma("sync", dstv, srcv[:, kq * 8:(kq + 1) * 8, pb * 256:(pb + 1) * 256],
                      s_w, waits=waits)
                W_D = s_w.n
                for j in range(2):
                    bankF = 2 + (pb % 2) * 2 + j
                    for k8 in range(8):
                        kt = kq * 8 + k8
                        mmw = [(s_w, W_D)]
                        if kt == 0:
                            mmw += psf_free[bankF]
                        b.op("tensor", (lambda par_, j_, k8_, kt_, bankF_: (lambda e: e.matmul(
                            psF[bankF_], wbuf[par_][:, k8_ * 256 + j_ * 128: k8_ * 256 + j_ * 128 + 128],
                            proj[:, kt_ * 512:(kt_ + 1) * 512],
                            start=(kt_ == 0), stop=(kt_ == 63))))(par, j, k8, kt, bankF),
                            waits=mmw, incs=([(s_pe, 1)] if (k8 == 7 and j == 1) else []))
                f4_mk[gstrip] = (s_pe, s_pe.n)
            PE_DN = s_pe.n
            for j in range(2):
                tI = pb * 2 + j
                bankF = 2 + (pb % 2) * 2 + j
                stg = a6[:, j * 512:(j + 1) * 512]
                b.op("vector", (lambda stg_, bf_: (lambda e: e.tensor_copy(out=stg_, in_=psF[bf_])))(stg, bankF),
                     waits=[(s_pe, PE_DN)], incs=[(s_v, 1)])
                V_STG = s_v.n
                psf_free[bankF] = [(s_v, V_STG)]

                def evac_z(key, slot_ap, pe_mark, tI_=tI):
                    a_ = key
                    b.op("vector", (lambda a2, t2, sl: (lambda e: e.tensor_tensor(
                        out=zfull[:, a2 * HD + t2 * 128: a2 * HD + (t2 + 1) * 128],
                        in0=sl, in1=x1_tm[:, a2 * HD + t2 * 128: a2 * HD + (t2 + 1) * 128],
                        op=ALU.add)))(a_, tI_, slot_ap),
                        waits=[pe_mark], incs=[(s_v, 1)])
                    return (s_v, s_v.n)

                blocks = [(stg[:, a * 128:(a + 1) * 128], a) for a in range(4)]
                tpose_rounds(blocks, [(s_v, V_STG)], evac_z, group=4)
        Z_DONE = s_v.n

        for a in range(4):
            zt = zfull[:, a * HD:(a + 1) * HD]
            ew = [(s_v, Z_DONE), (s_out, M_out_exit)] + list(M_exitst)
            b.op("scalar", (lambda zt_: (lambda e: e.activation(
                out=exitst, in_=zt_, func=AF.Square, accum_out=cols[:, 24:25])))(zt),
                waits=ew, incs=[(s_sc, 1)])
            b.op("scalar", (lambda zt_: (lambda e: e.activation(
                out=exitst, in_=zt_, func=AF.Copy, accum_out=cols[:, 25:26])))(zt),
                incs=[(s_sc, 1)])
            b.op("scalar", lambda e: e.mul(out=cols[:, 24:26], in_=cols[:, 24:26], mul=ccol("inv2048")))
            b.op("scalar", lambda e: e.activation(out=cols[:, 26:27], in_=cols[:, 25:26], func=AF.Square))
            b.op("scalar", lambda e: e.mul(out=cols[:, 26:27], in_=cols[:, 26:27], mul=-1.0))
            b.op("scalar", lambda e: e.add(out=cols[:, 24:25], in_=cols[:, 24:25], add=cols[:, 26:27]))
            b.op("scalar", lambda e: e.activation(out=cols[:, 24:25], in_=cols[:, 24:25], func=AF.Ln,
                                                  bias=ccol("eps")))
            b.op("scalar", lambda e: e.activation(out=cols[:, 24:25], in_=cols[:, 24:25], func=AF.Exp,
                                                  scale=ccol("mhalf")), incs=[(s_sc, 1)])
            SC_LN2 = s_sc.n
            b.op("vector", (lambda zt_: (lambda e: e.tensor_scalar(
                out=exitst, in0=zt_, scalar1=cols[:, 25:26], scalar2=cols[:, 24:25],
                op0=ALU.subtract, op1=ALU.mult)))(zt),
                waits=[(s_sc, SC_LN2)], incs=[(s_v, 1)])
            V_OUT = s_v.n
            b.dma("sync", d_out[seq, tok0 + a * 128: tok0 + (a + 1) * 128, :], exitst, s_out,
                  waits=[(s_v, V_OUT)])
            M_out_exit = s_out.n
            M_exitst = [(s_out, M_out_exit)]

        M_proj_free = [(s_pe, s_pe.n)]
        M_a5_free = [(s_pe, s_pe.n)]
        M_xsy_free = [(s_v, Z_DONE), (s_pe, s_pe.n)]
        if first_of_seq is False and g + 1 < NGRP and (g + 1) % (NGRP_FULL // SEQ) == 0:
            pass
        if g + 1 < NGRP and (g + 1) % (NGRP_FULL // SEQ) == 0:
            b.op("gpsimd", lambda e: e.memset(carry, 0.0),
                 waits=list(M_carry) + [(s_pe, s_pe.n)], incs=[(s_g, 1)])
            M_carry = [(s_g, s_g.n)]

    local_map = dict(locals())
    for nm in cfg.get("dumps", []):
        ap = local_map[nm]
        dd = nc.dram_tensor("dump_" + nm, list(ap.shape), ap.dtype, kind="ExternalOutput").ap()
        b.dma("sync", dd, ap, s_out,
              waits=[(s_v, s_v.n), (s_sc, s_sc.n), (s_pe, s_pe.n), (s_g, s_g.n),
                     (s_w, s_w.n), (s_io, s_io.n), (s_d2, s_d2.n)])

    b.op("sync", lambda e: e.nop(), waits=[(s_out, s_out.n)])

    import contextlib
    with contextlib.ExitStack() as ctx:
        for s in b.sems:
            s.h = ctx.enter_context(nc.semaphore(s.name))
        block = ctx.enter_context(nc.Block())
        b.emit(block)
    return nc



def _stage(inputs):
    ip = {k: np.asarray(v) for k, v in inputs.items()}
    st = {}
    win = np.ascontiguousarray(ip["L0_attn_in_proj_w"].T).astype(np.float32, copy=False)
    win_pad = np.zeros((HD, PROJ_PAD), np.float32)
    win_pad[:, :PROJ] = win
    st["w_in_T"] = win_pad
    st["w_op_T"] = np.ascontiguousarray(ip["L0_attn_out_proj_w"].T)
    st["w_up_T"] = np.ascontiguousarray(ip["L0_ffn_W_up"].T)
    st["w_gt_T"] = np.ascontiguousarray(ip["L0_ffn_W_gate"].T)
    st["w_dn_T"] = np.ascontiguousarray(ip["L0_ffn_W_down"].T)
    st["convw_in"] = np.ascontiguousarray(
        ip["L0_attn_conv_w"].reshape(NT_XBC, 128, KC).transpose(1, 0, 2).reshape(128, NT_XBC * KC))
    st["convb_in"] = np.ascontiguousarray(ip["L0_attn_conv_b"].reshape(NT_XBC, 128).T)
    st["lnw_cols"] = np.ascontiguousarray(ip["L0_attn_ln_weight"].reshape(16, 128).T)
    st["normw_cols"] = np.ascontiguousarray(ip["L0_attn_norm_weight"].reshape(32, 128).T)
    st["su_cols"] = np.ascontiguousarray(ip["L0_ffn_SU_ug"].reshape(16, 128).T)
    st["sudp_cols"] = np.ascontiguousarray(ip["L0_ffn_SU_dp"].reshape(64, 128).T)
    st["wsu_up"] = np.ascontiguousarray(ip["L0_ffn_Wscale_ug"][:INTER].reshape(64, 128).T)
    st["wsu_gt"] = np.ascontiguousarray(ip["L0_ffn_Wscale_ug"][INTER:].reshape(64, 128).T)
    st["wsdp_rep"] = np.ascontiguousarray(np.broadcast_to(ip["L0_ffn_Wscale_dp"][None, :], (64, HD)))
    st["d_cols_in"] = np.ascontiguousarray(np.repeat(ip["L0_attn_D"], P).reshape(32, 128).T)
    st["dt_bias"] = np.ascontiguousarray(ip["L0_attn_dt_bias"].reshape(NH, 1))
    st["a_log"] = np.ascontiguousarray(ip["L0_attn_A_log"].reshape(NH, 1))
    st["h128"] = _hadamard(128)
    st["h64"] = _hadamard(64)
    st["causalT"] = np.ascontiguousarray(np.triu(np.ones((128, 128), np.float32)))
    return st


def kernel(**inputs):
    cfg = {}
    nc = build_program(cfg)
    shared = _stage(inputs)
    x_full = np.asarray(inputs["L0_attn_hidden_states"])
    in_maps = []
    for c in range(8):
        m = dict(shared)
        m["x"] = np.ascontiguousarray(x_full[2 * c:2 * c + 2])
        in_maps.append(m)
    res = run_bass_kernel_spmd(nc, in_maps, core_ids=list(range(8)))
    out = np.concatenate([np.asarray(r["out"]) for r in res.results], axis=0)
    return out


# version 03
# speedup vs baseline: 1.0020x; 1.0020x over previous
import math
import numpy as np
import ml_dtypes
import concourse.bass as bass
import concourse.mybir as mybir
from concourse.bass_utils import run_bass_kernel_spmd

F32 = mybir.dt.float32
BF16 = mybir.dt.bfloat16
AF = mybir.ActivationFunctionType
ALU = mybir.AluOpType
AXL = mybir.AxisListType

SEQ = 2
T = 2048
HD = 2048
I = 4096
NH = 64
P = 64
NS = 64
KC = 4
CONV = I + 2 * NS
PROJ = I + CONV + NH
PROJ_PAD = 8448
NT_GATE = 32
NT_XBC = 33
CH = 128
G = 512
NCHUNK = G // CH
NGRP_FULL = SEQ * T // G
INTER = 8192
EPS = 1e-5

bfnp = ml_dtypes.bfloat16


def _hadamard(n):
    h = np.array([[1.0]], dtype=np.float64)
    while h.shape[0] < n:
        h = np.block([[h, h], [h, -h]])
    return np.ascontiguousarray(h.astype(np.float32))


class Sem:
    __slots__ = ("name", "n", "h")

    def __init__(self, name):
        self.name = name
        self.n = 0
        self.h = None


class Op:
    __slots__ = ("eng", "fn", "waits", "incs")

    def __init__(self, eng, fn, waits, incs):
        self.eng = eng
        self.fn = fn
        self.waits = waits
        self.incs = incs


class Builder:
    def __init__(self, cfg=None):
        self.cfg = cfg or {}
        self.ops = []
        self.sems = []

    def sem(self, name):
        s = Sem(name)
        self.sems.append(s)
        return s

    def op(self, eng, fn, waits=(), incs=()):
        winc = []
        for s, k in incs:
            s.n += k
            winc.append((s, k))
        self.ops.append(Op(eng, fn, [(s, v) for (s, v) in waits], winc))

    def dma(self, eng, out, in_, sem, waits=(), **kw):
        def fn(e, o=out, i=in_, kw2=dict(kw)):
            return e.dma_start(out=o, in_=i, **kw2)
        self.op(eng, fn, waits=waits, incs=[(sem, 16)])

    def emit(self, block):
        for eng_name in ["sync", "tensor", "vector", "scalar", "gpsimd"]:
            ops = [o for o in self.ops if o.eng == eng_name]

            def make(ops_):
                def run(eng):
                    waited = {}
                    for o in ops_:
                        for (s, v) in o.waits:
                            if waited.get(id(s), -1) < v:
                                eng.wait_ge(s.h, v)
                                waited[id(s)] = v
                        inst = o.fn(eng)
                        for (s, k) in o.incs:
                            inst.then_inc(s.h, k)
                return run
            getattr(block, eng_name)(make(ops))


def build_program(cfg):
    b = Builder(cfg)
    nc = bass.Bass("TRN2", target_bir_lowering=False, debug=False)
    NGRP = cfg.get("groups", NGRP_FULL)

    d_x = nc.dram_tensor("x", [SEQ, T, HD], F32, kind="ExternalInput").ap()
    d_win = nc.dram_tensor("w_in_T", [HD, PROJ_PAD], F32, kind="ExternalInput").ap()
    d_wop = nc.dram_tensor("w_op_T", [I, HD], F32, kind="ExternalInput").ap()
    d_wup = nc.dram_tensor("w_up_T", [HD, INTER], F32, kind="ExternalInput").ap()
    d_wgt = nc.dram_tensor("w_gt_T", [HD, INTER], F32, kind="ExternalInput").ap()
    d_wdn = nc.dram_tensor("w_dn_T", [INTER, HD], F32, kind="ExternalInput").ap()
    d_convw = nc.dram_tensor("convw_in", [128, NT_XBC * KC], F32, kind="ExternalInput").ap()
    d_convb = nc.dram_tensor("convb_in", [128, NT_XBC], F32, kind="ExternalInput").ap()
    d_lnw = nc.dram_tensor("lnw_cols", [128, 16], F32, kind="ExternalInput").ap()
    d_normw = nc.dram_tensor("normw_cols", [128, 32], F32, kind="ExternalInput").ap()
    d_su = nc.dram_tensor("su_cols", [128, 16], F32, kind="ExternalInput").ap()
    d_sudp = nc.dram_tensor("sudp_cols", [128, 64], F32, kind="ExternalInput").ap()
    d_wsu_u = nc.dram_tensor("wsu_up", [128, 64], F32, kind="ExternalInput").ap()
    d_wsu_g = nc.dram_tensor("wsu_gt", [128, 64], F32, kind="ExternalInput").ap()
    d_wsdp = nc.dram_tensor("wsdp_rep", [64, HD], F32, kind="ExternalInput").ap()
    d_dcols = nc.dram_tensor("d_cols_in", [128, 32], F32, kind="ExternalInput").ap()
    d_dtb = nc.dram_tensor("dt_bias", [NH, 1], F32, kind="ExternalInput").ap()
    d_alog = nc.dram_tensor("a_log", [NH, 1], F32, kind="ExternalInput").ap()
    d_h128 = nc.dram_tensor("h128", [128, 128], F32, kind="ExternalInput").ap()
    d_h64 = nc.dram_tensor("h64", [64, 64], F32, kind="ExternalInput").ap()
    d_causal = nc.dram_tensor("causalT", [128, 128], F32, kind="ExternalInput").ap()
    d_out = nc.dram_tensor("out", [SEQ, T, HD], F32, kind="ExternalOutput").ap()

    s_win = nc.dram_tensor("s_win", [66, 128, 2048], BF16, kind="Internal").ap()
    s_wop = nc.dram_tensor("s_wop", [32, 128, 2048], BF16, kind="Internal").ap()
    s_wup = nc.dram_tensor("s_wup", [64, 128, 2048], BF16, kind="Internal").ap()
    s_wgt = nc.dram_tensor("s_wgt", [64, 128, 2048], BF16, kind="Internal").ap()
    s_wdn1 = nc.dram_tensor("s_wdn1", [INTER, HD], BF16, kind="Internal").ap()
    s_wdn = nc.dram_tensor("s_wdn", [64, 128, 2048], BF16, kind="Internal").ap()
    s_acum = nc.dram_tensor("s_acum", [NH, CH], F32, kind="Internal").ap()
    s_row = nc.dram_tensor("s_row", [1, 512], F32, kind="Internal").ap()

    SBUF_BASE = 16384 + 2048
    cur = [SBUF_BASE]
    AL = 64

    def salloc(name, shape, dtype, at=None):
        per = 1
        for d in shape[1:]:
            per *= d
        per *= {F32: 4, BF16: 2}[dtype]
        if at is None:
            off = cur[0]
            cur[0] = (off + per + AL - 1) // AL * AL
        else:
            off = at
            assert at >= SBUF_BASE
        return nc.alloc_sbuf_tensor_at(name, shape, dtype, offset=off).ap(), off

    x_tm, _ = salloc("x_tm", [128, 4 * HD], BF16)
    exitst, exit_off = salloc("exitst", [128, HD], F32)
    xn_stage, _ = salloc("xn_stage", [128, HD], BF16, at=exit_off)
    x1_tm, _ = salloc("x1_tm", [128, 4 * HD], BF16)
    a5, a5_off = salloc("a5", [128, 16 * 512], BF16)
    a6, _ = salloc("a6", [128, 1024], BF16)
    proj, proj_off = salloc("proj", [128, NT_GATE * 512 + NT_XBC * 515 + 512], BF16)
    GATE_COLS = NT_GATE * 512
    XBC_COLS = NT_XBC * 515
    xbcwin = proj_off + GATE_COLS * 2
    replo = [salloc("replo0", [128, 1024], F32, at=xbcwin)[0],
             salloc("replo1", [128, 1024], F32, at=xbcwin + 4096)[0]]
    scoreso = [salloc("scores0", [128, 1024], BF16, at=xbcwin + 8192)[0],
               salloc("scores1", [128, 1024], BF16, at=xbcwin + 10240)[0]]
    bdeco = [salloc("bdec0", [128, 512], BF16, at=xbcwin + 12288)[0],
             salloc("bdec1", [128, 512], BF16, at=xbcwin + 13312)[0]]
    xct, _ = salloc("xct", [128, 4096], BF16, at=xbcwin + 14336)
    cdeco = [salloc("cdec0", [64, 1024], BF16, at=xbcwin + 22528)[0],
             salloc("cdec1", [64, 1024], BF16, at=xbcwin + 24576)[0]]
    y2win, _ = salloc("y2win", [128, NT_GATE * 512], BF16, at=xbcwin)
    pp_in = [salloc("pp_in0", [128, 8192], BF16, at=proj_off)[0],
             salloc("pp_in1", [128, 8192], BF16, at=proj_off + 16384)[0]]
    pp_inf = [salloc("pp_inf0", [128, 4096], F32, at=proj_off)[0],
              salloc("pp_inf1", [128, 4096], F32, at=proj_off + 16384)[0]]
    pp_out = [salloc("pp_out0", [128, 8192], BF16, at=proj_off + 32768)[0],
              salloc("pp_out1", [128, 8192], BF16, at=proj_off + 49152)[0]]
    wsdp, _ = salloc("wsdp", [64, HD], F32, at=proj_off + 65536)

    xsy, xsy_off = salloc("xsy", [128, NT_XBC * 512], BF16)
    ub1, _ = salloc("ub1", [128, 8192], BF16, at=xsy_off)
    zfull, _ = salloc("zfull", [128, 8192], BF16, at=xsy_off)
    usg, _ = salloc("usg", [128, 2048], BF16, at=xsy_off + 16384)

    carry, _ = salloc("carry", [64, 4096], F32)
    carryb = [salloc("carryb0", [64, 512], BF16)[0],
              salloc("carryb1", [64, 512], BF16)[0]]
    dt_sp, _ = salloc("dt_sp", [64, G], F32)
    dt_spb, _ = salloc("dt_spb", [64, G], BF16)
    acum, _ = salloc("acum", [64, G], F32)
    dA_t, _ = salloc("dA_t", [64, G], F32)
    dshm, _ = salloc("dshm", [64, CH], BF16)
    dssub, _ = salloc("dssub", [64, CH], F32)
    wbuf = [salloc("wbufA", [128, 2048], BF16)[0],
            salloc("wbufB", [128, 2048], BF16)[0]]
    sg = [salloc("sgA", [128, 512], BF16)[0],
          salloc("sgB", [128, 512], BF16)[0]]
    tmpg, _ = salloc("tmpg", [128, 512], BF16)
    halo, _ = salloc("halo", [128, NT_XBC * 3], BF16)
    cols, _ = salloc("cols", [128, 32], F32)
    consts, _ = salloc("consts", [128, 16], F32)
    lnw, _ = salloc("lnw", [128, 16], F32)
    normw, _ = salloc("normw", [128, 32], F32)
    su, _ = salloc("su", [128, 16], F32)
    cu, _ = salloc("cu", [128, 64], F32)
    cg, _ = salloc("cg", [128, 64], F32)
    sudp_t, _ = salloc("sudp_t", [128, 64], F32)
    dcol, _ = salloc("dcol", [128, 32], F32)
    convw, _ = salloc("convw", [128, NT_XBC * KC], F32)
    convb, _ = salloc("convb", [128, NT_XBC], F32)
    idb, _ = salloc("idb", [128, 128], BF16)
    idf, _ = salloc("idf", [128, 128], F32)
    h128b, _ = salloc("h128b", [128, 128], BF16)
    h64b, _ = salloc("h64b", [64, 64], BF16)
    causal, _ = salloc("causal", [128, 128], BF16)
    onesb, _ = salloc("onesb", [128, 1], BF16)
    onesf, _ = salloc("onesf", [1, 128], F32)
    ones64, _ = salloc("ones64", [64, CH], F32)
    abias, _ = salloc("abias", [64, 1], F32)
    acol, _ = salloc("acol", [64, 1], F32)
    cc_buf, _ = salloc("cc_buf", [64, 512], BF16)
    bct, _ = salloc("bct", [128, 64], BF16)
    dtt, _ = salloc("dtt", [128, 64], BF16)
    act_t, _ = salloc("act_t", [128, 64], F32)
    dst, _ = salloc("dst", [128, 64], BF16)
    cbm, _ = salloc("cbm", [128, 128], BF16)
    cdecr, _ = salloc("cdecr", [64, 64], F32)
    rowst, _ = salloc("rowst", [2, 512], F32)

    assert cur[0] <= 228000, f"SBUF over budget: {cur[0]} bytes/partition"

    psB = [nc.alloc_psum_tensor(f"psb{i}", [128, 1024], BF16).ap() for i in range(2)]
    psF = [nc.alloc_psum_tensor(f"psf{i}", [128, 512], F32).ap() for i in range(6)]

    CONSTS = {"inv2048": (0, 1.0 / 2048.0), "eps": (1, EPS), "mhalf": (2, -0.5),
              "invs2048": (3, 1.0 / math.sqrt(2048.0)), "one": (4, 1.0),
              "invs8192": (5, 1.0 / math.sqrt(8192.0)), "neg1": (6, -1.0),
              "inv4096": (7, 1.0 / 4096.0), "ln05": (8, -0.6931471805599453)}

    def ccol(name, lo=0, hi=128):
        return consts[lo:hi, CONSTS[name][0]:CONSTS[name][0] + 1]

    def proj_gate(t):
        return proj[:, t * 512:(t + 1) * 512]

    def proj_xbc(t, lo=0, span=512):
        base = GATE_COLS + t * 515 + 3
        return proj[:, base + lo:base + lo + span]

    def proj_xbc_halo(t):
        base = GATE_COLS + t * 515
        return proj[:, base:base + 515]

    def proj_dtile():
        base = GATE_COLS + XBC_COLS
        return proj[0:64, base:base + 512]

    def xs_tile(t):
        return xsy[:, t * 512:(t + 1) * 512]

    s_io = b.sem("io")
    s_w = b.sem("w")
    s_d2 = b.sem("d2")
    s_pe = b.sem("pe")
    s_v = b.sem("v")
    s_sc = b.sem("sc")
    s_g = b.sem("g")
    s_out = b.sem("out")

    psf_free = [[] for _ in range(6)]
    psb_free = [[] for _ in range(2)]

    for dst_, src_ in [(convw, d_convw), (convb, d_convb), (lnw, d_lnw), (normw, d_normw),
                       (su, d_su), (dcol, d_dcols), (wsdp, d_wsdp), (abias, d_dtb),
                       (acol, d_alog), (cu, d_wsu_u), (cg, d_wsu_g), (sudp_t, d_sudp)]:
        b.dma("sync", dst_, src_, s_w)
    b.dma("gpsimd", h128b, d_h128, s_io)
    b.dma("gpsimd", h64b, d_h64, s_io)
    b.dma("gpsimd", causal, d_causal, s_io)
    W0 = s_w.n
    IO0 = s_io.n

    b.op("vector", lambda e: e.memset(onesb, 1.0))
    b.op("vector", lambda e: e.memset(onesf, 1.0))
    b.op("vector", lambda e: e.memset(ones64, 1.0))
    for nm, (idx, val) in CONSTS.items():
        b.op("vector", (lambda i_, v_: (lambda e: e.memset(consts[:, i_:i_ + 1], v_)))(idx, val))
    b.op("gpsimd", lambda e: e.memset(idb, 0.0))
    b.op("gpsimd", lambda e: e.affine_select(out=idb, in_=idb, compare_op=ALU.not_equal,
                                             fill=1.0, base=0, pattern=[[-1, 128]], channel_multiplier=1))
    b.op("gpsimd", lambda e: e.memset(idf, 0.0))
    b.op("gpsimd", lambda e: e.affine_select(out=idf, in_=idf, compare_op=ALU.not_equal,
                                             fill=1.0, base=0, pattern=[[-1, 128]], channel_multiplier=1))
    b.op("gpsimd", lambda e: e.memset(halo, 0.0))
    b.op("gpsimd", lambda e: e.memset(carry, 0.0), incs=[(s_g, 1)])
    G_INIT = s_g.n

    b.op("scalar", lambda e: e.activation(out=acol, in_=acol, func=AF.Exp),
         waits=[(s_w, W0)], incs=[(s_sc, 1)])
    SC_A = s_sc.n
    b.op("vector", lambda e: e.tensor_scalar(out=acol, in0=acol, scalar1=-1.0,
                                             scalar2=None, op0=ALU.mult),
         waits=[(s_sc, SC_A), (s_w, W0)])
    b.op("vector", lambda e: e.tensor_tensor(out=cu, in0=cu, in1=sudp_t, op=ALU.mult),
         waits=[(s_w, W0)])
    b.op("vector", lambda e: e.tensor_scalar(out=cu, in0=cu, scalar1=ccol("invs2048"),
                                             scalar2=None, op0=ALU.mult))
    b.op("vector", lambda e: e.tensor_scalar(out=cg, in0=cg, scalar1=ccol("invs2048"),
                                             scalar2=None, op0=ALU.mult))
    b.op("vector", lambda e: e.tensor_scalar(out=wsdp, in0=wsdp, scalar1=ccol("invs8192", 0, 64),
                                             scalar2=None, op0=ALU.mult), incs=[(s_v, 1)])
    V_CONST = s_v.n

    skip_pre = cfg.get("skip_prepass", False)
    if not skip_pre:
        swin_v = s_win.rearrange("(pb kh) kp (ktl p) -> pb kh kp ktl p", kh=2, p=256)
        for i in range(16):
            p1dst = swin_v[:, i // 8, :, i % 8, :].transpose([1, 0, 2])
            srcv2 = d_win[i * 128:(i + 1) * 128, :].rearrange("kp (pb p) -> kp pb p", p=256)
            b.dma("gpsimd", p1dst, srcv2, s_io)

        mk = {}
        for i in range(32):
            par = i % 2
            fin = pp_inf[par][:, 0:2048]
            fout = pp_out[par][:, 0:2048]
            waits = [(s_v, V_CONST)]
            if i >= 2:
                waits.append((s_w, mk[("p2o", i - 2)]))
            b.dma("sync", fin, d_wop[i * 128:(i + 1) * 128, :], s_w, waits=waits)
            W_IN = s_w.n
            b.op("vector", (lambda fin_, fout_, i_: (lambda e: e.tensor_scalar(
                out=fout_, in0=fin_, scalar1=normw[:, i_:i_ + 1], scalar2=None, op0=ALU.mult)))(fin, fout, i),
                waits=[(s_w, W_IN)], incs=[(s_v, 1)])
            swop_v = s_wop.rearrange("(pb kq) kp (ktl p) -> pb kq kp ktl p", kq=4, p=256)
            b.dma("sync", swop_v[:, i // 8, :, i % 8, :].transpose([1, 0, 2]),
                  fout.rearrange("kp (pb p) -> kp pb p", p=256),
                  s_w, waits=[(s_v, s_v.n)])
            mk[("p2o", i)] = s_w.n

        P2_DONE = mk[("p2o", 31)]
        for wi, (dsrc, ddst) in enumerate([(d_wup, s_wup), (d_wgt, s_wgt)]):
            for kt in range(16):
                it = wi * 16 + kt
                par = it % 2
                waits = [(s_w, P2_DONE)]
                if it >= 2:
                    waits.append((s_w, mk[("p3o", it - 2)]))
                iow = s_io.n
                b.dma("gpsimd", pp_in[par][:, 0:4096], dsrc[kt * 128:(kt + 1) * 128, 0:4096], s_io, waits=waits)
                b.dma("gpsimd", pp_in[par][:, 4096:8192], dsrc[kt * 128:(kt + 1) * 128, 4096:8192], s_io, waits=waits)
                IOW = s_io.n
                for j in range(16):
                    bk = j % 2
                    b.op("tensor", (lambda par_, j_, bk_: (lambda e: e.matmul(
                        psF[bk_], h128b, pp_in[par_][:, j_ * 512:(j_ + 1) * 512],
                        start=True, stop=True)))(par, j, bk),
                        waits=[(s_io, IOW), (s_io, IO0)] + psf_free[bk], incs=[(s_pe, 1)])
                    b.op("vector", (lambda par_, j_, bk_: (lambda e: e.tensor_copy(
                        out=pp_out[par_][:, j_ * 512:(j_ + 1) * 512], in_=psF[bk_])))(par, j, bk),
                        waits=[(s_pe, s_pe.n)], incs=[(s_v, 1)])
                    psf_free[bk] = [(s_v, s_v.n)]
                ddst_v = ddst.rearrange("(pb kh) kp (ktl p) -> pb kh kp ktl p", kh=2, p=256)
                b.dma("sync", ddst_v[:, kt // 8, :, kt % 8, :].transpose([1, 0, 2]),
                      pp_out[par].rearrange("kp (pb p) -> kp pb p", p=256),
                      s_w, waits=[(s_v, s_v.n)])
                mk[("p3o", it)] = s_w.n

        dwdn_v = d_wdn.rearrange("(i jr) r -> i (jr r)", i=64)
        swdn1_v = s_wdn1.rearrange("(i jr) r -> i (jr r)", i=64)
        P3_DONE = mk[("p3o", 31)]
        for c in range(32):
            par = c % 2
            fin = pp_in[par][0:64, 0:8192]
            fout = pp_out[par][0:64, 0:8192]
            waits = [(s_w, P3_DONE)]
            if c >= 2:
                waits.append((s_w, mk[("p4o", c - 2)]))
            b.dma("gpsimd", fin, dwdn_v[:, c * 8192:(c + 1) * 8192], s_io, waits=waits)
            IOW = s_io.n
            for j in range(16):
                bk = j % 2
                rstart = (c * 8192 + j * 512) % 2048
                b.op("tensor", (lambda fin_, j_, bk_: (lambda e: e.matmul(
                    psF[bk_][0:64, :], h64b, fin_[:, j_ * 512:(j_ + 1) * 512],
                    start=True, stop=True)))(fin, j, bk),
                    waits=[(s_io, IOW), (s_v, V_CONST)] + psf_free[bk], incs=[(s_pe, 1)])
                b.op("vector", (lambda fout_, j_, rs_, bk_: (lambda e: e.tensor_tensor(
                    out=fout_[:, j_ * 512:(j_ + 1) * 512], in0=psF[bk_][0:64, :],
                    in1=wsdp[:, rs_:rs_ + 512], op=ALU.mult)))(fout, j, rstart, bk),
                    waits=[(s_pe, s_pe.n)], incs=[(s_v, 1)])
                psf_free[bk] = [(s_v, s_v.n)]
            b.dma("sync", swdn1_v[:, c * 8192:(c + 1) * 8192], fout, s_w, waits=[(s_v, s_v.n)])
            mk[("p4o", c)] = s_w.n

        for kt in range(64):
            par = kt % 2
            fin = pp_in[par][:, 0:2048]
            fout = pp_out[par][:, 0:2048]
            waits = [(s_w, mk[("p4o", 31)])]
            if kt >= 2:
                waits.append((s_w, mk[("p5o", kt - 2)]))
            b.dma("sync", fin, s_wdn1[kt * 128:(kt + 1) * 128, :], s_w, waits=waits)
            W_IN = s_w.n
            for j in range(4):
                bk = j % 2
                b.op("tensor", (lambda fin_, j_, bk_: (lambda e: e.matmul(
                    psF[bk_], h128b, fin_[:, j_ * 512:(j_ + 1) * 512],
                    start=True, stop=True)))(fin, j, bk),
                    waits=[(s_w, W_IN)] + psf_free[bk], incs=[(s_pe, 1)])
                b.op("vector", (lambda fout_, j_, bk_: (lambda e: e.tensor_copy(
                    out=fout_[:, j_ * 512:(j_ + 1) * 512], in_=psF[bk_])))(fout, j, bk),
                    waits=[(s_pe, s_pe.n)], incs=[(s_v, 1)])
                psf_free[bk] = [(s_v, s_v.n)]
            swdn_v = s_wdn.rearrange("(pb kq) kp (ktl p) -> pb kq kp ktl p", kq=8, p=256)
            b.dma("sync", swdn_v[:, kt // 8, :, kt % 8, :].transpose([1, 0, 2]),
                  fout.rearrange("kp (pb p) -> kp pb p", p=256),
                  s_w, waits=[(s_v, s_v.n)])
            mk[("p5o", kt)] = s_w.n

    PRE_DONE = [(s_w, s_w.n), (s_v, s_v.n), (s_io, s_io.n), (s_g, G_INIT), (s_sc, s_sc.n)]

    tb_par = [0]

    def tpose_rounds(blocks, first_waits, evac_emit, group=4):
        nrounds = (len(blocks) + group - 1) // group
        for r in range(nrounds):
            bank = tb_par[0] % 2
            tb_par[0] += 1
            chunk = blocks[r * group:(r + 1) * group]
            for si, (in_ap, key) in enumerate(chunk):
                b.op("tensor", (lambda a_, bk_, s_: (lambda e: e.transpose(
                    psB[bk_][:, s_ * 128:(s_ + 1) * 128], a_, idb)))(in_ap, bank, si),
                    waits=list(first_waits) + psb_free[bank], incs=[(s_pe, 1)])
            PE_M = s_pe.n
            last = None
            for si, (in_ap, key) in enumerate(chunk):
                last = evac_emit(key, psB[bank][:, si * 128:(si + 1) * 128], (s_pe, PE_M))
            psb_free[bank] = [last]

    M_out_exit = 0
    M_xtm_free = [(s_v, 0)]
    M_proj_free = list(PRE_DONE)
    M_a5_free = list(PRE_DONE)
    M_xsy_free = [(s_g, G_INIT)]
    M_carry = [(s_g, G_INIT)]
    M_exitst = [(s_out, 0)]

    STOP = cfg.get("stop_after")

    def stop_at(name, _done=[False]):
        order = ["entry", "in_proj", "conv", "dt", "ssd", "op", "ln1", "f1", "f2", "ffn3", "ffn", None]
        if STOP is None:
            return False
        return order.index(STOP) < order.index(name)

    for g in range(NGRP):
        seq = g // (NGRP_FULL // SEQ)
        tok0 = (g % (NGRP_FULL // SEQ)) * G
        first_of_seq = (tok0 == 0)

        src = d_x[seq, tok0:tok0 + G, :].rearrange("(a p) h -> p a h", p=128)
        b.dma("gpsimd", x_tm.rearrange("p (a h) -> p a h", h=HD), src, s_io,
              waits=list(M_xtm_free))
        IO_X = s_io.n

        for a in range(4):
            b.op("scalar", (lambda a_: (lambda e: e.activation(
                out=exitst, in_=x_tm[:, a_ * HD:(a_ + 1) * HD], func=AF.Square,
                accum_out=cols[:, a_:a_ + 1])))(a),
                waits=[(s_io, IO_X)] + list(M_exitst), incs=[(s_sc, 1)])
        b.op("scalar", lambda e: e.activation(out=cols[:, 4:8], in_=cols[:, 0:4], func=AF.Ln,
                                              scale=ccol("inv2048"), bias=ccol("eps")),
             waits=[(s_sc, s_sc.n)])
        b.op("scalar", lambda e: e.activation(out=cols[:, 4:8], in_=cols[:, 4:8], func=AF.Exp,
                                              scale=ccol("mhalf")), incs=[(s_sc, 1)])
        SC_RSTD1 = s_sc.n
        M_exitst = [(s_sc, SC_RSTD1)]

        for a in range(4):
            b.op("vector", (lambda a_: (lambda e: e.tensor_scalar(
                out=xn_stage, in0=x_tm[:, a_ * HD:(a_ + 1) * HD],
                scalar1=cols[:, 4 + a_:5 + a_], scalar2=None, op0=ALU.mult)))(a),
                waits=[(s_sc, SC_RSTD1)] + psb_free[0] + psb_free[1] + list(M_exitst),
                incs=[(s_v, 1)])
            V_STAGE = s_v.n
            M_exitst = [(s_v, V_STAGE)]

            def evac_xn(key, slot_ap, pe_mark, a_=a):
                i_ = key
                b.op("vector", (lambda i2, a2, sl: (lambda e: e.tensor_scalar(
                    out=a5[:, i2 * 512 + a2 * 128: i2 * 512 + (a2 + 1) * 128],
                    in0=sl, scalar1=lnw[:, i2:i2 + 1], scalar2=None, op0=ALU.mult)))(i_, a_, slot_ap),
                    waits=[pe_mark] + list(M_a5_free), incs=[(s_v, 1)])
                return (s_v, s_v.n)

            blocks = [(xn_stage[:, i * 128:(i + 1) * 128], i) for i in range(16)]
            tpose_rounds(blocks, [(s_v, V_STAGE)], evac_xn, group=8)
            M_exitst = [(s_pe, s_pe.n)]
        V_XN_DONE = s_v.n
        M_a5_free = []

        if stop_at("in_proj"):
            continue
        ip_ev = {}
        ip_strip = {}
        for pb in range(33):
            for kh in range(2):
                gstrip = pb * 2 + kh
                par = gstrip % 2
                waits = list(M_proj_free) + [(s_v, V_XN_DONE)]
                if gstrip >= 2:
                    waits.append(ip_strip[gstrip - 2])
                b.dma("sync", wbuf[par], s_win[gstrip], s_w, waits=waits)
                W_IP = s_w.n
                for j in range(2):
                    bankF = 2 + (pb % 2) * 2 + j
                    for k8 in range(8):
                        kt = kh * 8 + k8
                        mmw = [(s_w, W_IP)]
                        if kt == 0:
                            mmw += psf_free[bankF]
                        b.op("tensor", (lambda par_, j_, k8_, kt_, bankF_: (lambda e: e.matmul(
                            psF[bankF_], wbuf[par_][:, k8_ * 256 + j_ * 128: k8_ * 256 + j_ * 128 + 128],
                            a5[:, kt_ * 512:(kt_ + 1) * 512],
                            start=(kt_ == 0), stop=(kt_ == 15))))(par, j, k8, kt, bankF),
                            waits=mmw, incs=([(s_pe, 1)] if (k8 == 7 and j == 1) else []))
                ip_strip[gstrip] = (s_pe, s_pe.n)
            PE_MM = s_pe.n
            for j in range(2):
                tI = pb * 2 + j
                bankF = 2 + (pb % 2) * 2 + j
                if tI < NT_GATE:
                    dstap = proj_gate(tI)
                elif tI < NT_GATE + NT_XBC:
                    dstap = proj_xbc(tI - NT_GATE)
                else:
                    dstap = proj[:, GATE_COLS + XBC_COLS: GATE_COLS + XBC_COLS + 512]
                if tI % 2 == 0:
                    b.op("vector", (lambda d_, bf_: (lambda e: e.tensor_copy(out=d_, in_=psF[bf_])))(dstap, bankF),
                         waits=[(s_pe, PE_MM)], incs=[(s_v, 1)])
                    psf_free[bankF] = [(s_v, s_v.n)]
                    ip_ev[tI] = (s_v, s_v.n)
                else:
                    b.op("scalar", (lambda d_, bf_: (lambda e: e.copy(out=d_, in_=psF[bf_])))(dstap, bankF),
                         waits=[(s_pe, PE_MM)], incs=[(s_sc, 1)])
                    psf_free[bankF] = [(s_sc, s_sc.n)]
                    ip_ev[tI] = (s_sc, s_sc.n)
        IP_ALL = [(s_v, s_v.n), (s_sc, s_sc.n)]

        if stop_at("conv"):
            continue
        if first_of_seq and g > 0:
            b.op("gpsimd", lambda e: e.memset(halo, 0.0), waits=list(M_xsy_free), incs=[(s_g, 1)])
        HALO_RDY = (s_g, s_g.n)
        cv_marks = {}
        for t in range(NT_XBC):
            eng = "vector"
            sm = s_v
            w0 = GATE_COLS + t * 515
            b.op(eng, (lambda t_, w0_: (lambda e: e.tensor_copy(
                out=proj[:, w0_:w0_ + 3], in_=halo[:, t_ * 3:t_ * 3 + 3])))(t, w0),
                waits=[ip_ev[NT_GATE + t], HALO_RDY], incs=[(sm, 1)])
            b.op(eng, (lambda t_: (lambda e: e.tensor_copy(
                out=halo[:, t_ * 3:t_ * 3 + 3], in_=proj_xbc(t_, 509, 3))))(t),
                incs=[(sm, 1)])
            b.op(eng, (lambda t_: (lambda e: e.tensor_scalar(
                out=xs_tile(t_), in0=proj_xbc_halo(t_)[:, 0:512],
                scalar1=convw[:, t_ * KC:t_ * KC + 1], scalar2=None, op0=ALU.mult)))(t),
                waits=list(M_xsy_free), incs=[(sm, 1)])
            for k in range(1, KC):
                b.op(eng, (lambda t_, k_: (lambda e: e.scalar_tensor_tensor(
                    out=xs_tile(t_), in0=proj_xbc_halo(t_)[:, k_:k_ + 512],
                    scalar=convw[:, t_ * KC + k_:t_ * KC + k_ + 1], in1=xs_tile(t_),
                    op0=ALU.mult, op1=ALU.add)))(t, k),
                    incs=[(sm, 1)])
            b.op("scalar", (lambda t_: (lambda e: e.activation(
                out=xs_tile(t_), in_=xs_tile(t_), func=AF.Silu, bias=convb[:, t_:t_ + 1])))(t),
                waits=[(sm, sm.n)], incs=[(s_sc, 1)])
            cv_marks[t] = (s_sc, s_sc.n)
        CV_ALL = (s_sc, s_sc.n)
        b.dma("scalar", cc_buf, xs_tile(32)[64:128, :], s_d2, waits=[CV_ALL])
        D2_CC = s_d2.n

        if stop_at("dt"):
            continue
        b.op("vector", lambda e: e.tensor_scalar(out=dt_sp, in0=proj_dtile(),
                                                 scalar1=abias, scalar2=None, op0=ALU.add),
             waits=[ip_ev[65]] + IP_ALL, incs=[(s_v, 1)])
        V_DT0 = s_v.n
        b.op("scalar", lambda e: e.activation(out=dA_t, in_=proj_dtile(), func=AF.Abs,
                                              bias=abias),
             waits=[ip_ev[65]] + IP_ALL, incs=[(s_sc, 1)])
        SC_ABS = s_sc.n
        b.op("scalar", lambda e: e.activation(out=dA_t, in_=dA_t, func=AF.Exp,
                                              scale=ccol("neg1", 0, 64)))
        b.op("scalar", lambda e: e.activation(out=dA_t, in_=dA_t, func=AF.Ln,
                                              bias=ccol("one", 0, 64)), incs=[(s_sc, 1)])
        SC_DT = s_sc.n
        b.op("vector", lambda e: e.tensor_scalar(out=dt_sp, in0=dt_sp, scalar1=0.0,
                                                 scalar2=None, op0=ALU.max))
        b.op("vector", lambda e: e.tensor_tensor(out=dt_sp, in0=dt_sp, in1=dA_t, op=ALU.add),
             waits=[(s_sc, SC_DT)])
        b.op("vector", lambda e: e.tensor_copy(out=dt_spb, in_=dt_sp))
        b.op("vector", lambda e: e.tensor_scalar(out=dA_t, in0=dt_sp, scalar1=acol,
                                                 scalar2=None, op0=ALU.mult), incs=[(s_v, 1)])
        for c in range(NCHUNK):
            b.op("vector", (lambda c_: (lambda e: e.tensor_tensor_scan(
                out=acum[:, c_ * CH:(c_ + 1) * CH], data0=ones64,
                data1=dA_t[:, c_ * CH:(c_ + 1) * CH], initial=0.0,
                op0=ALU.mult, op1=ALU.add)))(c), incs=[(s_v, 1)])
        V_ACUM = s_v.n

        if stop_at("ssd"):
            continue
        prev_repl_reads = None
        gate_done = []
        for c in range(NCHUNK):
            csl = slice(c * CH, (c + 1) * CH)
            fw = [(s_v, V_ACUM)]
            if prev_repl_reads is not None:
                fw += prev_repl_reads
            b.dma("scalar", s_acum, acum[:, csl], s_d2, waits=fw)
            D2_FLAT = s_d2.n

            b.op("vector", (lambda c_: (lambda e: e.tensor_scalar(
                out=dssub, in0=acum[:, c_ * CH:(c_ + 1) * CH],
                scalar1=acum[:, c_ * CH + CH - 1: c_ * CH + CH],
                scalar2=None, op0=ALU.subtract)))(c), waits=[(s_v, V_ACUM)], incs=[(s_v, 1)])
            b.op("scalar", lambda e: e.activation(out=dshm, in_=dssub, func=AF.Exp,
                                                  scale=ccol("neg1", 0, 64)),
                 waits=[(s_v, s_v.n)], incs=[(s_sc, 1)])
            SC_DS = s_sc.n

            bank = tb_par[0] % 2
            tb_par[0] += 1
            tw = psb_free[bank] + [(s_v, s_v.n), (s_sc, SC_DS)]
            b.op("tensor", (lambda c_, bk_: (lambda e: e.transpose(
                psB[bk_][:, 0:64], dt_spb[:, c_ * CH:(c_ + 1) * CH], idb[0:64, 0:64])))(c, bank),
                waits=tw, incs=[(s_pe, 1)])
            b.op("tensor", (lambda bk_: (lambda e: e.transpose(
                psB[bk_][:, 64:128], dshm, idb[0:64, 0:64])))(bank), incs=[(s_pe, 1)])
            b.op("tensor", (lambda c_, bk_: (lambda e: e.transpose(
                psB[bk_][:, 128:192], xs_tile(32)[0:64, c_ * CH:(c_ + 1) * CH], idb[0:64, 0:64])))(c, bank),
                waits=[CV_ALL], incs=[(s_pe, 1)])
            b.op("tensor", (lambda c_: (lambda e: e.transpose(
                psF[0][:, 0:64], acum[:, c_ * CH:(c_ + 1) * CH], idf[0:64, 0:64])))(c),
                waits=psf_free[0], incs=[(s_pe, 1)])
            b.op("tensor", (lambda c_: (lambda e: e.transpose(
                psF[0][0:1, 64:128], acum[:, c_ * CH + CH - 1:c_ * CH + CH], idf[0:64, 0:64])))(c),
                incs=[(s_pe, 1)])
            PE_TR = s_pe.n
            b.op("vector", (lambda bk_: (lambda e: e.tensor_copy(out=dtt, in_=psB[bk_][:, 0:64])))(bank),
                 waits=[(s_pe, PE_TR)], incs=[(s_v, 1)])
            b.op("vector", (lambda bk_: (lambda e: e.tensor_copy(out=dst, in_=psB[bk_][:, 64:128])))(bank),
                 incs=[(s_v, 1)])
            b.op("vector", (lambda bk_: (lambda e: e.tensor_copy(out=bct, in_=psB[bk_][:, 128:192])))(bank),
                 incs=[(s_v, 1)])
            b.op("vector", lambda e: e.tensor_copy(out=act_t, in_=psF[0][:, 0:64]), incs=[(s_v, 1)])
            b.op("vector", lambda e: e.tensor_copy(out=rowst[0:1, 0:64], in_=psF[0][0:1, 64:128]),
                 incs=[(s_v, 1)])
            V_TR = s_v.n
            psb_free[bank] = [(s_v, V_TR)]
            psf_free[0] = [(s_v, V_TR)]
            b.op("tensor", lambda e: e.matmul(psF[1][0:64, 0:64], onesf[:, 0:64], rowst[0:1, 0:64],
                                              start=True, stop=True),
                 waits=[(s_v, V_TR)] + psf_free[1], incs=[(s_pe, 1)])
            b.op("scalar", lambda e: e.activation(out=cdecr, in_=psF[1][0:64, 0:64], func=AF.Exp),
                 waits=[(s_pe, s_pe.n)], incs=[(s_sc, 1)])
            SC_CDECR = s_sc.n
            b.op("tensor", (lambda c_: (lambda e: e.matmul(
                psF[1][:, 128:256], xs_tile(32)[0:64, c_ * CH:(c_ + 1) * CH],
                cc_buf[:, c_ * CH:(c_ + 1) * CH], start=True, stop=True)))(c),
                waits=[(s_d2, D2_CC), CV_ALL, (s_sc, SC_CDECR)], incs=[(s_pe, 1)])
            PE_CB = s_pe.n
            b.op("vector", lambda e: e.tensor_tensor(out=cbm, in0=psF[1][:, 128:256],
                                                     in1=causal, op=ALU.mult),
                 waits=[(s_pe, PE_CB)], incs=[(s_v, 1)])
            V_CBM = s_v.n
            psf_free[1] = [(s_v, V_CBM)]

            xct_ev = []
            for half in range(4):
                bank = tb_par[0] % 2
                tb_par[0] += 1
                for q in range(8):
                    t = half * 8 + q
                    b.op("tensor", (lambda t_, c_, bk_, q_: (lambda e: e.transpose(
                        psB[bk_][:, q_ * 128:(q_ + 1) * 128],
                        xs_tile(t_)[:, c_ * CH:(c_ + 1) * CH], idb)))(t, c, bank, q),
                        waits=[CV_ALL] + psb_free[bank], incs=[(s_pe, 1)])
                PE_T8 = s_pe.n
                b.op("vector", (lambda half_, bk_: (lambda e: e.tensor_tensor(
                    out=xct.rearrange("p (h f) -> p h f", f=64)[:, half_ * 16:(half_ + 1) * 16, :],
                    in0=psB[bk_].rearrange("p (h f) -> p h f", f=64),
                    in1=dtt[:, half_ * 16:(half_ + 1) * 16].unsqueeze(2).broadcast_to([128, 16, 64]),
                    op=ALU.mult)))(half, bank),
                    waits=[(s_pe, PE_T8), (s_v, V_TR)], incs=[(s_v, 1)])
                psb_free[bank] = [(s_v, s_v.n)]
                xct_ev.append((s_v, s_v.n))
            V_XCT = s_v.n

            psY = [psF[2], psF[3]]
            psS = [psF[4], psF[5]]
            repl_reads = []
            oct_marks = []
            for o in range(8):
                par = o % 2
                rw = [(s_d2, D2_FLAT)]
                if len(oct_marks) >= 2:
                    rw += oct_marks[-2]["repl_reads"]
                b.dma("scalar", replo[par],
                      s_acum.rearrange("h s -> (h s)").rearrange("(o2 f) -> o2 f", o2=1)[:, o * 1024:(o + 1) * 1024].broadcast_to([128, 1024]),
                      s_d2, waits=rw)
                D2_R = s_d2.n
                msk_w = [(s_d2, D2_R), (s_v, V_TR)]
                if len(oct_marks) >= 2:
                    msk_w.append(oct_marks[-2]["y_pe"])
                b.op("vector", (lambda par_, o_: (lambda e: e.tensor_tensor(
                    out=scoreso[par_].rearrange("p (h l) -> p h l", l=128),
                    in0=replo[par_].rearrange("p (h l) -> p h l", l=128),
                    in1=act_t[:, o_ * 8:(o_ + 1) * 8].unsqueeze(2).broadcast_to([128, 8, 128]),
                    op=ALU.subtract)))(par, o), waits=msk_w, incs=[(s_v, 1)])
                V_SUB = s_v.n
                b.op("vector", (lambda par_: (lambda e: e.tensor_scalar(
                    out=scoreso[par_], in0=scoreso[par_], scalar1=0.0, scalar2=None,
                    op0=ALU.min)))(par), incs=[(s_v, 1)])
                V_MS = s_v.n
                b.op("scalar", (lambda par_: (lambda e: e.activation(
                    out=scoreso[par_], in_=scoreso[par_], func=AF.Exp)))(par),
                    waits=[(s_v, V_MS)], incs=[(s_sc, 1)])
                SC_EXP = s_sc.n
                b.op("vector", (lambda par_: (lambda e: e.tensor_tensor(
                    out=scoreso[par_].rearrange("p (h l) -> p h l", l=128),
                    in0=scoreso[par_].rearrange("p (h l) -> p h l", l=128),
                    in1=cbm.rearrange("p (h2 l) -> p h2 l", h2=1).broadcast_to([128, 8, 128]),
                    op=ALU.mult)))(par),
                    waits=[(s_sc, SC_EXP), (s_v, V_CBM)], incs=[(s_v, 1)])
                V_SCORE = s_v.n
                cd_w = [(s_d2, D2_R)]
                if len(oct_marks) >= 2:
                    cd_w.append(oct_marks[-2]["y_pe"])
                b.op("scalar", (lambda par_: (lambda e: e.activation(
                    out=cdeco[par_], in_=replo[par_][0:64, :], func=AF.Exp)))(par),
                    waits=cd_w, incs=[(s_sc, 1)])
                SC_CD = s_sc.n
                b.op("gpsimd", (lambda par_, c_: (lambda e: e.tensor_tensor(
                    out=cdeco[par_].rearrange("p (h l) -> p h l", l=128),
                    in0=cdeco[par_].rearrange("p (h l) -> p h l", l=128),
                    in1=cc_buf[:, c_ * CH:(c_ + 1) * CH].rearrange("p (h2 l) -> p h2 l", h2=1).broadcast_to([64, 8, 128]),
                    op=ALU.mult)))(par, c),
                    waits=[(s_sc, SC_CD), (s_d2, D2_CC)], incs=[(s_g, 1)])
                G_CD = s_g.n
                bd_w = [(s_v, V_TR)]
                if len(oct_marks) >= 2:
                    bd_w.append(oct_marks[-2]["st_pe"])
                b.op("gpsimd", (lambda par_, o_: (lambda e: e.tensor_tensor(
                    out=bdeco[par_].rearrange("p (h n) -> p h n", n=64),
                    in0=bct.rearrange("p (h2 n) -> p h2 n", h2=1).broadcast_to([128, 8, 64]),
                    in1=dst[:, o_ * 8:(o_ + 1) * 8].unsqueeze(2).broadcast_to([128, 8, 64]),
                    op=ALU.mult)))(par, o), waits=bd_w, incs=[(s_g, 1)])
                G_BD = s_g.n
                cb_w = list(M_carry)
                if len(oct_marks) >= 2:
                    cb_w.append(oct_marks[-2]["y_pe"])
                b.op("vector", (lambda par_, o_: (lambda e: e.tensor_copy(
                    out=carryb[par_], in_=carry[:, o_ * 512:(o_ + 1) * 512])))(par, o),
                    waits=cb_w, incs=[(s_v, 1)])
                V_CBo = s_v.n

                ymm_w = [(s_v, V_SCORE), (s_v, V_XCT), (s_v, V_CBo), (s_g, G_CD)]
                if len(gate_done) >= 2:
                    ymm_w.append(gate_done[-2])
                for j2 in range(4):
                    h = 2 * j2
                    habs = o * 8 + h
                    b.op("tensor", (lambda par_, j2_, habs_, h_, f_: (lambda e: e.matmul(
                        psY[par_][0:64, j2_ * 128:(j2_ + 1) * 128],
                        xct[:, habs_ * 64:(habs_ + 1) * 64],
                        scoreso[par_][:, h_ * 128:(h_ + 1) * 128],
                        start=f_, stop=False, skip_group_check=True)))(par, j2, habs, h, j2 == 0),
                        waits=ymm_w, incs=[(s_pe, 1)])
                    b.op("tensor", (lambda par_, j2_, h_: (lambda e: e.matmul(
                        psY[par_][0:64, j2_ * 128:(j2_ + 1) * 128],
                        carryb[par_][:, h_ * 64:(h_ + 1) * 64],
                        cdeco[par_][:, h_ * 128:(h_ + 1) * 128],
                        start=False, stop=False, skip_group_check=True)))(par, j2, h),
                        incs=[(s_pe, 1)])
                    b.op("tensor", (lambda par_, j2_, habs_, h_, f_: (lambda e: e.matmul(
                        psY[par_][64:128, j2_ * 128:(j2_ + 1) * 128],
                        xct[:, (habs_ + 1) * 64:(habs_ + 2) * 64],
                        scoreso[par_][:, (h_ + 1) * 128:(h_ + 2) * 128],
                        start=f_, stop=False, skip_group_check=True,
                        tile_position=(0, 64))))(par, j2, habs, h, j2 == 0),
                        incs=[(s_pe, 1)])
                    b.op("tensor", (lambda par_, j2_, h_, l_: (lambda e: e.matmul(
                        psY[par_][64:128, j2_ * 128:(j2_ + 1) * 128],
                        carryb[par_][:, (h_ + 1) * 64:(h_ + 2) * 64],
                        cdeco[par_][:, (h_ + 1) * 128:(h_ + 2) * 128],
                        start=False, stop=l_, skip_group_check=True,
                        tile_position=(0, 64))))(par, j2, h, j2 == 3),
                        incs=[(s_pe, 1)])
                PE_Y = s_pe.n
                st_w = [(s_g, G_BD), (s_v, V_XCT)]
                if len(oct_marks) >= 2:
                    st_w.append(oct_marks[-2]["st_rd"])
                for hh in range(8):
                    habs = o * 8 + hh
                    b.op("tensor", (lambda par_, hh_, habs_: (lambda e: e.matmul(
                        psS[par_][0:64, hh_ * 64:(hh_ + 1) * 64],
                        bdeco[par_][:, hh_ * 64:(hh_ + 1) * 64],
                        xct[:, habs_ * 64:(habs_ + 1) * 64],
                        start=(hh_ == 0), stop=(hh_ == 7), skip_group_check=True)))(par, hh, habs),
                        waits=st_w, incs=([(s_pe, 1)] if hh == 7 else []))
                PE_S = s_pe.n

                sg_w = [(s_pe, PE_Y), CV_ALL] + IP_ALL
                if len(gate_done) >= 2:
                    sg_w.append(gate_done[-2])
                b.op("scalar", (lambda o_, c_, par_: (lambda e: e.activation(
                    out=sg[par_].rearrange("p (t l) -> p t l", l=128),
                    in_=proj[:, 0:GATE_COLS].rearrange("p (t f) -> p t f", f=512)[:, o_ * 4:(o_ + 1) * 4, c_ * CH:(c_ + 1) * CH],
                    func=AF.Silu)))(o, c, par), waits=sg_w, incs=[(s_sc, 1)])
                SC_SG = s_sc.n
                b.op("vector", (lambda o_, c_: (lambda e: e.tensor_tensor(
                    out=tmpg.rearrange("p (t l) -> p t l", l=128),
                    in0=xsy.rearrange("p (t f) -> p t f", f=512)[:, o_ * 4:(o_ + 1) * 4, c_ * CH:(c_ + 1) * CH],
                    in1=dcol[:, o_ * 4:(o_ + 1) * 4].unsqueeze(2).broadcast_to([128, 4, 128]),
                    op=ALU.mult)))(o, c),
                    waits=[(s_v, V_XCT)], incs=[(s_v, 1)])
                b.op("vector", (lambda par_: (lambda e: e.tensor_tensor(
                    out=tmpg, in0=tmpg, in1=psY[par_], op=ALU.add)))(par),
                    waits=[(s_pe, PE_Y)], incs=[(s_v, 1)])
                b.op("vector", (lambda o_, c_, par_: (lambda e: e.tensor_tensor(
                    out=xsy.rearrange("p (t f) -> p t f", f=512)[:, o_ * 4:(o_ + 1) * 4, c_ * CH:(c_ + 1) * CH],
                    in0=tmpg.rearrange("p (t l) -> p t l", l=128),
                    in1=sg[par_].rearrange("p (t l) -> p t l", l=128),
                    op=ALU.mult)))(o, c, par),
                    waits=[(s_sc, SC_SG)], incs=[(s_v, 1)])
                gate_done.append((s_v, s_v.n))

                b.op("gpsimd", (lambda o_: (lambda e: e.tensor_tensor(
                    out=carry.rearrange("p (h f) -> p h f", f=64)[:, o_ * 8:(o_ + 1) * 8, :],
                    in0=carry.rearrange("p (h f) -> p h f", f=64)[:, o_ * 8:(o_ + 1) * 8, :],
                    in1=cdecr[:, o_ * 8:(o_ + 1) * 8].unsqueeze(2).broadcast_to([64, 8, 64]),
                    op=ALU.mult)))(o),
                    waits=[(s_v, V_CBo), (s_sc, SC_CDECR)], incs=[(s_g, 1)])
                G_MUL = s_g.n
                b.op("vector", (lambda o_, par_: (lambda e: e.tensor_tensor(
                    out=carry[:, o_ * 512:(o_ + 1) * 512], in0=carry[:, o_ * 512:(o_ + 1) * 512],
                    in1=psS[par_][0:64, :], op=ALU.add)))(o, par),
                    waits=[(s_g, G_MUL), (s_pe, PE_S)], incs=[(s_v, 1)])
                V_ST = s_v.n
                oct_marks.append({"repl_reads": [(s_v, V_SUB), (s_sc, SC_CD)],
                                  "y_pe": (s_pe, PE_Y), "st_pe": (s_pe, PE_S),
                                  "st_rd": (s_v, V_ST)})
            M_carry = [(s_v, s_v.n)]
            psf_free[2] = [gate_done[-2]]
            psf_free[3] = [gate_done[-1]]
            psf_free[4] = [oct_marks[-2]["st_rd"]]
            psf_free[5] = [oct_marks[-1]["st_rd"]]
            prev_repl_reads = oct_marks[-1]["repl_reads"] + oct_marks[-2]["repl_reads"]
        GATE_ALL = [(s_v, s_v.n)]

        if stop_at("op"):
            continue
        for t in range(NT_GATE):
            b.op("scalar", (lambda t_: (lambda e: e.activation(
                out=y2win[:, t_ * 512:(t_ + 1) * 512], in_=xs_tile(t_), func=AF.Square)))(t),
                waits=GATE_ALL + list(M_carry), incs=[(s_sc, 1)])
        SC_Y2 = s_sc.n
        for t in range(NT_GATE):
            mmw = [(s_sc, SC_Y2)]
            if t == 0:
                mmw += psf_free[0]
            b.op("tensor", (lambda t_: (lambda e: e.matmul(
                psF[0][0:1, :], onesb, y2win[:, t_ * 512:(t_ + 1) * 512],
                start=(t_ == 0), stop=(t_ == NT_GATE - 1))))(t),
                waits=mmw, incs=([(s_pe, 1)] if t == NT_GATE - 1 else []))
        PE_YS = s_pe.n
        b.op("scalar", lambda e: e.activation(out=rowst[0:1, :], in_=psF[0][0:1, :],
                                              func=AF.Ln, scale=ccol("inv4096", 0, 1), bias=ccol("eps", 0, 1)),
             waits=[(s_pe, PE_YS)])
        b.op("scalar", lambda e: e.activation(out=rowst[0:1, :], in_=rowst[0:1, :], func=AF.Exp,
                                              scale=ccol("mhalf", 0, 1), bias=ccol("ln05", 0, 1)),
             incs=[(s_sc, 1)])
        SC_RY = s_sc.n
        psf_free[0] = [(s_sc, SC_RY)]
        for i4 in range(4):
            b.op("tensor", (lambda i_: (lambda e: e.transpose(
                psF[0][:, i_:i_ + 1], rowst[0:1, i_ * 128:(i_ + 1) * 128], idf[0:1, 0:1])))(i4),
                waits=[(s_sc, SC_RY)] + (psf_free[0] if i4 == 0 else []), incs=[(s_pe, 1)])
        b.op("vector", lambda e: e.tensor_copy(out=cols[:, 8:12], in_=psF[0][:, 0:4]),
             waits=[(s_pe, s_pe.n)], incs=[(s_v, 1)])
        psf_free[0] = [(s_v, s_v.n)]
        RY_COLS = (s_v, s_v.n)

        op_mk = {}
        for pb in range(8):
            for kq in range(4):
                gstrip = pb * 4 + kq
                par = gstrip % 2
                waits = [(s_sc, SC_Y2)]
                if gstrip >= 2:
                    waits.append(op_mk[gstrip - 2])
                b.dma("sync", wbuf[par], s_wop[gstrip], s_w, waits=waits)
                W_OP = s_w.n
                for j in range(2):
                    bankF = 2 + (pb % 2) * 2 + j
                    for k8 in range(8):
                        kt = kq * 8 + k8
                        mmw = [(s_w, W_OP)]
                        if kt == 0:
                            mmw += psf_free[bankF]
                        b.op("tensor", (lambda par_, j_, k8_, kt_, bankF_: (lambda e: e.matmul(
                            psF[bankF_], wbuf[par_][:, k8_ * 256 + j_ * 128: k8_ * 256 + j_ * 128 + 128],
                            xs_tile(kt_), start=(kt_ == 0), stop=(kt_ == 31))))(par, j, k8, kt, bankF),
                            waits=mmw, incs=([(s_pe, 1)] if (k8 == 7 and j == 1) else []))
                op_mk[gstrip] = (s_pe, s_pe.n)
            PE_OP2 = s_pe.n
            for j in range(2):
                tI = pb * 2 + j
                bankF = 2 + (pb % 2) * 2 + j
                stg = a6[:, j * 512:(j + 1) * 512]
                b.op("vector", (lambda stg_, bf_: (lambda e: e.tensor_copy(out=stg_, in_=psF[bf_])))(stg, bankF),
                     waits=[(s_pe, PE_OP2)], incs=[(s_v, 1)])
                V_STG = s_v.n
                psf_free[bankF] = [(s_v, V_STG)]

                def evac_z1(key, slot_ap, pe_mark, tI_=tI):
                    a_ = key
                    b.op("vector", (lambda a2, t2, sl: (lambda e: e.scalar_tensor_tensor(
                        out=x_tm[:, a2 * HD + t2 * 128: a2 * HD + (t2 + 1) * 128],
                        in0=sl, scalar=cols[:, 8 + a2:9 + a2],
                        in1=x_tm[:, a2 * HD + t2 * 128: a2 * HD + (t2 + 1) * 128],
                        op0=ALU.mult, op1=ALU.add)))(a_, tI_, slot_ap),
                        waits=[pe_mark, RY_COLS], incs=[(s_v, 1)])
                    return (s_v, s_v.n)

                blocks = [(stg[:, a * 128:(a + 1) * 128], a) for a in range(4)]
                tpose_rounds(blocks, [(s_v, V_STG)], evac_z1, group=4)
        Z1_DONE = s_v.n

        if stop_at("ln1"):
            continue
        for a in range(4):
            b.op("scalar", (lambda a_: (lambda e: e.activation(
                out=exitst, in_=x_tm[:, a_ * HD:(a_ + 1) * HD], func=AF.Square,
                accum_out=cols[:, 28 + a_:29 + a_])))(a),
                waits=[(s_v, Z1_DONE)] + list(M_exitst), incs=[(s_sc, 1)])
            b.op("scalar", (lambda a_: (lambda e: e.activation(
                out=exitst, in_=x_tm[:, a_ * HD:(a_ + 1) * HD], func=AF.Copy,
                accum_out=cols[:, 16 + a_:17 + a_])))(a), incs=[(s_sc, 1)])
        SC_LN1S = s_sc.n
        M_exitst = [(s_sc, SC_LN1S)]
        b.op("scalar", lambda e: e.mul(out=cols[:, 16:20], in_=cols[:, 16:20], mul=ccol("inv2048")))
        b.op("scalar", lambda e: e.mul(out=cols[:, 28:32], in_=cols[:, 28:32], mul=ccol("inv2048")))
        b.op("scalar", lambda e: e.activation(out=cols[:, 20:24], in_=cols[:, 16:20], func=AF.Square))
        b.op("scalar", lambda e: e.mul(out=cols[:, 20:24], in_=cols[:, 20:24], mul=-1.0))
        for a4 in range(4):
            b.op("scalar", (lambda a_: (lambda e: e.add(
                out=cols[:, 12 + a_:13 + a_], in_=cols[:, 28 + a_:29 + a_],
                add=cols[:, 20 + a_:21 + a_])))(a4))
        b.op("scalar", lambda e: e.activation(out=cols[:, 12:16], in_=cols[:, 12:16], func=AF.Ln,
                                              bias=ccol("eps")))
        b.op("scalar", lambda e: e.activation(out=cols[:, 12:16], in_=cols[:, 12:16], func=AF.Exp,
                                              scale=ccol("mhalf")), incs=[(s_sc, 1)])
        SC_R1 = s_sc.n
        for a in range(4):
            b.op("vector", (lambda a_: (lambda e: e.tensor_scalar(
                out=x1_tm[:, a_ * HD:(a_ + 1) * HD], in0=x_tm[:, a_ * HD:(a_ + 1) * HD],
                scalar1=cols[:, 16 + a_:17 + a_], scalar2=cols[:, 12 + a_:13 + a_],
                op0=ALU.subtract, op1=ALU.mult)))(a),
                waits=[(s_sc, SC_R1)], incs=[(s_v, 1)])
        X1_DONE = s_v.n
        M_xtm_free = [(s_v, X1_DONE)]

        if stop_at("f1"):
            continue
        f1_first = GATE_ALL + [(s_v, X1_DONE), (s_sc, SC_Y2)]
        for a in range(4):
            def evac_ub(key, slot_ap, pe_mark, a_=a):
                i_ = key
                b.op("vector", (lambda i2, a2, sl: (lambda e: e.tensor_scalar(
                    out=a5[:, i2 * 512 + a2 * 128: i2 * 512 + (a2 + 1) * 128],
                    in0=sl, scalar1=su[:, i2:i2 + 1], scalar2=None, op0=ALU.mult)))(i_, a_, slot_ap),
                    waits=[pe_mark], incs=[(s_v, 1)])
                return (s_v, s_v.n)

            blocks = [(x1_tm[:, a * HD + i * 128: a * HD + (i + 1) * 128], i) for i in range(16)]
            tpose_rounds(blocks, f1_first, evac_ub, group=8)
        UB_RAW = s_v.n

        if stop_at("f2"):
            continue
        for st in range(4):
            bit = 1 << st
            srcbuf = a5 if st % 2 == 0 else ub1
            dstbuf = ub1 if st % 2 == 0 else a5
            stm = s_v.n
            gm = s_g.n
            extra = [(s_v, UB_RAW)] if st == 0 else []
            for i in range(16):
                if i & bit:
                    continue
                jp = i | bit
                for (oi, sgn) in ((i, ALU.add), (jp, ALU.subtract)):
                    eng = "vector" if (oi % 2 == 0) else "gpsimd"
                    sm = s_v if eng == "vector" else s_g
                    b.op(eng, (lambda sb_, db_, i_, jp_, oi_, sg2: (lambda e: e.tensor_tensor(
                        out=db_[:, oi_ * 512:(oi_ + 1) * 512],
                        in0=sb_[:, i_ * 512:(i_ + 1) * 512],
                        in1=sb_[:, jp_ * 512:(jp_ + 1) * 512],
                        op=sg2)))(srcbuf, dstbuf, i, jp, oi, sgn),
                        waits=[(s_v, stm), (s_g, gm)] + extra, incs=[(sm, 1)])
        b.op("vector", lambda e: e.engine_nop(), waits=[(s_v, s_v.n), (s_g, s_g.n)], incs=[(s_v, 1)])
        UB_DONE = s_v.n

        if stop_at("ffn3"):
            continue
        f3_mk = {}
        sidx = [0]
        for pb in range(32):
            for wsel, wsrc in enumerate([s_wup, s_wgt]):
                for kh in range(2):
                    gstrip = sidx[0]
                    sidx[0] += 1
                    par = gstrip % 2
                    waits = [(s_v, UB_DONE)]
                    if gstrip >= 2:
                        waits.append(f3_mk[gstrip - 2])
                    b.dma("sync", wbuf[par], wsrc[pb * 2 + kh], s_w, waits=waits)
                    W_F3 = s_w.n
                    for j in range(2):
                        if wsel == 0:
                            bankF = 2 + (pb % 2) * 2 + j
                        else:
                            bankF = 0 + j
                        for k8 in range(8):
                            kt = kh * 8 + k8
                            mmw = [(s_w, W_F3)]
                            if kt == 0:
                                mmw += psf_free[bankF]
                            b.op("tensor", (lambda par_, j_, k8_, kt_, bankF_: (lambda e: e.matmul(
                                psF[bankF_], wbuf[par_][:, k8_ * 256 + j_ * 128: k8_ * 256 + j_ * 128 + 128],
                                a5[:, kt_ * 512:(kt_ + 1) * 512],
                                start=(kt_ == 0), stop=(kt_ == 15))))(par, j, k8, kt, bankF),
                                waits=mmw, incs=([(s_pe, 1)] if (k8 == 7 and j == 1) else []))
                    f3_mk[gstrip] = (s_pe, s_pe.n)
            PE_UG = s_pe.n
            for j in range(2):
                tI = pb * 2 + j
                bank_u = 2 + (pb % 2) * 2 + j
                bank_g = 0 + j
                b.op("vector", (lambda tI_, bu_, j_: (lambda e: e.tensor_scalar(
                    out=usg[:, j_ * 1024:j_ * 1024 + 512], in0=psF[bu_],
                    scalar1=cu[:, tI_:tI_ + 1], scalar2=None, op0=ALU.mult)))(tI, bank_u, j),
                    waits=[(s_pe, PE_UG)], incs=[(s_v, 1)])
                psf_free[bank_u] = [(s_v, s_v.n)]
                b.op("scalar", (lambda tI_, bg_, j_: (lambda e: e.activation(
                    out=usg[:, j_ * 1024 + 512:j_ * 1024 + 1024], in_=psF[bg_], func=AF.Silu,
                    scale=cg[:, tI_:tI_ + 1])))(tI, bank_g, j),
                    waits=[(s_pe, PE_UG)], incs=[(s_sc, 1)])
                psf_free[bank_g] = [(s_sc, s_sc.n)]
                b.op("vector", (lambda tI_, j_: (lambda e: e.tensor_tensor(
                    out=proj[:, tI_ * 512:(tI_ + 1) * 512], in0=usg[:, j_ * 1024:j_ * 1024 + 512],
                    in1=usg[:, j_ * 1024 + 512:j_ * 1024 + 1024], op=ALU.mult)))(tI, j),
                    waits=[(s_v, s_v.n), (s_sc, s_sc.n)], incs=[(s_v, 1)])
        V_FFNV = s_v.n

        if stop_at("ffn"):
            continue
        f4_mk = {}
        for pb in range(8):
            for kq in range(8):
                gstrip = pb * 8 + kq
                par = gstrip % 2
                waits = [(s_v, V_FFNV)]
                if gstrip >= 2:
                    waits.append(f4_mk[gstrip - 2])
                b.dma("sync", wbuf[par], s_wdn[pb * 8 + kq], s_w, waits=waits)
                W_D = s_w.n
                for j in range(2):
                    bankF = 2 + (pb % 2) * 2 + j
                    for k8 in range(8):
                        kt = kq * 8 + k8
                        mmw = [(s_w, W_D)]
                        if kt == 0:
                            mmw += psf_free[bankF]
                        b.op("tensor", (lambda par_, j_, k8_, kt_, bankF_: (lambda e: e.matmul(
                            psF[bankF_], wbuf[par_][:, k8_ * 256 + j_ * 128: k8_ * 256 + j_ * 128 + 128],
                            proj[:, kt_ * 512:(kt_ + 1) * 512],
                            start=(kt_ == 0), stop=(kt_ == 63))))(par, j, k8, kt, bankF),
                            waits=mmw, incs=([(s_pe, 1)] if (k8 == 7 and j == 1) else []))
                f4_mk[gstrip] = (s_pe, s_pe.n)
            PE_DN = s_pe.n
            for j in range(2):
                tI = pb * 2 + j
                bankF = 2 + (pb % 2) * 2 + j
                stg = a6[:, j * 512:(j + 1) * 512]
                b.op("vector", (lambda stg_, bf_: (lambda e: e.tensor_copy(out=stg_, in_=psF[bf_])))(stg, bankF),
                     waits=[(s_pe, PE_DN)], incs=[(s_v, 1)])
                V_STG = s_v.n
                psf_free[bankF] = [(s_v, V_STG)]

                def evac_z(key, slot_ap, pe_mark, tI_=tI):
                    a_ = key
                    b.op("vector", (lambda a2, t2, sl: (lambda e: e.tensor_tensor(
                        out=zfull[:, a2 * HD + t2 * 128: a2 * HD + (t2 + 1) * 128],
                        in0=sl, in1=x1_tm[:, a2 * HD + t2 * 128: a2 * HD + (t2 + 1) * 128],
                        op=ALU.add)))(a_, tI_, slot_ap),
                        waits=[pe_mark], incs=[(s_v, 1)])
                    return (s_v, s_v.n)

                blocks = [(stg[:, a * 128:(a + 1) * 128], a) for a in range(4)]
                tpose_rounds(blocks, [(s_v, V_STG)], evac_z, group=4)
        Z_DONE = s_v.n

        for a in range(4):
            zt = zfull[:, a * HD:(a + 1) * HD]
            ew = [(s_v, Z_DONE), (s_out, M_out_exit)] + list(M_exitst)
            b.op("scalar", (lambda zt_: (lambda e: e.activation(
                out=exitst, in_=zt_, func=AF.Square, accum_out=cols[:, 24:25])))(zt),
                waits=ew, incs=[(s_sc, 1)])
            b.op("scalar", (lambda zt_: (lambda e: e.activation(
                out=exitst, in_=zt_, func=AF.Copy, accum_out=cols[:, 25:26])))(zt),
                incs=[(s_sc, 1)])
            b.op("scalar", lambda e: e.mul(out=cols[:, 24:26], in_=cols[:, 24:26], mul=ccol("inv2048")))
            b.op("scalar", lambda e: e.activation(out=cols[:, 26:27], in_=cols[:, 25:26], func=AF.Square))
            b.op("scalar", lambda e: e.mul(out=cols[:, 26:27], in_=cols[:, 26:27], mul=-1.0))
            b.op("scalar", lambda e: e.add(out=cols[:, 24:25], in_=cols[:, 24:25], add=cols[:, 26:27]))
            b.op("scalar", lambda e: e.activation(out=cols[:, 24:25], in_=cols[:, 24:25], func=AF.Ln,
                                                  bias=ccol("eps")))
            b.op("scalar", lambda e: e.activation(out=cols[:, 24:25], in_=cols[:, 24:25], func=AF.Exp,
                                                  scale=ccol("mhalf")), incs=[(s_sc, 1)])
            SC_LN2 = s_sc.n
            b.op("vector", (lambda zt_: (lambda e: e.tensor_scalar(
                out=exitst, in0=zt_, scalar1=cols[:, 25:26], scalar2=cols[:, 24:25],
                op0=ALU.subtract, op1=ALU.mult)))(zt),
                waits=[(s_sc, SC_LN2)], incs=[(s_v, 1)])
            V_OUT = s_v.n
            b.dma("sync", d_out[seq, tok0 + a * 128: tok0 + (a + 1) * 128, :], exitst, s_out,
                  waits=[(s_v, V_OUT)])
            M_out_exit = s_out.n
            M_exitst = [(s_out, M_out_exit)]

        M_proj_free = [(s_pe, s_pe.n)]
        M_a5_free = [(s_pe, s_pe.n)]
        M_xsy_free = [(s_v, Z_DONE), (s_pe, s_pe.n)]
        if first_of_seq is False and g + 1 < NGRP and (g + 1) % (NGRP_FULL // SEQ) == 0:
            pass
        if g + 1 < NGRP and (g + 1) % (NGRP_FULL // SEQ) == 0:
            b.op("gpsimd", lambda e: e.memset(carry, 0.0),
                 waits=list(M_carry) + [(s_pe, s_pe.n)], incs=[(s_g, 1)])
            M_carry = [(s_g, s_g.n)]

    local_map = dict(locals())
    for nm in cfg.get("dumps", []):
        ap = local_map[nm]
        dd = nc.dram_tensor("dump_" + nm, list(ap.shape), ap.dtype, kind="ExternalOutput").ap()
        b.dma("sync", dd, ap, s_out,
              waits=[(s_v, s_v.n), (s_sc, s_sc.n), (s_pe, s_pe.n), (s_g, s_g.n),
                     (s_w, s_w.n), (s_io, s_io.n), (s_d2, s_d2.n)])

    b.op("sync", lambda e: e.nop(), waits=[(s_out, s_out.n)])

    import contextlib
    with contextlib.ExitStack() as ctx:
        for s in b.sems:
            s.h = ctx.enter_context(nc.semaphore(s.name))
        block = ctx.enter_context(nc.Block())
        b.emit(block)
    return nc



def _stage(inputs):
    ip = {k: np.asarray(v) for k, v in inputs.items()}
    st = {}
    win = np.ascontiguousarray(ip["L0_attn_in_proj_w"].T).astype(np.float32, copy=False)
    win_pad = np.zeros((HD, PROJ_PAD), np.float32)
    win_pad[:, :PROJ] = win
    st["w_in_T"] = win_pad
    st["w_op_T"] = np.ascontiguousarray(ip["L0_attn_out_proj_w"].T)
    st["w_up_T"] = np.ascontiguousarray(ip["L0_ffn_W_up"].T)
    st["w_gt_T"] = np.ascontiguousarray(ip["L0_ffn_W_gate"].T)
    st["w_dn_T"] = np.ascontiguousarray(ip["L0_ffn_W_down"].T)
    st["convw_in"] = np.ascontiguousarray(
        ip["L0_attn_conv_w"].reshape(NT_XBC, 128, KC).transpose(1, 0, 2).reshape(128, NT_XBC * KC))
    st["convb_in"] = np.ascontiguousarray(ip["L0_attn_conv_b"].reshape(NT_XBC, 128).T)
    st["lnw_cols"] = np.ascontiguousarray(ip["L0_attn_ln_weight"].reshape(16, 128).T)
    st["normw_cols"] = np.ascontiguousarray(ip["L0_attn_norm_weight"].reshape(32, 128).T)
    st["su_cols"] = np.ascontiguousarray(ip["L0_ffn_SU_ug"].reshape(16, 128).T)
    st["sudp_cols"] = np.ascontiguousarray(ip["L0_ffn_SU_dp"].reshape(64, 128).T)
    st["wsu_up"] = np.ascontiguousarray(ip["L0_ffn_Wscale_ug"][:INTER].reshape(64, 128).T)
    st["wsu_gt"] = np.ascontiguousarray(ip["L0_ffn_Wscale_ug"][INTER:].reshape(64, 128).T)
    st["wsdp_rep"] = np.ascontiguousarray(np.broadcast_to(ip["L0_ffn_Wscale_dp"][None, :], (64, HD)))
    st["d_cols_in"] = np.ascontiguousarray(np.repeat(ip["L0_attn_D"], P).reshape(32, 128).T)
    st["dt_bias"] = np.ascontiguousarray(ip["L0_attn_dt_bias"].reshape(NH, 1))
    st["a_log"] = np.ascontiguousarray(ip["L0_attn_A_log"].reshape(NH, 1))
    st["h128"] = _hadamard(128)
    st["h64"] = _hadamard(64)
    st["causalT"] = np.ascontiguousarray(np.triu(np.ones((128, 128), np.float32)))
    return st


def kernel(**inputs):
    cfg = {}
    nc = build_program(cfg)
    shared = _stage(inputs)
    x_full = np.asarray(inputs["L0_attn_hidden_states"])
    in_maps = []
    for c in range(8):
        m = dict(shared)
        m["x"] = np.ascontiguousarray(x_full[2 * c:2 * c + 2])
        in_maps.append(m)
    res = run_bass_kernel_spmd(nc, in_maps, core_ids=list(range(8)))
    out = np.concatenate([np.asarray(r["out"]) for r in res.results], axis=0)
    return out


# version 04
# speedup vs baseline: 1.0756x; 1.0734x over previous
import math
import numpy as np
import ml_dtypes
import concourse.bass as bass
import concourse.mybir as mybir
from concourse.bass_utils import run_bass_kernel_spmd

F32 = mybir.dt.float32
BF16 = mybir.dt.bfloat16
AF = mybir.ActivationFunctionType
ALU = mybir.AluOpType
AXL = mybir.AxisListType

SEQ = 2
T = 2048
HD = 2048
I = 4096
NH = 64
P = 64
NS = 64
KC = 4
CONV = I + 2 * NS
PROJ = I + CONV + NH
PROJ_PAD = 8448
NT_GATE = 32
NT_XBC = 33
CH = 128
G = 512
NCHUNK = G // CH
NGRP_FULL = SEQ * T // G
INTER = 8192
EPS = 1e-5

bfnp = ml_dtypes.bfloat16


def _hadamard(n):
    h = np.array([[1.0]], dtype=np.float64)
    while h.shape[0] < n:
        h = np.block([[h, h], [h, -h]])
    return np.ascontiguousarray(h.astype(np.float32))


class Sem:
    __slots__ = ("name", "n", "h")

    def __init__(self, name):
        self.name = name
        self.n = 0
        self.h = None


class Op:
    __slots__ = ("eng", "fn", "waits", "incs")

    def __init__(self, eng, fn, waits, incs):
        self.eng = eng
        self.fn = fn
        self.waits = waits
        self.incs = incs


class Builder:
    def __init__(self, cfg=None):
        self.cfg = cfg or {}
        self.ops = []
        self.sems = []

    def sem(self, name):
        s = Sem(name)
        self.sems.append(s)
        return s

    def op(self, eng, fn, waits=(), incs=()):
        winc = []
        for s, k in incs:
            s.n += k
            winc.append((s, k))
        self.ops.append(Op(eng, fn, [(s, v) for (s, v) in waits], winc))

    def dma(self, eng, out, in_, sem, waits=(), **kw):
        def fn(e, o=out, i=in_, kw2=dict(kw)):
            return e.dma_start(out=o, in_=i, **kw2)
        self.op(eng, fn, waits=waits, incs=[(sem, 16)])

    def emit(self, block):
        for eng_name in ["sync", "tensor", "vector", "scalar", "gpsimd"]:
            ops = [o for o in self.ops if o.eng == eng_name]

            def make(ops_):
                def run(eng):
                    waited = {}
                    for o in ops_:
                        for (s, v) in o.waits:
                            if waited.get(id(s), -1) < v:
                                eng.wait_ge(s.h, v)
                                waited[id(s)] = v
                        inst = o.fn(eng)
                        for (s, k) in o.incs:
                            inst.then_inc(s.h, k)
                return run
            getattr(block, eng_name)(make(ops))


def build_program(cfg):
    b = Builder(cfg)
    nc = bass.Bass("TRN2", target_bir_lowering=False, debug=False)
    NGRP = cfg.get("groups", NGRP_FULL)

    d_x = nc.dram_tensor("x", [SEQ, T, HD], F32, kind="ExternalInput").ap()
    d_win = nc.dram_tensor("w_in_T", [HD, PROJ_PAD], F32, kind="ExternalInput").ap()
    d_wop = nc.dram_tensor("w_op_T", [I, HD], F32, kind="ExternalInput").ap()
    d_wup = nc.dram_tensor("w_up_T", [HD, INTER], F32, kind="ExternalInput").ap()
    d_wgt = nc.dram_tensor("w_gt_T", [HD, INTER], F32, kind="ExternalInput").ap()
    d_wdn = nc.dram_tensor("w_dn_T", [INTER, HD], F32, kind="ExternalInput").ap()
    d_convw = nc.dram_tensor("convw_in", [128, NT_XBC * KC], F32, kind="ExternalInput").ap()
    d_convb = nc.dram_tensor("convb_in", [128, NT_XBC], F32, kind="ExternalInput").ap()
    d_lnw = nc.dram_tensor("lnw_cols", [128, 16], F32, kind="ExternalInput").ap()
    d_normw = nc.dram_tensor("normw_cols", [128, 32], F32, kind="ExternalInput").ap()
    d_su = nc.dram_tensor("su_cols", [128, 16], F32, kind="ExternalInput").ap()
    d_sudp = nc.dram_tensor("sudp_cols", [128, 64], F32, kind="ExternalInput").ap()
    d_wsu_u = nc.dram_tensor("wsu_up", [128, 64], F32, kind="ExternalInput").ap()
    d_wsu_g = nc.dram_tensor("wsu_gt", [128, 64], F32, kind="ExternalInput").ap()
    d_wsdp = nc.dram_tensor("wsdp_rep", [64, HD], F32, kind="ExternalInput").ap()
    d_dcols = nc.dram_tensor("d_cols_in", [128, 32], F32, kind="ExternalInput").ap()
    d_dtb = nc.dram_tensor("dt_bias", [NH, 1], F32, kind="ExternalInput").ap()
    d_alog = nc.dram_tensor("a_log", [NH, 1], F32, kind="ExternalInput").ap()
    d_h128 = nc.dram_tensor("h128", [128, 128], F32, kind="ExternalInput").ap()
    d_h64 = nc.dram_tensor("h64", [64, 64], F32, kind="ExternalInput").ap()
    d_causal = nc.dram_tensor("causalT", [128, 128], F32, kind="ExternalInput").ap()
    d_out = nc.dram_tensor("out", [SEQ, T, HD], F32, kind="ExternalOutput").ap()

    s_win = nc.dram_tensor("s_win", [66, 128, 2048], BF16, kind="Internal").ap()
    s_wop = nc.dram_tensor("s_wop", [32, 128, 2048], BF16, kind="Internal").ap()
    s_wup = nc.dram_tensor("s_wup", [64, 128, 2048], BF16, kind="Internal").ap()
    s_wgt = nc.dram_tensor("s_wgt", [64, 128, 2048], BF16, kind="Internal").ap()
    s_wdn1 = nc.dram_tensor("s_wdn1", [INTER, HD], BF16, kind="Internal").ap()
    s_wdn = nc.dram_tensor("s_wdn", [64, 128, 2048], BF16, kind="Internal").ap()
    s_acum = nc.dram_tensor("s_acum", [NH, CH], F32, kind="Internal").ap()
    s_row = nc.dram_tensor("s_row", [1, 512], F32, kind="Internal").ap()

    SBUF_BASE = 16384 + 2048
    cur = [SBUF_BASE]
    AL = 64

    def salloc(name, shape, dtype, at=None):
        per = 1
        for d in shape[1:]:
            per *= d
        per *= {F32: 4, BF16: 2}[dtype]
        if at is None:
            off = cur[0]
            cur[0] = (off + per + AL - 1) // AL * AL
        else:
            off = at
            assert at >= SBUF_BASE
        return nc.alloc_sbuf_tensor_at(name, shape, dtype, offset=off).ap(), off

    x_tm, _ = salloc("x_tm", [128, 4 * HD], BF16)
    exitst, exit_off = salloc("exitst", [128, HD], F32)
    xn_stage, _ = salloc("xn_stage", [128, HD], BF16, at=exit_off)
    x1_tm, _ = salloc("x1_tm", [128, 4 * HD], BF16)
    a5, a5_off = salloc("a5", [128, 16 * 512], BF16)
    a6, _ = salloc("a6", [128, 1024], BF16)
    proj, proj_off = salloc("proj", [128, NT_GATE * 512 + NT_XBC * 515 + 512], BF16)
    GATE_COLS = NT_GATE * 512
    XBC_COLS = NT_XBC * 515
    xbcwin = proj_off + GATE_COLS * 2
    replo = [salloc("replo0", [128, 1024], F32, at=xbcwin)[0],
             salloc("replo1", [128, 1024], F32, at=xbcwin + 4096)[0]]
    scoreso = [salloc("scores0", [128, 1024], BF16, at=xbcwin + 8192)[0],
               salloc("scores1", [128, 1024], BF16, at=xbcwin + 10240)[0]]
    bdeco = [salloc("bdec0", [128, 512], BF16, at=xbcwin + 12288)[0],
             salloc("bdec1", [128, 512], BF16, at=xbcwin + 13312)[0]]
    xct, _ = salloc("xct", [128, 4096], BF16, at=xbcwin + 14336)
    cdeco = [salloc("cdec0", [64, 1024], BF16, at=xbcwin + 22528)[0],
             salloc("cdec1", [64, 1024], BF16, at=xbcwin + 24576)[0]]
    y2win, _ = salloc("y2win", [128, NT_GATE * 512], BF16, at=xbcwin)
    pp_in = [salloc("pp_in0", [128, 8192], BF16, at=proj_off)[0],
             salloc("pp_in1", [128, 8192], BF16, at=proj_off + 16384)[0]]
    pp_inf = [salloc("pp_inf0", [128, 4096], F32, at=proj_off)[0],
              salloc("pp_inf1", [128, 4096], F32, at=proj_off + 16384)[0]]
    pp_out = [salloc("pp_out0", [128, 8192], BF16, at=proj_off + 32768)[0],
              salloc("pp_out1", [128, 8192], BF16, at=proj_off + 49152)[0]]
    wsdp, _ = salloc("wsdp", [64, HD], F32, at=proj_off + 65536)

    xsy, xsy_off = salloc("xsy", [128, NT_XBC * 512], BF16)
    ub1, _ = salloc("ub1", [128, 8192], BF16, at=xsy_off)
    zfull, _ = salloc("zfull", [128, 8192], BF16, at=xsy_off)
    usg, _ = salloc("usg", [128, 2048], BF16, at=xsy_off + 16384)

    wbuf_x = [salloc("wbufC", [128, 2048], BF16, at=xsy_off + 20480)[0],
              salloc("wbufD", [128, 2048], BF16, at=xsy_off + 24576)[0]]
    carry, _ = salloc("carry", [64, 4096], F32)
    carryb = [salloc("carryb0", [64, 512], BF16)[0],
              salloc("carryb1", [64, 512], BF16)[0]]
    dt_sp, _ = salloc("dt_sp", [64, G], F32)
    dt_spb, _ = salloc("dt_spb", [64, G], BF16)
    acum, _ = salloc("acum", [64, G], F32)
    dA_t, _ = salloc("dA_t", [64, G], F32)
    dshm, _ = salloc("dshm", [64, CH], BF16)
    dssub, _ = salloc("dssub", [64, CH], F32)
    wbuf = [salloc("wbufA", [128, 2048], BF16)[0],
            salloc("wbufB", [128, 2048], BF16)[0]]
    wbuf4 = wbuf + wbuf_x
    sg = [salloc("sgA", [128, 512], BF16)[0],
          salloc("sgB", [128, 512], BF16)[0]]
    tmpg, _ = salloc("tmpg", [128, 512], BF16)
    halo, _ = salloc("halo", [128, NT_XBC * 3], BF16)
    cols, _ = salloc("cols", [128, 32], F32)
    consts, _ = salloc("consts", [128, 16], F32)
    lnw, _ = salloc("lnw", [128, 16], F32)
    normw, _ = salloc("normw", [128, 32], F32)
    su, _ = salloc("su", [128, 16], F32)
    cu, _ = salloc("cu", [128, 64], F32)
    cg, _ = salloc("cg", [128, 64], F32)
    sudp_t, _ = salloc("sudp_t", [128, 64], F32)
    dcol, _ = salloc("dcol", [128, 32], F32)
    convw, _ = salloc("convw", [128, NT_XBC * KC], F32)
    convb, _ = salloc("convb", [128, NT_XBC], F32)
    idb, _ = salloc("idb", [128, 128], BF16)
    idf, _ = salloc("idf", [128, 128], F32)
    h128b, _ = salloc("h128b", [128, 128], BF16)
    h64b, _ = salloc("h64b", [64, 64], BF16)
    causal, _ = salloc("causal", [128, 128], BF16)
    onesb, _ = salloc("onesb", [128, 1], BF16)
    onesf, _ = salloc("onesf", [1, 128], F32)
    ones64, _ = salloc("ones64", [64, CH], F32)
    abias, _ = salloc("abias", [64, 1], F32)
    acol, _ = salloc("acol", [64, 1], F32)
    cc_buf, _ = salloc("cc_buf", [64, 512], BF16)
    bct, _ = salloc("bct", [128, 64], BF16)
    dtt, _ = salloc("dtt", [128, 64], BF16)
    act_t, _ = salloc("act_t", [128, 64], F32)
    dst, _ = salloc("dst", [128, 64], BF16)
    cbm, _ = salloc("cbm", [128, 128], BF16)
    cdecr, _ = salloc("cdecr", [64, 64], F32)
    rowst, _ = salloc("rowst", [2, 512], F32)

    assert cur[0] <= 228000, f"SBUF over budget: {cur[0]} bytes/partition"

    psB = [nc.alloc_psum_tensor(f"psb{i}", [128, 1024], BF16).ap() for i in range(2)]
    psF = [nc.alloc_psum_tensor(f"psf{i}", [128, 512], F32).ap() for i in range(6)]

    CONSTS = {"inv2048": (0, 1.0 / 2048.0), "eps": (1, EPS), "mhalf": (2, -0.5),
              "invs2048": (3, 1.0 / math.sqrt(2048.0)), "one": (4, 1.0),
              "invs8192": (5, 1.0 / math.sqrt(8192.0)), "neg1": (6, -1.0),
              "inv4096": (7, 1.0 / 4096.0), "ln05": (8, -0.6931471805599453)}

    def ccol(name, lo=0, hi=128):
        return consts[lo:hi, CONSTS[name][0]:CONSTS[name][0] + 1]

    def proj_gate(t):
        return proj[:, t * 512:(t + 1) * 512]

    def proj_xbc(t, lo=0, span=512):
        base = GATE_COLS + t * 515 + 3
        return proj[:, base + lo:base + lo + span]

    def proj_xbc_halo(t):
        base = GATE_COLS + t * 515
        return proj[:, base:base + 515]

    def proj_dtile():
        base = GATE_COLS + XBC_COLS
        return proj[0:64, base:base + 512]

    def xs_tile(t):
        return xsy[:, t * 512:(t + 1) * 512]

    s_io = b.sem("io")
    s_w = b.sem("w")
    s_d2 = b.sem("d2")
    s_pe = b.sem("pe")
    s_v = b.sem("v")
    s_sc = b.sem("sc")
    s_g = b.sem("g")
    s_out = b.sem("out")

    psf_free = [[] for _ in range(6)]
    psb_free = [[] for _ in range(2)]

    for dst_, src_ in [(convw, d_convw), (convb, d_convb), (lnw, d_lnw), (normw, d_normw),
                       (su, d_su), (dcol, d_dcols), (wsdp, d_wsdp), (abias, d_dtb),
                       (acol, d_alog), (cu, d_wsu_u), (cg, d_wsu_g), (sudp_t, d_sudp)]:
        b.dma("sync", dst_, src_, s_w)
    b.dma("gpsimd", h128b, d_h128, s_io)
    b.dma("gpsimd", h64b, d_h64, s_io)
    b.dma("gpsimd", causal, d_causal, s_io)
    W0 = s_w.n
    IO0 = s_io.n

    b.op("vector", lambda e: e.memset(onesb, 1.0))
    b.op("vector", lambda e: e.memset(onesf, 1.0))
    b.op("vector", lambda e: e.memset(ones64, 1.0))
    for nm, (idx, val) in CONSTS.items():
        b.op("vector", (lambda i_, v_: (lambda e: e.memset(consts[:, i_:i_ + 1], v_)))(idx, val))
    b.op("gpsimd", lambda e: e.memset(idb, 0.0))
    b.op("gpsimd", lambda e: e.affine_select(out=idb, in_=idb, compare_op=ALU.not_equal,
                                             fill=1.0, base=0, pattern=[[-1, 128]], channel_multiplier=1))
    b.op("gpsimd", lambda e: e.memset(idf, 0.0))
    b.op("gpsimd", lambda e: e.affine_select(out=idf, in_=idf, compare_op=ALU.not_equal,
                                             fill=1.0, base=0, pattern=[[-1, 128]], channel_multiplier=1))
    b.op("gpsimd", lambda e: e.memset(halo, 0.0))
    b.op("gpsimd", lambda e: e.memset(carry, 0.0), incs=[(s_g, 1)])
    G_INIT = s_g.n

    b.op("scalar", lambda e: e.activation(out=acol, in_=acol, func=AF.Exp),
         waits=[(s_w, W0)], incs=[(s_sc, 1)])
    SC_A = s_sc.n
    b.op("vector", lambda e: e.tensor_scalar(out=acol, in0=acol, scalar1=-1.0,
                                             scalar2=None, op0=ALU.mult),
         waits=[(s_sc, SC_A), (s_w, W0)])
    b.op("vector", lambda e: e.tensor_tensor(out=cu, in0=cu, in1=sudp_t, op=ALU.mult),
         waits=[(s_w, W0)])
    b.op("vector", lambda e: e.tensor_scalar(out=cu, in0=cu, scalar1=ccol("invs2048"),
                                             scalar2=None, op0=ALU.mult))
    b.op("vector", lambda e: e.tensor_scalar(out=cg, in0=cg, scalar1=ccol("invs2048"),
                                             scalar2=None, op0=ALU.mult))
    b.op("vector", lambda e: e.tensor_scalar(out=wsdp, in0=wsdp, scalar1=ccol("invs8192", 0, 64),
                                             scalar2=None, op0=ALU.mult), incs=[(s_v, 1)])
    V_CONST = s_v.n

    skip_pre = cfg.get("skip_prepass", False)
    if not skip_pre:
        swin_v = s_win.rearrange("(pb kh) kp (ktl p) -> pb kh kp ktl p", kh=2, p=256)
        for i in range(16):
            p1dst = swin_v[:, i // 8, :, i % 8, :].transpose([1, 0, 2])
            srcv2 = d_win[i * 128:(i + 1) * 128, :].rearrange("kp (pb p) -> kp pb p", p=256)
            b.dma("gpsimd", p1dst, srcv2, s_io)

        mk = {}
        for i in range(32):
            par = i % 2
            fin = pp_inf[par][:, 0:2048]
            fout = pp_out[par][:, 0:2048]
            waits = [(s_v, V_CONST)]
            if i >= 2:
                waits.append((s_w, mk[("p2o", i - 2)]))
            b.dma("sync", fin, d_wop[i * 128:(i + 1) * 128, :], s_w, waits=waits)
            W_IN = s_w.n
            b.op("vector", (lambda fin_, fout_, i_: (lambda e: e.tensor_scalar(
                out=fout_, in0=fin_, scalar1=normw[:, i_:i_ + 1], scalar2=None, op0=ALU.mult)))(fin, fout, i),
                waits=[(s_w, W_IN)], incs=[(s_v, 1)])
            swop_v = s_wop.rearrange("(pb kq) kp (ktl p) -> pb kq kp ktl p", kq=4, p=256)
            b.dma("sync", swop_v[:, i // 8, :, i % 8, :].transpose([1, 0, 2]),
                  fout.rearrange("kp (pb p) -> kp pb p", p=256),
                  s_w, waits=[(s_v, s_v.n)])
            mk[("p2o", i)] = s_w.n

        P2_DONE = mk[("p2o", 31)]
        for wi, (dsrc, ddst) in enumerate([(d_wup, s_wup), (d_wgt, s_wgt)]):
            for kt in range(16):
                it = wi * 16 + kt
                par = it % 2
                waits = [(s_w, P2_DONE)]
                if it >= 2:
                    waits.append((s_w, mk[("p3o", it - 2)]))
                iow = s_io.n
                b.dma("gpsimd", pp_in[par][:, 0:4096], dsrc[kt * 128:(kt + 1) * 128, 0:4096], s_io, waits=waits)
                b.dma("gpsimd", pp_in[par][:, 4096:8192], dsrc[kt * 128:(kt + 1) * 128, 4096:8192], s_io, waits=waits)
                IOW = s_io.n
                for j in range(16):
                    bk = j % 2
                    b.op("tensor", (lambda par_, j_, bk_: (lambda e: e.matmul(
                        psF[bk_], h128b, pp_in[par_][:, j_ * 512:(j_ + 1) * 512],
                        start=True, stop=True)))(par, j, bk),
                        waits=[(s_io, IOW), (s_io, IO0)] + psf_free[bk], incs=[(s_pe, 1)])
                    b.op("vector", (lambda par_, j_, bk_: (lambda e: e.tensor_copy(
                        out=pp_out[par_][:, j_ * 512:(j_ + 1) * 512], in_=psF[bk_])))(par, j, bk),
                        waits=[(s_pe, s_pe.n)], incs=[(s_v, 1)])
                    psf_free[bk] = [(s_v, s_v.n)]
                ddst_v = ddst.rearrange("(pb kh) kp (ktl p) -> pb kh kp ktl p", kh=2, p=256)
                b.dma("sync", ddst_v[:, kt // 8, :, kt % 8, :].transpose([1, 0, 2]),
                      pp_out[par].rearrange("kp (pb p) -> kp pb p", p=256),
                      s_w, waits=[(s_v, s_v.n)])
                mk[("p3o", it)] = s_w.n

        dwdn_v = d_wdn.rearrange("(i jr) r -> i (jr r)", i=64)
        swdn1_v = s_wdn1.rearrange("(i jr) r -> i (jr r)", i=64)
        P3_DONE = mk[("p3o", 31)]
        for c in range(32):
            par = c % 2
            fin = pp_in[par][0:64, 0:8192]
            fout = pp_out[par][0:64, 0:8192]
            waits = [(s_w, P3_DONE)]
            if c >= 2:
                waits.append((s_w, mk[("p4o", c - 2)]))
            b.dma("gpsimd", fin, dwdn_v[:, c * 8192:(c + 1) * 8192], s_io, waits=waits)
            IOW = s_io.n
            for j in range(16):
                bk = j % 2
                rstart = (c * 8192 + j * 512) % 2048
                b.op("tensor", (lambda fin_, j_, bk_: (lambda e: e.matmul(
                    psF[bk_][0:64, :], h64b, fin_[:, j_ * 512:(j_ + 1) * 512],
                    start=True, stop=True)))(fin, j, bk),
                    waits=[(s_io, IOW), (s_v, V_CONST)] + psf_free[bk], incs=[(s_pe, 1)])
                b.op("vector", (lambda fout_, j_, rs_, bk_: (lambda e: e.tensor_tensor(
                    out=fout_[:, j_ * 512:(j_ + 1) * 512], in0=psF[bk_][0:64, :],
                    in1=wsdp[:, rs_:rs_ + 512], op=ALU.mult)))(fout, j, rstart, bk),
                    waits=[(s_pe, s_pe.n)], incs=[(s_v, 1)])
                psf_free[bk] = [(s_v, s_v.n)]
            b.dma("sync", swdn1_v[:, c * 8192:(c + 1) * 8192], fout, s_w, waits=[(s_v, s_v.n)])
            mk[("p4o", c)] = s_w.n

        for kt in range(64):
            par = kt % 2
            fin = pp_in[par][:, 0:2048]
            fout = pp_out[par][:, 0:2048]
            waits = [(s_w, mk[("p4o", 31)])]
            if kt >= 2:
                waits.append((s_w, mk[("p5o", kt - 2)]))
            b.dma("sync", fin, s_wdn1[kt * 128:(kt + 1) * 128, :], s_w, waits=waits)
            W_IN = s_w.n
            for j in range(4):
                bk = j % 2
                b.op("tensor", (lambda fin_, j_, bk_: (lambda e: e.matmul(
                    psF[bk_], h128b, fin_[:, j_ * 512:(j_ + 1) * 512],
                    start=True, stop=True)))(fin, j, bk),
                    waits=[(s_w, W_IN)] + psf_free[bk], incs=[(s_pe, 1)])
                b.op("vector", (lambda fout_, j_, bk_: (lambda e: e.tensor_copy(
                    out=fout_[:, j_ * 512:(j_ + 1) * 512], in_=psF[bk_])))(fout, j, bk),
                    waits=[(s_pe, s_pe.n)], incs=[(s_v, 1)])
                psf_free[bk] = [(s_v, s_v.n)]
            swdn_v = s_wdn.rearrange("(pb kq) kp (ktl p) -> pb kq kp ktl p", kq=8, p=256)
            b.dma("sync", swdn_v[:, kt // 8, :, kt % 8, :].transpose([1, 0, 2]),
                  fout.rearrange("kp (pb p) -> kp pb p", p=256),
                  s_w, waits=[(s_v, s_v.n)])
            mk[("p5o", kt)] = s_w.n

    PRE_DONE = [(s_w, s_w.n), (s_v, s_v.n), (s_io, s_io.n), (s_g, G_INIT), (s_sc, s_sc.n)]

    tb_par = [0]

    def tpose_rounds(blocks, first_waits, evac_emit, group=4):
        nrounds = (len(blocks) + group - 1) // group
        for r in range(nrounds):
            bank = tb_par[0] % 2
            tb_par[0] += 1
            chunk = blocks[r * group:(r + 1) * group]
            for si, (in_ap, key) in enumerate(chunk):
                b.op("tensor", (lambda a_, bk_, s_: (lambda e: e.transpose(
                    psB[bk_][:, s_ * 128:(s_ + 1) * 128], a_, idb)))(in_ap, bank, si),
                    waits=list(first_waits) + psb_free[bank], incs=[(s_pe, 1)])
            PE_M = s_pe.n
            last = None
            for si, (in_ap, key) in enumerate(chunk):
                last = evac_emit(key, psB[bank][:, si * 128:(si + 1) * 128], (s_pe, PE_M))
            psb_free[bank] = [last]

    M_out_exit = 0
    M_xtm_free = [(s_v, 0)]
    M_proj_free = list(PRE_DONE)
    M_a5_free = list(PRE_DONE)
    M_xsy_free = [(s_g, G_INIT)]
    M_carry = [(s_g, G_INIT)]
    M_exitst = [(s_out, 0)]

    STOP = cfg.get("stop_after")

    def stop_at(name, _done=[False]):
        order = ["entry", "in_proj", "conv", "dt", "ssd", "op", "ln1", "f1", "f2", "ffn3", "ffn", None]
        if STOP is None:
            return False
        return order.index(STOP) < order.index(name)

    for g in range(NGRP):
        seq = g // (NGRP_FULL // SEQ)
        tok0 = (g % (NGRP_FULL // SEQ)) * G
        first_of_seq = (tok0 == 0)

        src = d_x[seq, tok0:tok0 + G, :].rearrange("(a p) h -> p a h", p=128)
        b.dma("gpsimd", x_tm.rearrange("p (a h) -> p a h", h=HD), src, s_io,
              waits=list(M_xtm_free))
        IO_X = s_io.n

        for a in range(4):
            b.op("scalar", (lambda a_: (lambda e: e.activation(
                out=exitst, in_=x_tm[:, a_ * HD:(a_ + 1) * HD], func=AF.Square,
                accum_out=cols[:, a_:a_ + 1])))(a),
                waits=[(s_io, IO_X)] + list(M_exitst), incs=[(s_sc, 1)])
        b.op("scalar", lambda e: e.activation(out=cols[:, 4:8], in_=cols[:, 0:4], func=AF.Ln,
                                              scale=ccol("inv2048"), bias=ccol("eps")),
             waits=[(s_sc, s_sc.n)])
        b.op("scalar", lambda e: e.activation(out=cols[:, 4:8], in_=cols[:, 4:8], func=AF.Exp,
                                              scale=ccol("mhalf")), incs=[(s_sc, 1)])
        SC_RSTD1 = s_sc.n
        M_exitst = [(s_sc, SC_RSTD1)]

        for a in range(4):
            b.op("vector", (lambda a_: (lambda e: e.tensor_scalar(
                out=xn_stage, in0=x_tm[:, a_ * HD:(a_ + 1) * HD],
                scalar1=cols[:, 4 + a_:5 + a_], scalar2=None, op0=ALU.mult)))(a),
                waits=[(s_sc, SC_RSTD1)] + psb_free[0] + psb_free[1] + list(M_exitst),
                incs=[(s_v, 1)])
            V_STAGE = s_v.n
            M_exitst = [(s_v, V_STAGE)]

            def evac_xn(key, slot_ap, pe_mark, a_=a):
                i_ = key
                b.op("vector", (lambda i2, a2, sl: (lambda e: e.tensor_scalar(
                    out=a5[:, i2 * 512 + a2 * 128: i2 * 512 + (a2 + 1) * 128],
                    in0=sl, scalar1=lnw[:, i2:i2 + 1], scalar2=None, op0=ALU.mult)))(i_, a_, slot_ap),
                    waits=[pe_mark] + list(M_a5_free), incs=[(s_v, 1)])
                return (s_v, s_v.n)

            blocks = [(xn_stage[:, i * 128:(i + 1) * 128], i) for i in range(16)]
            tpose_rounds(blocks, [(s_v, V_STAGE)], evac_xn, group=8)
            M_exitst = [(s_pe, s_pe.n)]
        V_XN_DONE = s_v.n
        M_a5_free = []

        if stop_at("in_proj"):
            continue
        ip_ev = {}
        ip_strip = {}
        for pb in range(33):
            for kh in range(2):
                gstrip = pb * 2 + kh
                par = gstrip % 4
                waits = list(M_proj_free) + [(s_v, V_XN_DONE)]
                if gstrip >= 4:
                    waits.append(ip_strip[gstrip - 4])
                b.dma("sync", wbuf4[par], s_win[gstrip], s_w, waits=waits)
                W_IP = s_w.n
                for j in range(2):
                    bankF = 2 + (pb % 2) * 2 + j
                    for k8 in range(8):
                        kt = kh * 8 + k8
                        mmw = [(s_w, W_IP)]
                        if kt == 0:
                            mmw += psf_free[bankF]
                        b.op("tensor", (lambda par_, j_, k8_, kt_, bankF_: (lambda e: e.matmul(
                            psF[bankF_], wbuf4[par_][:, k8_ * 256 + j_ * 128: k8_ * 256 + j_ * 128 + 128],
                            a5[:, kt_ * 512:(kt_ + 1) * 512],
                            start=(kt_ == 0), stop=(kt_ == 15))))(par, j, k8, kt, bankF),
                            waits=mmw, incs=([(s_pe, 1)] if (k8 == 7 and j == 1) else []))
                ip_strip[gstrip] = (s_pe, s_pe.n)
            PE_MM = s_pe.n
            for j in range(2):
                tI = pb * 2 + j
                bankF = 2 + (pb % 2) * 2 + j
                if tI < NT_GATE:
                    dstap = proj_gate(tI)
                elif tI < NT_GATE + NT_XBC:
                    dstap = proj_xbc(tI - NT_GATE)
                else:
                    dstap = proj[:, GATE_COLS + XBC_COLS: GATE_COLS + XBC_COLS + 512]
                if tI % 2 == 0:
                    b.op("vector", (lambda d_, bf_: (lambda e: e.tensor_copy(out=d_, in_=psF[bf_])))(dstap, bankF),
                         waits=[(s_pe, PE_MM)], incs=[(s_v, 1)])
                    psf_free[bankF] = [(s_v, s_v.n)]
                    ip_ev[tI] = (s_v, s_v.n)
                else:
                    b.op("scalar", (lambda d_, bf_: (lambda e: e.copy(out=d_, in_=psF[bf_])))(dstap, bankF),
                         waits=[(s_pe, PE_MM)], incs=[(s_sc, 1)])
                    psf_free[bankF] = [(s_sc, s_sc.n)]
                    ip_ev[tI] = (s_sc, s_sc.n)
        IP_ALL = [(s_v, s_v.n), (s_sc, s_sc.n)]
        IP_PE_END = (s_pe, s_pe.n)

        if stop_at("conv"):
            continue
        if first_of_seq and g > 0:
            b.op("gpsimd", lambda e: e.memset(halo, 0.0), waits=list(M_xsy_free), incs=[(s_g, 1)])
        HALO_RDY = (s_g, s_g.n)
        cv_marks = {}
        for t in range(NT_XBC):
            eng = "vector"
            sm = s_v
            w0 = GATE_COLS + t * 515
            b.op(eng, (lambda t_, w0_: (lambda e: e.tensor_copy(
                out=proj[:, w0_:w0_ + 3], in_=halo[:, t_ * 3:t_ * 3 + 3])))(t, w0),
                waits=[ip_ev[NT_GATE + t], HALO_RDY], incs=[(sm, 1)])
            b.op(eng, (lambda t_: (lambda e: e.tensor_copy(
                out=halo[:, t_ * 3:t_ * 3 + 3], in_=proj_xbc(t_, 509, 3))))(t),
                incs=[(sm, 1)])
            b.op(eng, (lambda t_: (lambda e: e.tensor_scalar(
                out=xs_tile(t_), in0=proj_xbc_halo(t_)[:, 0:512],
                scalar1=convw[:, t_ * KC:t_ * KC + 1], scalar2=None, op0=ALU.mult)))(t),
                waits=list(M_xsy_free) + [IP_PE_END], incs=[(sm, 1)])
            for k in range(1, KC):
                b.op(eng, (lambda t_, k_: (lambda e: e.scalar_tensor_tensor(
                    out=xs_tile(t_), in0=proj_xbc_halo(t_)[:, k_:k_ + 512],
                    scalar=convw[:, t_ * KC + k_:t_ * KC + k_ + 1], in1=xs_tile(t_),
                    op0=ALU.mult, op1=ALU.add)))(t, k),
                    incs=[(sm, 1)])
            b.op("scalar", (lambda t_: (lambda e: e.activation(
                out=xs_tile(t_), in_=xs_tile(t_), func=AF.Silu, bias=convb[:, t_:t_ + 1])))(t),
                waits=[(sm, sm.n)], incs=[(s_sc, 1)])
            cv_marks[t] = (s_sc, s_sc.n)
        CV_ALL = (s_sc, s_sc.n)
        b.dma("scalar", cc_buf, xs_tile(32)[64:128, :], s_d2, waits=[CV_ALL])
        D2_CC = s_d2.n

        if stop_at("dt"):
            continue
        b.op("vector", lambda e: e.tensor_scalar(out=dt_sp, in0=proj_dtile(),
                                                 scalar1=abias, scalar2=None, op0=ALU.add),
             waits=[ip_ev[65]] + IP_ALL, incs=[(s_v, 1)])
        V_DT0 = s_v.n
        b.op("scalar", lambda e: e.activation(out=dA_t, in_=proj_dtile(), func=AF.Abs,
                                              bias=abias),
             waits=[ip_ev[65]] + IP_ALL, incs=[(s_sc, 1)])
        SC_ABS = s_sc.n
        b.op("scalar", lambda e: e.activation(out=dA_t, in_=dA_t, func=AF.Exp,
                                              scale=ccol("neg1", 0, 64)))
        b.op("scalar", lambda e: e.activation(out=dA_t, in_=dA_t, func=AF.Ln,
                                              bias=ccol("one", 0, 64)), incs=[(s_sc, 1)])
        SC_DT = s_sc.n
        b.op("vector", lambda e: e.tensor_scalar(out=dt_sp, in0=dt_sp, scalar1=0.0,
                                                 scalar2=None, op0=ALU.max))
        b.op("vector", lambda e: e.tensor_tensor(out=dt_sp, in0=dt_sp, in1=dA_t, op=ALU.add),
             waits=[(s_sc, SC_DT)])
        b.op("vector", lambda e: e.tensor_copy(out=dt_spb, in_=dt_sp))
        b.op("vector", lambda e: e.tensor_scalar(out=dA_t, in0=dt_sp, scalar1=acol,
                                                 scalar2=None, op0=ALU.mult), incs=[(s_v, 1)])
        for c in range(NCHUNK):
            b.op("vector", (lambda c_: (lambda e: e.tensor_tensor_scan(
                out=acum[:, c_ * CH:(c_ + 1) * CH], data0=ones64,
                data1=dA_t[:, c_ * CH:(c_ + 1) * CH], initial=0.0,
                op0=ALU.mult, op1=ALU.add)))(c), incs=[(s_v, 1)])
        V_ACUM = s_v.n

        if stop_at("ssd"):
            continue
        prev_repl_reads = None
        gate_done = []
        for c in range(NCHUNK):
            csl = slice(c * CH, (c + 1) * CH)
            fw = [(s_v, V_ACUM)]
            if prev_repl_reads is not None:
                fw += prev_repl_reads
            b.dma("scalar", s_acum, acum[:, csl], s_d2, waits=fw)
            D2_FLAT = s_d2.n

            b.op("vector", (lambda c_: (lambda e: e.tensor_scalar(
                out=dssub, in0=acum[:, c_ * CH:(c_ + 1) * CH],
                scalar1=acum[:, c_ * CH + CH - 1: c_ * CH + CH],
                scalar2=None, op0=ALU.subtract)))(c), waits=[(s_v, V_ACUM)], incs=[(s_v, 1)])
            b.op("scalar", lambda e: e.activation(out=dshm, in_=dssub, func=AF.Exp,
                                                  scale=ccol("neg1", 0, 64)),
                 waits=[(s_v, s_v.n)], incs=[(s_sc, 1)])
            SC_DS = s_sc.n

            bank = tb_par[0] % 2
            tb_par[0] += 1
            tw = psb_free[bank] + [(s_v, s_v.n), (s_sc, SC_DS)]
            b.op("tensor", (lambda c_, bk_: (lambda e: e.transpose(
                psB[bk_][:, 0:64], dt_spb[:, c_ * CH:(c_ + 1) * CH], idb[0:64, 0:64])))(c, bank),
                waits=tw, incs=[(s_pe, 1)])
            b.op("tensor", (lambda bk_: (lambda e: e.transpose(
                psB[bk_][:, 64:128], dshm, idb[0:64, 0:64])))(bank), incs=[(s_pe, 1)])
            b.op("tensor", (lambda c_, bk_: (lambda e: e.transpose(
                psB[bk_][:, 128:192], xs_tile(32)[0:64, c_ * CH:(c_ + 1) * CH], idb[0:64, 0:64])))(c, bank),
                waits=[CV_ALL], incs=[(s_pe, 1)])
            b.op("tensor", (lambda c_: (lambda e: e.transpose(
                psF[0][:, 0:64], acum[:, c_ * CH:(c_ + 1) * CH], idf[0:64, 0:64])))(c),
                waits=psf_free[0], incs=[(s_pe, 1)])
            b.op("tensor", (lambda c_: (lambda e: e.transpose(
                psF[0][0:1, 64:128], acum[:, c_ * CH + CH - 1:c_ * CH + CH], idf[0:64, 0:64])))(c),
                incs=[(s_pe, 1)])
            PE_TR = s_pe.n
            b.op("vector", (lambda bk_: (lambda e: e.tensor_copy(out=dtt, in_=psB[bk_][:, 0:64])))(bank),
                 waits=[(s_pe, PE_TR)], incs=[(s_v, 1)])
            b.op("vector", (lambda bk_: (lambda e: e.tensor_copy(out=dst, in_=psB[bk_][:, 64:128])))(bank),
                 incs=[(s_v, 1)])
            b.op("vector", (lambda bk_: (lambda e: e.tensor_copy(out=bct, in_=psB[bk_][:, 128:192])))(bank),
                 incs=[(s_v, 1)])
            b.op("vector", lambda e: e.tensor_copy(out=act_t, in_=psF[0][:, 0:64]), incs=[(s_v, 1)])
            b.op("vector", lambda e: e.tensor_copy(out=rowst[0:1, 0:64], in_=psF[0][0:1, 64:128]),
                 incs=[(s_v, 1)])
            V_TR = s_v.n
            psb_free[bank] = [(s_v, V_TR)]
            psf_free[0] = [(s_v, V_TR)]
            b.op("tensor", lambda e: e.matmul(psF[1][0:64, 0:64], onesf[:, 0:64], rowst[0:1, 0:64],
                                              start=True, stop=True),
                 waits=[(s_v, V_TR)] + psf_free[1], incs=[(s_pe, 1)])
            b.op("scalar", lambda e: e.activation(out=cdecr, in_=psF[1][0:64, 0:64], func=AF.Exp),
                 waits=[(s_pe, s_pe.n)], incs=[(s_sc, 1)])
            SC_CDECR = s_sc.n
            b.op("tensor", (lambda c_: (lambda e: e.matmul(
                psF[1][:, 128:256], xs_tile(32)[0:64, c_ * CH:(c_ + 1) * CH],
                cc_buf[:, c_ * CH:(c_ + 1) * CH], start=True, stop=True)))(c),
                waits=[(s_d2, D2_CC), CV_ALL, (s_sc, SC_CDECR)], incs=[(s_pe, 1)])
            PE_CB = s_pe.n
            b.op("vector", lambda e: e.tensor_tensor(out=cbm, in0=psF[1][:, 128:256],
                                                     in1=causal, op=ALU.mult),
                 waits=[(s_pe, PE_CB)], incs=[(s_v, 1)])
            V_CBM = s_v.n
            psf_free[1] = [(s_v, V_CBM)]

            xct_ev = []
            for half in range(4):
                bank = tb_par[0] % 2
                tb_par[0] += 1
                for q in range(8):
                    t = half * 8 + q
                    b.op("tensor", (lambda t_, c_, bk_, q_: (lambda e: e.transpose(
                        psB[bk_][:, q_ * 128:(q_ + 1) * 128],
                        xs_tile(t_)[:, c_ * CH:(c_ + 1) * CH], idb)))(t, c, bank, q),
                        waits=[CV_ALL] + psb_free[bank], incs=[(s_pe, 1)])
                PE_T8 = s_pe.n
                b.op("vector", (lambda half_, bk_: (lambda e: e.tensor_tensor(
                    out=xct.rearrange("p (h f) -> p h f", f=64)[:, half_ * 16:(half_ + 1) * 16, :],
                    in0=psB[bk_].rearrange("p (h f) -> p h f", f=64),
                    in1=dtt[:, half_ * 16:(half_ + 1) * 16].unsqueeze(2).broadcast_to([128, 16, 64]),
                    op=ALU.mult)))(half, bank),
                    waits=[(s_pe, PE_T8), (s_v, V_TR)], incs=[(s_v, 1)])
                psb_free[bank] = [(s_v, s_v.n)]
                xct_ev.append((s_v, s_v.n))
            V_XCT = s_v.n

            psY = [psF[2], psF[3]]
            psS = [psF[4], psF[5]]
            repl_reads = []
            oct_marks = []
            for o in range(8):
                par = o % 2
                rw = [(s_d2, D2_FLAT)]
                if len(oct_marks) >= 2:
                    rw += oct_marks[-2]["repl_reads"]
                b.dma("scalar", replo[par],
                      s_acum.rearrange("h s -> (h s)").rearrange("(o2 f) -> o2 f", o2=1)[:, o * 1024:(o + 1) * 1024].broadcast_to([128, 1024]),
                      s_d2, waits=rw)
                D2_R = s_d2.n
                msk_w = [(s_d2, D2_R), (s_v, V_TR)]
                if len(oct_marks) >= 2:
                    msk_w.append(oct_marks[-2]["y_pe"])
                b.op("vector", (lambda par_, o_: (lambda e: e.tensor_tensor(
                    out=scoreso[par_].rearrange("p (h l) -> p h l", l=128),
                    in0=replo[par_].rearrange("p (h l) -> p h l", l=128),
                    in1=act_t[:, o_ * 8:(o_ + 1) * 8].unsqueeze(2).broadcast_to([128, 8, 128]),
                    op=ALU.subtract)))(par, o), waits=msk_w, incs=[(s_v, 1)])
                V_SUB = s_v.n
                b.op("vector", (lambda par_: (lambda e: e.tensor_scalar(
                    out=scoreso[par_], in0=scoreso[par_], scalar1=0.0, scalar2=None,
                    op0=ALU.min)))(par), incs=[(s_v, 1)])
                V_MS = s_v.n
                b.op("scalar", (lambda par_: (lambda e: e.activation(
                    out=scoreso[par_], in_=scoreso[par_], func=AF.Exp)))(par),
                    waits=[(s_v, V_MS)], incs=[(s_sc, 1)])
                SC_EXP = s_sc.n
                b.op("vector", (lambda par_: (lambda e: e.tensor_tensor(
                    out=scoreso[par_].rearrange("p (h l) -> p h l", l=128),
                    in0=scoreso[par_].rearrange("p (h l) -> p h l", l=128),
                    in1=cbm.rearrange("p (h2 l) -> p h2 l", h2=1).broadcast_to([128, 8, 128]),
                    op=ALU.mult)))(par),
                    waits=[(s_sc, SC_EXP), (s_v, V_CBM)], incs=[(s_v, 1)])
                V_SCORE = s_v.n
                cd_w = [(s_d2, D2_R)]
                if len(oct_marks) >= 2:
                    cd_w.append(oct_marks[-2]["y_pe"])
                b.op("scalar", (lambda par_: (lambda e: e.activation(
                    out=cdeco[par_], in_=replo[par_][0:64, :], func=AF.Exp)))(par),
                    waits=cd_w, incs=[(s_sc, 1)])
                SC_CD = s_sc.n
                b.op("gpsimd", (lambda par_, c_: (lambda e: e.tensor_tensor(
                    out=cdeco[par_].rearrange("p (h l) -> p h l", l=128),
                    in0=cdeco[par_].rearrange("p (h l) -> p h l", l=128),
                    in1=cc_buf[:, c_ * CH:(c_ + 1) * CH].rearrange("p (h2 l) -> p h2 l", h2=1).broadcast_to([64, 8, 128]),
                    op=ALU.mult)))(par, c),
                    waits=[(s_sc, SC_CD), (s_d2, D2_CC)], incs=[(s_g, 1)])
                G_CD = s_g.n
                bd_w = [(s_v, V_TR)]
                if len(oct_marks) >= 2:
                    bd_w.append(oct_marks[-2]["st_pe"])
                b.op("gpsimd", (lambda par_, o_: (lambda e: e.tensor_tensor(
                    out=bdeco[par_].rearrange("p (h n) -> p h n", n=64),
                    in0=bct.rearrange("p (h2 n) -> p h2 n", h2=1).broadcast_to([128, 8, 64]),
                    in1=dst[:, o_ * 8:(o_ + 1) * 8].unsqueeze(2).broadcast_to([128, 8, 64]),
                    op=ALU.mult)))(par, o), waits=bd_w, incs=[(s_g, 1)])
                G_BD = s_g.n
                cb_w = list(M_carry)
                if len(oct_marks) >= 2:
                    cb_w.append(oct_marks[-2]["y_pe"])
                b.op("vector", (lambda par_, o_: (lambda e: e.tensor_copy(
                    out=carryb[par_], in_=carry[:, o_ * 512:(o_ + 1) * 512])))(par, o),
                    waits=cb_w, incs=[(s_v, 1)])
                V_CBo = s_v.n

                ymm_w = [(s_v, V_SCORE), (s_v, V_XCT), (s_v, V_CBo), (s_g, G_CD)]
                if len(gate_done) >= 2:
                    ymm_w.append(gate_done[-2])
                for j2 in range(4):
                    h = 2 * j2
                    habs = o * 8 + h
                    b.op("tensor", (lambda par_, j2_, habs_, h_, f_: (lambda e: e.matmul(
                        psY[par_][0:64, j2_ * 128:(j2_ + 1) * 128],
                        xct[:, habs_ * 64:(habs_ + 1) * 64],
                        scoreso[par_][:, h_ * 128:(h_ + 1) * 128],
                        start=f_, stop=False, skip_group_check=True)))(par, j2, habs, h, j2 == 0),
                        waits=ymm_w, incs=[(s_pe, 1)])
                    b.op("tensor", (lambda par_, j2_, h_: (lambda e: e.matmul(
                        psY[par_][0:64, j2_ * 128:(j2_ + 1) * 128],
                        carryb[par_][:, h_ * 64:(h_ + 1) * 64],
                        cdeco[par_][:, h_ * 128:(h_ + 1) * 128],
                        start=False, stop=False, skip_group_check=True)))(par, j2, h),
                        incs=[(s_pe, 1)])
                    b.op("tensor", (lambda par_, j2_, habs_, h_, f_: (lambda e: e.matmul(
                        psY[par_][64:128, j2_ * 128:(j2_ + 1) * 128],
                        xct[:, (habs_ + 1) * 64:(habs_ + 2) * 64],
                        scoreso[par_][:, (h_ + 1) * 128:(h_ + 2) * 128],
                        start=f_, stop=False, skip_group_check=True,
                        tile_position=(0, 64))))(par, j2, habs, h, j2 == 0),
                        incs=[(s_pe, 1)])
                    b.op("tensor", (lambda par_, j2_, h_, l_: (lambda e: e.matmul(
                        psY[par_][64:128, j2_ * 128:(j2_ + 1) * 128],
                        carryb[par_][:, (h_ + 1) * 64:(h_ + 2) * 64],
                        cdeco[par_][:, (h_ + 1) * 128:(h_ + 2) * 128],
                        start=False, stop=l_, skip_group_check=True,
                        tile_position=(0, 64))))(par, j2, h, j2 == 3),
                        incs=[(s_pe, 1)])
                PE_Y = s_pe.n
                st_w = [(s_g, G_BD), (s_v, V_XCT)]
                if len(oct_marks) >= 2:
                    st_w.append(oct_marks[-2]["st_rd"])
                for hh in range(8):
                    habs = o * 8 + hh
                    b.op("tensor", (lambda par_, hh_, habs_: (lambda e: e.matmul(
                        psS[par_][0:64, hh_ * 64:(hh_ + 1) * 64],
                        bdeco[par_][:, hh_ * 64:(hh_ + 1) * 64],
                        xct[:, habs_ * 64:(habs_ + 1) * 64],
                        start=(hh_ == 0), stop=(hh_ == 7), skip_group_check=True)))(par, hh, habs),
                        waits=st_w, incs=([(s_pe, 1)] if hh == 7 else []))
                PE_S = s_pe.n

                sg_w = [(s_pe, PE_Y), CV_ALL] + IP_ALL
                if len(gate_done) >= 2:
                    sg_w.append(gate_done[-2])
                b.op("scalar", (lambda o_, c_, par_: (lambda e: e.activation(
                    out=sg[par_].rearrange("p (t l) -> p t l", l=128),
                    in_=proj[:, 0:GATE_COLS].rearrange("p (t f) -> p t f", f=512)[:, o_ * 4:(o_ + 1) * 4, c_ * CH:(c_ + 1) * CH],
                    func=AF.Silu)))(o, c, par), waits=sg_w, incs=[(s_sc, 1)])
                SC_SG = s_sc.n
                b.op("vector", (lambda o_, c_: (lambda e: e.tensor_tensor(
                    out=tmpg.rearrange("p (t l) -> p t l", l=128),
                    in0=xsy.rearrange("p (t f) -> p t f", f=512)[:, o_ * 4:(o_ + 1) * 4, c_ * CH:(c_ + 1) * CH],
                    in1=dcol[:, o_ * 4:(o_ + 1) * 4].unsqueeze(2).broadcast_to([128, 4, 128]),
                    op=ALU.mult)))(o, c),
                    waits=[(s_v, V_XCT)], incs=[(s_v, 1)])
                b.op("vector", (lambda par_: (lambda e: e.tensor_tensor(
                    out=tmpg, in0=tmpg, in1=psY[par_], op=ALU.add)))(par),
                    waits=[(s_pe, PE_Y)], incs=[(s_v, 1)])
                b.op("vector", (lambda o_, c_, par_: (lambda e: e.tensor_tensor(
                    out=xsy.rearrange("p (t f) -> p t f", f=512)[:, o_ * 4:(o_ + 1) * 4, c_ * CH:(c_ + 1) * CH],
                    in0=tmpg.rearrange("p (t l) -> p t l", l=128),
                    in1=sg[par_].rearrange("p (t l) -> p t l", l=128),
                    op=ALU.mult)))(o, c, par),
                    waits=[(s_sc, SC_SG)], incs=[(s_v, 1)])
                gate_done.append((s_v, s_v.n))

                b.op("gpsimd", (lambda o_: (lambda e: e.tensor_tensor(
                    out=carry.rearrange("p (h f) -> p h f", f=64)[:, o_ * 8:(o_ + 1) * 8, :],
                    in0=carry.rearrange("p (h f) -> p h f", f=64)[:, o_ * 8:(o_ + 1) * 8, :],
                    in1=cdecr[:, o_ * 8:(o_ + 1) * 8].unsqueeze(2).broadcast_to([64, 8, 64]),
                    op=ALU.mult)))(o),
                    waits=[(s_v, V_CBo), (s_sc, SC_CDECR)], incs=[(s_g, 1)])
                G_MUL = s_g.n
                b.op("vector", (lambda o_, par_: (lambda e: e.tensor_tensor(
                    out=carry[:, o_ * 512:(o_ + 1) * 512], in0=carry[:, o_ * 512:(o_ + 1) * 512],
                    in1=psS[par_][0:64, :], op=ALU.add)))(o, par),
                    waits=[(s_g, G_MUL), (s_pe, PE_S)], incs=[(s_v, 1)])
                V_ST = s_v.n
                oct_marks.append({"repl_reads": [(s_v, V_SUB), (s_sc, SC_CD)],
                                  "y_pe": (s_pe, PE_Y), "st_pe": (s_pe, PE_S),
                                  "st_rd": (s_v, V_ST)})
            M_carry = [(s_v, s_v.n)]
            psf_free[2] = [gate_done[-2]]
            psf_free[3] = [gate_done[-1]]
            psf_free[4] = [oct_marks[-2]["st_rd"]]
            psf_free[5] = [oct_marks[-1]["st_rd"]]
            prev_repl_reads = oct_marks[-1]["repl_reads"] + oct_marks[-2]["repl_reads"]
        GATE_ALL = [(s_v, s_v.n)]

        if stop_at("op"):
            continue
        for t in range(NT_GATE):
            b.op("scalar", (lambda t_: (lambda e: e.activation(
                out=y2win[:, t_ * 512:(t_ + 1) * 512], in_=xs_tile(t_), func=AF.Square)))(t),
                waits=GATE_ALL + list(M_carry), incs=[(s_sc, 1)])
        SC_Y2 = s_sc.n
        for t in range(NT_GATE):
            mmw = [(s_sc, SC_Y2)]
            if t == 0:
                mmw += psf_free[0]
            b.op("tensor", (lambda t_: (lambda e: e.matmul(
                psF[0][0:1, :], onesb, y2win[:, t_ * 512:(t_ + 1) * 512],
                start=(t_ == 0), stop=(t_ == NT_GATE - 1))))(t),
                waits=mmw, incs=([(s_pe, 1)] if t == NT_GATE - 1 else []))
        PE_YS = s_pe.n
        b.op("scalar", lambda e: e.activation(out=rowst[0:1, :], in_=psF[0][0:1, :],
                                              func=AF.Ln, scale=ccol("inv4096", 0, 1), bias=ccol("eps", 0, 1)),
             waits=[(s_pe, PE_YS)])
        b.op("scalar", lambda e: e.activation(out=rowst[0:1, :], in_=rowst[0:1, :], func=AF.Exp,
                                              scale=ccol("mhalf", 0, 1), bias=ccol("ln05", 0, 1)),
             incs=[(s_sc, 1)])
        SC_RY = s_sc.n
        psf_free[0] = [(s_sc, SC_RY)]
        for i4 in range(4):
            b.op("tensor", (lambda i_: (lambda e: e.transpose(
                psF[0][:, i_:i_ + 1], rowst[0:1, i_ * 128:(i_ + 1) * 128], idf[0:1, 0:1])))(i4),
                waits=[(s_sc, SC_RY)] + (psf_free[0] if i4 == 0 else []), incs=[(s_pe, 1)])
        b.op("vector", lambda e: e.tensor_copy(out=cols[:, 8:12], in_=psF[0][:, 0:4]),
             waits=[(s_pe, s_pe.n)], incs=[(s_v, 1)])
        psf_free[0] = [(s_v, s_v.n)]
        RY_COLS = (s_v, s_v.n)

        op_mk = {}
        for pb in range(8):
            for kq in range(4):
                gstrip = pb * 4 + kq
                par = gstrip % 2
                waits = [(s_sc, SC_Y2)]
                if gstrip >= 2:
                    waits.append(op_mk[gstrip - 2])
                b.dma("sync", wbuf[par], s_wop[gstrip], s_w, waits=waits)
                W_OP = s_w.n
                for j in range(2):
                    bankF = 2 + (pb % 2) * 2 + j
                    for k8 in range(8):
                        kt = kq * 8 + k8
                        mmw = [(s_w, W_OP)]
                        if kt == 0:
                            mmw += psf_free[bankF]
                        b.op("tensor", (lambda par_, j_, k8_, kt_, bankF_: (lambda e: e.matmul(
                            psF[bankF_], wbuf[par_][:, k8_ * 256 + j_ * 128: k8_ * 256 + j_ * 128 + 128],
                            xs_tile(kt_), start=(kt_ == 0), stop=(kt_ == 31))))(par, j, k8, kt, bankF),
                            waits=mmw, incs=([(s_pe, 1)] if (k8 == 7 and j == 1) else []))
                op_mk[gstrip] = (s_pe, s_pe.n)
            PE_OP2 = s_pe.n
            for j in range(2):
                tI = pb * 2 + j
                bankF = 2 + (pb % 2) * 2 + j
                stg = a6[:, j * 512:(j + 1) * 512]
                b.op("vector", (lambda stg_, bf_: (lambda e: e.tensor_copy(out=stg_, in_=psF[bf_])))(stg, bankF),
                     waits=[(s_pe, PE_OP2)], incs=[(s_v, 1)])
                V_STG = s_v.n
                psf_free[bankF] = [(s_v, V_STG)]

                def evac_z1(key, slot_ap, pe_mark, tI_=tI):
                    a_ = key
                    b.op("vector", (lambda a2, t2, sl: (lambda e: e.scalar_tensor_tensor(
                        out=x_tm[:, a2 * HD + t2 * 128: a2 * HD + (t2 + 1) * 128],
                        in0=sl, scalar=cols[:, 8 + a2:9 + a2],
                        in1=x_tm[:, a2 * HD + t2 * 128: a2 * HD + (t2 + 1) * 128],
                        op0=ALU.mult, op1=ALU.add)))(a_, tI_, slot_ap),
                        waits=[pe_mark, RY_COLS], incs=[(s_v, 1)])
                    return (s_v, s_v.n)

                blocks = [(stg[:, a * 128:(a + 1) * 128], a) for a in range(4)]
                tpose_rounds(blocks, [(s_v, V_STG)], evac_z1, group=4)
        Z1_DONE = s_v.n

        if stop_at("ln1"):
            continue
        for a in range(4):
            b.op("scalar", (lambda a_: (lambda e: e.activation(
                out=exitst, in_=x_tm[:, a_ * HD:(a_ + 1) * HD], func=AF.Square,
                accum_out=cols[:, 28 + a_:29 + a_])))(a),
                waits=[(s_v, Z1_DONE)] + list(M_exitst), incs=[(s_sc, 1)])
            b.op("scalar", (lambda a_: (lambda e: e.activation(
                out=exitst, in_=x_tm[:, a_ * HD:(a_ + 1) * HD], func=AF.Copy,
                accum_out=cols[:, 16 + a_:17 + a_])))(a), incs=[(s_sc, 1)])
        SC_LN1S = s_sc.n
        M_exitst = [(s_sc, SC_LN1S)]
        b.op("scalar", lambda e: e.mul(out=cols[:, 16:20], in_=cols[:, 16:20], mul=ccol("inv2048")))
        b.op("scalar", lambda e: e.mul(out=cols[:, 28:32], in_=cols[:, 28:32], mul=ccol("inv2048")))
        b.op("scalar", lambda e: e.activation(out=cols[:, 20:24], in_=cols[:, 16:20], func=AF.Square))
        b.op("scalar", lambda e: e.mul(out=cols[:, 20:24], in_=cols[:, 20:24], mul=-1.0))
        for a4 in range(4):
            b.op("scalar", (lambda a_: (lambda e: e.add(
                out=cols[:, 12 + a_:13 + a_], in_=cols[:, 28 + a_:29 + a_],
                add=cols[:, 20 + a_:21 + a_])))(a4))
        b.op("scalar", lambda e: e.activation(out=cols[:, 12:16], in_=cols[:, 12:16], func=AF.Ln,
                                              bias=ccol("eps")))
        b.op("scalar", lambda e: e.activation(out=cols[:, 12:16], in_=cols[:, 12:16], func=AF.Exp,
                                              scale=ccol("mhalf")), incs=[(s_sc, 1)])
        SC_R1 = s_sc.n
        for a in range(4):
            b.op("vector", (lambda a_: (lambda e: e.tensor_scalar(
                out=x1_tm[:, a_ * HD:(a_ + 1) * HD], in0=x_tm[:, a_ * HD:(a_ + 1) * HD],
                scalar1=cols[:, 16 + a_:17 + a_], scalar2=cols[:, 12 + a_:13 + a_],
                op0=ALU.subtract, op1=ALU.mult)))(a),
                waits=[(s_sc, SC_R1)], incs=[(s_v, 1)])
        X1_DONE = s_v.n
        M_xtm_free = [(s_v, X1_DONE)]

        if stop_at("f1"):
            continue
        f1_first = GATE_ALL + [(s_v, X1_DONE), (s_sc, SC_Y2)]
        for a in range(4):
            def evac_ub(key, slot_ap, pe_mark, a_=a):
                i_ = key
                b.op("vector", (lambda i2, a2, sl: (lambda e: e.tensor_scalar(
                    out=a5[:, i2 * 512 + a2 * 128: i2 * 512 + (a2 + 1) * 128],
                    in0=sl, scalar1=su[:, i2:i2 + 1], scalar2=None, op0=ALU.mult)))(i_, a_, slot_ap),
                    waits=[pe_mark], incs=[(s_v, 1)])
                return (s_v, s_v.n)

            blocks = [(x1_tm[:, a * HD + i * 128: a * HD + (i + 1) * 128], i) for i in range(16)]
            tpose_rounds(blocks, f1_first, evac_ub, group=8)
        UB_RAW = s_v.n

        if stop_at("f2"):
            continue
        for st in range(4):
            bit = 1 << st
            srcbuf = a5 if st % 2 == 0 else ub1
            dstbuf = ub1 if st % 2 == 0 else a5
            stm = s_v.n
            gm = s_g.n
            extra = [(s_v, UB_RAW)] if st == 0 else []
            for i in range(16):
                if i & bit:
                    continue
                jp = i | bit
                for (oi, sgn) in ((i, ALU.add), (jp, ALU.subtract)):
                    eng = "vector" if (oi % 2 == 0) else "gpsimd"
                    sm = s_v if eng == "vector" else s_g
                    b.op(eng, (lambda sb_, db_, i_, jp_, oi_, sg2: (lambda e: e.tensor_tensor(
                        out=db_[:, oi_ * 512:(oi_ + 1) * 512],
                        in0=sb_[:, i_ * 512:(i_ + 1) * 512],
                        in1=sb_[:, jp_ * 512:(jp_ + 1) * 512],
                        op=sg2)))(srcbuf, dstbuf, i, jp, oi, sgn),
                        waits=[(s_v, stm), (s_g, gm)] + extra, incs=[(sm, 1)])
        b.op("vector", lambda e: e.engine_nop(), waits=[(s_v, s_v.n), (s_g, s_g.n)], incs=[(s_v, 1)])
        UB_DONE = s_v.n

        if stop_at("ffn3"):
            continue
        f3_mk = {}
        sidx = [0]
        for pb in range(32):
            for wsel, wsrc in enumerate([s_wup, s_wgt]):
                for kh in range(2):
                    gstrip = sidx[0]
                    sidx[0] += 1
                    par = gstrip % 4
                    waits = [(s_v, UB_DONE)]
                    if gstrip >= 4:
                        waits.append(f3_mk[gstrip - 4])
                    b.dma("sync", wbuf4[par], wsrc[pb * 2 + kh], s_w, waits=waits)
                    W_F3 = s_w.n
                    for j in range(2):
                        if wsel == 0:
                            bankF = 2 + (pb % 2) * 2 + j
                        else:
                            bankF = 0 + j
                        for k8 in range(8):
                            kt = kh * 8 + k8
                            mmw = [(s_w, W_F3)]
                            if kt == 0:
                                mmw += psf_free[bankF]
                            b.op("tensor", (lambda par_, j_, k8_, kt_, bankF_: (lambda e: e.matmul(
                                psF[bankF_], wbuf4[par_][:, k8_ * 256 + j_ * 128: k8_ * 256 + j_ * 128 + 128],
                                a5[:, kt_ * 512:(kt_ + 1) * 512],
                                start=(kt_ == 0), stop=(kt_ == 15))))(par, j, k8, kt, bankF),
                                waits=mmw, incs=([(s_pe, 1)] if (k8 == 7 and j == 1) else []))
                    f3_mk[gstrip] = (s_pe, s_pe.n)
            PE_UG = s_pe.n
            for j in range(2):
                tI = pb * 2 + j
                bank_u = 2 + (pb % 2) * 2 + j
                bank_g = 0 + j
                b.op("vector", (lambda tI_, bu_, j_: (lambda e: e.tensor_scalar(
                    out=usg[:, j_ * 1024:j_ * 1024 + 512], in0=psF[bu_],
                    scalar1=cu[:, tI_:tI_ + 1], scalar2=None, op0=ALU.mult)))(tI, bank_u, j),
                    waits=[(s_pe, PE_UG)], incs=[(s_v, 1)])
                psf_free[bank_u] = [(s_v, s_v.n)]
                b.op("scalar", (lambda tI_, bg_, j_: (lambda e: e.activation(
                    out=usg[:, j_ * 1024 + 512:j_ * 1024 + 1024], in_=psF[bg_], func=AF.Silu,
                    scale=cg[:, tI_:tI_ + 1])))(tI, bank_g, j),
                    waits=[(s_pe, PE_UG)], incs=[(s_sc, 1)])
                psf_free[bank_g] = [(s_sc, s_sc.n)]
                b.op("vector", (lambda tI_, j_: (lambda e: e.tensor_tensor(
                    out=proj[:, tI_ * 512:(tI_ + 1) * 512], in0=usg[:, j_ * 1024:j_ * 1024 + 512],
                    in1=usg[:, j_ * 1024 + 512:j_ * 1024 + 1024], op=ALU.mult)))(tI, j),
                    waits=[(s_v, s_v.n), (s_sc, s_sc.n)], incs=[(s_v, 1)])
        V_FFNV = s_v.n

        if stop_at("ffn"):
            continue
        f4_mk = {}
        for pb in range(8):
            for kq in range(8):
                gstrip = pb * 8 + kq
                par = gstrip % 4
                waits = [(s_v, V_FFNV)]
                if gstrip >= 4:
                    waits.append(f4_mk[gstrip - 4])
                b.dma("sync", wbuf4[par], s_wdn[pb * 8 + kq], s_w, waits=waits)
                W_D = s_w.n
                for j in range(2):
                    bankF = 2 + (pb % 2) * 2 + j
                    for k8 in range(8):
                        kt = kq * 8 + k8
                        mmw = [(s_w, W_D)]
                        if kt == 0:
                            mmw += psf_free[bankF]
                        b.op("tensor", (lambda par_, j_, k8_, kt_, bankF_: (lambda e: e.matmul(
                            psF[bankF_], wbuf4[par_][:, k8_ * 256 + j_ * 128: k8_ * 256 + j_ * 128 + 128],
                            proj[:, kt_ * 512:(kt_ + 1) * 512],
                            start=(kt_ == 0), stop=(kt_ == 63))))(par, j, k8, kt, bankF),
                            waits=mmw, incs=([(s_pe, 1)] if (k8 == 7 and j == 1) else []))
                f4_mk[gstrip] = (s_pe, s_pe.n)
            PE_DN = s_pe.n
            for j in range(2):
                tI = pb * 2 + j
                bankF = 2 + (pb % 2) * 2 + j
                stg = a6[:, j * 512:(j + 1) * 512]
                b.op("vector", (lambda stg_, bf_: (lambda e: e.tensor_copy(out=stg_, in_=psF[bf_])))(stg, bankF),
                     waits=[(s_pe, PE_DN)], incs=[(s_v, 1)])
                V_STG = s_v.n
                psf_free[bankF] = [(s_v, V_STG)]

                def evac_z(key, slot_ap, pe_mark, tI_=tI):
                    a_ = key
                    b.op("vector", (lambda a2, t2, sl: (lambda e: e.tensor_tensor(
                        out=zfull[:, a2 * HD + t2 * 128: a2 * HD + (t2 + 1) * 128],
                        in0=sl, in1=x1_tm[:, a2 * HD + t2 * 128: a2 * HD + (t2 + 1) * 128],
                        op=ALU.add)))(a_, tI_, slot_ap),
                        waits=[pe_mark], incs=[(s_v, 1)])
                    return (s_v, s_v.n)

                blocks = [(stg[:, a * 128:(a + 1) * 128], a) for a in range(4)]
                tpose_rounds(blocks, [(s_v, V_STG)], evac_z, group=4)
        Z_DONE = s_v.n

        for a in range(4):
            zt = zfull[:, a * HD:(a + 1) * HD]
            ew = [(s_v, Z_DONE), (s_out, M_out_exit)] + list(M_exitst)
            b.op("scalar", (lambda zt_: (lambda e: e.activation(
                out=exitst, in_=zt_, func=AF.Square, accum_out=cols[:, 24:25])))(zt),
                waits=ew, incs=[(s_sc, 1)])
            b.op("scalar", (lambda zt_: (lambda e: e.activation(
                out=exitst, in_=zt_, func=AF.Copy, accum_out=cols[:, 25:26])))(zt),
                incs=[(s_sc, 1)])
            b.op("scalar", lambda e: e.mul(out=cols[:, 24:26], in_=cols[:, 24:26], mul=ccol("inv2048")))
            b.op("scalar", lambda e: e.activation(out=cols[:, 26:27], in_=cols[:, 25:26], func=AF.Square))
            b.op("scalar", lambda e: e.mul(out=cols[:, 26:27], in_=cols[:, 26:27], mul=-1.0))
            b.op("scalar", lambda e: e.add(out=cols[:, 24:25], in_=cols[:, 24:25], add=cols[:, 26:27]))
            b.op("scalar", lambda e: e.activation(out=cols[:, 24:25], in_=cols[:, 24:25], func=AF.Ln,
                                                  bias=ccol("eps")))
            b.op("scalar", lambda e: e.activation(out=cols[:, 24:25], in_=cols[:, 24:25], func=AF.Exp,
                                                  scale=ccol("mhalf")), incs=[(s_sc, 1)])
            SC_LN2 = s_sc.n
            b.op("vector", (lambda zt_: (lambda e: e.tensor_scalar(
                out=exitst, in0=zt_, scalar1=cols[:, 25:26], scalar2=cols[:, 24:25],
                op0=ALU.subtract, op1=ALU.mult)))(zt),
                waits=[(s_sc, SC_LN2)], incs=[(s_v, 1)])
            V_OUT = s_v.n
            b.dma("sync", d_out[seq, tok0 + a * 128: tok0 + (a + 1) * 128, :], exitst, s_out,
                  waits=[(s_v, V_OUT)])
            M_out_exit = s_out.n
            M_exitst = [(s_out, M_out_exit)]

        M_proj_free = [(s_pe, s_pe.n)]
        M_a5_free = [(s_pe, s_pe.n)]
        M_xsy_free = [(s_v, Z_DONE), (s_pe, s_pe.n)]
        if first_of_seq is False and g + 1 < NGRP and (g + 1) % (NGRP_FULL // SEQ) == 0:
            pass
        if g + 1 < NGRP and (g + 1) % (NGRP_FULL // SEQ) == 0:
            b.op("gpsimd", lambda e: e.memset(carry, 0.0),
                 waits=list(M_carry) + [(s_pe, s_pe.n)], incs=[(s_g, 1)])
            M_carry = [(s_g, s_g.n)]

    local_map = dict(locals())
    for nm in cfg.get("dumps", []):
        ap = local_map[nm]
        dd = nc.dram_tensor("dump_" + nm, list(ap.shape), ap.dtype, kind="ExternalOutput").ap()
        b.dma("sync", dd, ap, s_out,
              waits=[(s_v, s_v.n), (s_sc, s_sc.n), (s_pe, s_pe.n), (s_g, s_g.n),
                     (s_w, s_w.n), (s_io, s_io.n), (s_d2, s_d2.n)])

    b.op("sync", lambda e: e.nop(), waits=[(s_out, s_out.n)])

    import contextlib
    with contextlib.ExitStack() as ctx:
        for s in b.sems:
            s.h = ctx.enter_context(nc.semaphore(s.name))
        block = ctx.enter_context(nc.Block())
        b.emit(block)
    return nc



def _stage(inputs):
    ip = {k: np.asarray(v) for k, v in inputs.items()}
    st = {}
    win = np.ascontiguousarray(ip["L0_attn_in_proj_w"].T).astype(np.float32, copy=False)
    win_pad = np.zeros((HD, PROJ_PAD), np.float32)
    win_pad[:, :PROJ] = win
    st["w_in_T"] = win_pad
    st["w_op_T"] = np.ascontiguousarray(ip["L0_attn_out_proj_w"].T)
    st["w_up_T"] = np.ascontiguousarray(ip["L0_ffn_W_up"].T)
    st["w_gt_T"] = np.ascontiguousarray(ip["L0_ffn_W_gate"].T)
    st["w_dn_T"] = np.ascontiguousarray(ip["L0_ffn_W_down"].T)
    st["convw_in"] = np.ascontiguousarray(
        ip["L0_attn_conv_w"].reshape(NT_XBC, 128, KC).transpose(1, 0, 2).reshape(128, NT_XBC * KC))
    st["convb_in"] = np.ascontiguousarray(ip["L0_attn_conv_b"].reshape(NT_XBC, 128).T)
    st["lnw_cols"] = np.ascontiguousarray(ip["L0_attn_ln_weight"].reshape(16, 128).T)
    st["normw_cols"] = np.ascontiguousarray(ip["L0_attn_norm_weight"].reshape(32, 128).T)
    st["su_cols"] = np.ascontiguousarray(ip["L0_ffn_SU_ug"].reshape(16, 128).T)
    st["sudp_cols"] = np.ascontiguousarray(ip["L0_ffn_SU_dp"].reshape(64, 128).T)
    st["wsu_up"] = np.ascontiguousarray(ip["L0_ffn_Wscale_ug"][:INTER].reshape(64, 128).T)
    st["wsu_gt"] = np.ascontiguousarray(ip["L0_ffn_Wscale_ug"][INTER:].reshape(64, 128).T)
    st["wsdp_rep"] = np.ascontiguousarray(np.broadcast_to(ip["L0_ffn_Wscale_dp"][None, :], (64, HD)))
    st["d_cols_in"] = np.ascontiguousarray(np.repeat(ip["L0_attn_D"], P).reshape(32, 128).T)
    st["dt_bias"] = np.ascontiguousarray(ip["L0_attn_dt_bias"].reshape(NH, 1))
    st["a_log"] = np.ascontiguousarray(ip["L0_attn_A_log"].reshape(NH, 1))
    st["h128"] = _hadamard(128)
    st["h64"] = _hadamard(64)
    st["causalT"] = np.ascontiguousarray(np.triu(np.ones((128, 128), np.float32)))
    return st


def kernel(**inputs):
    cfg = {}
    nc = build_program(cfg)
    shared = _stage(inputs)
    x_full = np.asarray(inputs["L0_attn_hidden_states"])
    in_maps = []
    for c in range(8):
        m = dict(shared)
        m["x"] = np.ascontiguousarray(x_full[2 * c:2 * c + 2])
        in_maps.append(m)
    res = run_bass_kernel_spmd(nc, in_maps, core_ids=list(range(8)))
    out = np.concatenate([np.asarray(r["out"]) for r in res.results], axis=0)
    return out


# version 05 (final)
# speedup vs baseline: 1.1234x; 1.0445x over previous
import math
import numpy as np
import ml_dtypes
import concourse.bass as bass
import concourse.mybir as mybir
from concourse.bass_utils import run_bass_kernel_spmd

F32 = mybir.dt.float32
BF16 = mybir.dt.bfloat16
AF = mybir.ActivationFunctionType
ALU = mybir.AluOpType
AXL = mybir.AxisListType

SEQ = 2
T = 2048
HD = 2048
I = 4096
NH = 64
P = 64
NS = 64
KC = 4
CONV = I + 2 * NS
PROJ = I + CONV + NH
PROJ_PAD = 8448
NT_GATE = 32
NT_XBC = 33
CH = 128
G = 512
NCHUNK = G // CH
NGRP_FULL = SEQ * T // G
INTER = 8192
EPS = 1e-5

bfnp = ml_dtypes.bfloat16


def _hadamard(n):
    h = np.array([[1.0]], dtype=np.float64)
    while h.shape[0] < n:
        h = np.block([[h, h], [h, -h]])
    return np.ascontiguousarray(h.astype(np.float32))


class Sem:
    __slots__ = ("name", "n", "h")

    def __init__(self, name):
        self.name = name
        self.n = 0
        self.h = None


class Op:
    __slots__ = ("eng", "fn", "waits", "incs")

    def __init__(self, eng, fn, waits, incs):
        self.eng = eng
        self.fn = fn
        self.waits = waits
        self.incs = incs


class Builder:
    def __init__(self, cfg=None):
        self.cfg = cfg or {}
        self.ops = []
        self.sems = []

    def sem(self, name):
        s = Sem(name)
        self.sems.append(s)
        return s

    def op(self, eng, fn, waits=(), incs=()):
        winc = []
        for s, k in incs:
            s.n += k
            winc.append((s, k))
        self.ops.append(Op(eng, fn, [(s, v) for (s, v) in waits], winc))

    def dma(self, eng, out, in_, sem, waits=(), **kw):
        def fn(e, o=out, i=in_, kw2=dict(kw)):
            return e.dma_start(out=o, in_=i, **kw2)
        self.op(eng, fn, waits=waits, incs=[(sem, 16)])

    def emit(self, block):
        for eng_name in ["sync", "tensor", "vector", "scalar", "gpsimd"]:
            ops = [o for o in self.ops if o.eng == eng_name]

            def make(ops_):
                def run(eng):
                    waited = {}
                    for o in ops_:
                        for (s, v) in o.waits:
                            if waited.get(id(s), -1) < v:
                                eng.wait_ge(s.h, v)
                                waited[id(s)] = v
                        inst = o.fn(eng)
                        for (s, k) in o.incs:
                            inst.then_inc(s.h, k)
                return run
            getattr(block, eng_name)(make(ops))


def build_program(cfg):
    b = Builder(cfg)
    nc = bass.Bass("TRN2", target_bir_lowering=False, debug=False)
    NGRP = cfg.get("groups", NGRP_FULL)

    d_x = nc.dram_tensor("x", [SEQ, T, HD], F32, kind="ExternalInput").ap()
    d_win = nc.dram_tensor("w_in_T", [HD, PROJ_PAD], F32, kind="ExternalInput").ap()
    d_wop = nc.dram_tensor("w_op_T", [I, HD], F32, kind="ExternalInput").ap()
    d_wup = nc.dram_tensor("w_up_T", [HD, INTER], F32, kind="ExternalInput").ap()
    d_wgt = nc.dram_tensor("w_gt_T", [HD, INTER], F32, kind="ExternalInput").ap()
    d_wdn = nc.dram_tensor("w_dn_T", [INTER, HD], F32, kind="ExternalInput").ap()
    d_convw = nc.dram_tensor("convw_in", [128, NT_XBC * KC], F32, kind="ExternalInput").ap()
    d_convb = nc.dram_tensor("convb_in", [128, NT_XBC], F32, kind="ExternalInput").ap()
    d_lnw = nc.dram_tensor("lnw_cols", [128, 16], F32, kind="ExternalInput").ap()
    d_normw = nc.dram_tensor("normw_cols", [128, 32], F32, kind="ExternalInput").ap()
    d_su = nc.dram_tensor("su_cols", [128, 16], F32, kind="ExternalInput").ap()
    d_sudp = nc.dram_tensor("sudp_cols", [128, 64], F32, kind="ExternalInput").ap()
    d_wsu_u = nc.dram_tensor("wsu_up", [128, 64], F32, kind="ExternalInput").ap()
    d_wsu_g = nc.dram_tensor("wsu_gt", [128, 64], F32, kind="ExternalInput").ap()
    d_wsdp = nc.dram_tensor("wsdp_rep", [64, HD], F32, kind="ExternalInput").ap()
    d_dcols = nc.dram_tensor("d_cols_in", [128, 32], F32, kind="ExternalInput").ap()
    d_dtb = nc.dram_tensor("dt_bias", [NH, 1], F32, kind="ExternalInput").ap()
    d_alog = nc.dram_tensor("a_log", [NH, 1], F32, kind="ExternalInput").ap()
    d_h128 = nc.dram_tensor("h128", [128, 128], F32, kind="ExternalInput").ap()
    d_h64 = nc.dram_tensor("h64", [64, 64], F32, kind="ExternalInput").ap()
    d_causal = nc.dram_tensor("causalT", [128, 128], F32, kind="ExternalInput").ap()
    d_out = nc.dram_tensor("out", [SEQ, T, HD], F32, kind="ExternalOutput").ap()

    s_win = nc.dram_tensor("s_win", [66, 128, 2048], BF16, kind="Internal").ap()
    s_wop = nc.dram_tensor("s_wop", [32, 128, 2048], BF16, kind="Internal").ap()
    s_wup = nc.dram_tensor("s_wup", [64, 128, 2048], BF16, kind="Internal").ap()
    s_wgt = nc.dram_tensor("s_wgt", [64, 128, 2048], BF16, kind="Internal").ap()
    s_wdn1 = nc.dram_tensor("s_wdn1", [INTER, HD], BF16, kind="Internal").ap()
    s_wdn = nc.dram_tensor("s_wdn", [64, 128, 2048], BF16, kind="Internal").ap()
    s_acum = nc.dram_tensor("s_acum", [NH, CH], F32, kind="Internal").ap()
    s_row = nc.dram_tensor("s_row", [1, 512], F32, kind="Internal").ap()

    SBUF_BASE = 16384 + 2048
    cur = [SBUF_BASE]
    AL = 64

    def salloc(name, shape, dtype, at=None):
        per = 1
        for d in shape[1:]:
            per *= d
        per *= {F32: 4, BF16: 2}[dtype]
        if at is None:
            off = cur[0]
            cur[0] = (off + per + AL - 1) // AL * AL
        else:
            off = at
            assert at >= SBUF_BASE
        return nc.alloc_sbuf_tensor_at(name, shape, dtype, offset=off).ap(), off

    x_tm, _ = salloc("x_tm", [128, 4 * HD], BF16)
    exitst, exit_off = salloc("exitst", [128, HD], F32)
    xn_stage, _ = salloc("xn_stage", [128, HD], BF16, at=exit_off)
    x1_tm, _ = salloc("x1_tm", [128, 4 * HD], BF16)
    a5, a5_off = salloc("a5", [128, 16 * 512], BF16)
    a6, _ = salloc("a6", [128, 1024], BF16)
    proj, proj_off = salloc("proj", [128, NT_GATE * 512 + NT_XBC * 515 + 512], BF16)
    GATE_COLS = NT_GATE * 512
    XBC_COLS = NT_XBC * 515
    xbcwin = proj_off + GATE_COLS * 2
    replo = [salloc("replo0", [128, 1024], F32, at=xbcwin)[0],
             salloc("replo1", [128, 1024], F32, at=xbcwin + 4096)[0]]
    scoreso = [salloc("scores0", [128, 1024], BF16, at=xbcwin + 8192)[0],
               salloc("scores1", [128, 1024], BF16, at=xbcwin + 10240)[0]]
    bdeco = [salloc("bdec0", [128, 512], BF16, at=xbcwin + 12288)[0],
             salloc("bdec1", [128, 512], BF16, at=xbcwin + 13312)[0]]
    xct, _ = salloc("xct", [128, 4096], BF16, at=xbcwin + 14336)
    cdeco = [salloc("cdec0", [64, 1024], BF16, at=xbcwin + 22528)[0],
             salloc("cdec1", [64, 1024], BF16, at=xbcwin + 24576)[0]]
    y2win, _ = salloc("y2win", [128, NT_GATE * 512], BF16, at=xbcwin)
    pp_in = [salloc("pp_in0", [128, 8192], BF16, at=proj_off)[0],
             salloc("pp_in1", [128, 8192], BF16, at=proj_off + 16384)[0]]
    pp_inf = [salloc("pp_inf0", [128, 4096], F32, at=proj_off)[0],
              salloc("pp_inf1", [128, 4096], F32, at=proj_off + 16384)[0]]
    pp_out = [salloc("pp_out0", [128, 8192], BF16, at=proj_off + 32768)[0],
              salloc("pp_out1", [128, 8192], BF16, at=proj_off + 49152)[0]]
    wsdp, _ = salloc("wsdp", [64, HD], F32, at=proj_off + 65536)

    xsy, xsy_off = salloc("xsy", [128, NT_XBC * 512], BF16)
    ub1, _ = salloc("ub1", [128, 8192], BF16, at=xsy_off)
    zfull, _ = salloc("zfull", [128, 8192], BF16, at=xsy_off)
    usg, _ = salloc("usg", [128, 2048], BF16, at=xsy_off + 16384)

    wbuf_x = [salloc("wbufC", [128, 2048], BF16, at=xsy_off + 20480)[0],
              salloc("wbufD", [128, 2048], BF16, at=xsy_off + 24576)[0]]
    carry, _ = salloc("carry", [64, 4096], F32)
    carryb = [salloc("carryb0", [64, 512], BF16)[0],
              salloc("carryb1", [64, 512], BF16)[0]]
    dt_sp, _ = salloc("dt_sp", [64, G], F32)
    dt_spb, _ = salloc("dt_spb", [64, G], BF16)
    acum, _ = salloc("acum", [64, G], F32)
    dA_t, _ = salloc("dA_t", [64, G], F32)
    dshm, _ = salloc("dshm", [64, CH], BF16)
    dssub, _ = salloc("dssub", [64, CH], F32)
    wbuf = [salloc("wbufA", [128, 2048], BF16)[0],
            salloc("wbufB", [128, 2048], BF16)[0]]
    wbuf4 = wbuf + wbuf_x
    sg = [salloc("sgA", [128, 512], BF16)[0],
          salloc("sgB", [128, 512], BF16)[0]]
    tmpg, _ = salloc("tmpg", [128, 512], BF16)
    halo, _ = salloc("halo", [128, NT_XBC * 3], BF16)
    cols, _ = salloc("cols", [128, 32], F32)
    consts, _ = salloc("consts", [128, 16], F32)
    lnw, _ = salloc("lnw", [128, 16], F32)
    normw, _ = salloc("normw", [128, 32], F32)
    su, _ = salloc("su", [128, 16], F32)
    cu, _ = salloc("cu", [128, 64], F32)
    cg, _ = salloc("cg", [128, 64], F32)
    sudp_t, _ = salloc("sudp_t", [128, 64], F32)
    dcol, _ = salloc("dcol", [128, 32], F32)
    convw, _ = salloc("convw", [128, NT_XBC * KC], F32)
    convb, _ = salloc("convb", [128, NT_XBC], F32)
    idb, _ = salloc("idb", [128, 128], BF16)
    idf, _ = salloc("idf", [128, 128], F32)
    h128b, _ = salloc("h128b", [128, 128], BF16)
    h64b, _ = salloc("h64b", [64, 64], BF16)
    causal, _ = salloc("causal", [128, 128], BF16)
    onesb, _ = salloc("onesb", [128, 1], BF16)
    onesf, _ = salloc("onesf", [1, 128], F32)
    ones64, _ = salloc("ones64", [64, CH], F32)
    abias, _ = salloc("abias", [64, 1], F32)
    acol, _ = salloc("acol", [64, 1], F32)
    cc_buf, _ = salloc("cc_buf", [64, 512], BF16)
    bct, _ = salloc("bct", [128, 64], BF16)
    dtt, _ = salloc("dtt", [128, 64], BF16)
    act_t, _ = salloc("act_t", [128, 64], F32)
    dst, _ = salloc("dst", [128, 64], BF16)
    cbm, _ = salloc("cbm", [128, 128], BF16)
    cdecr, _ = salloc("cdecr", [64, 64], F32)
    rowst, _ = salloc("rowst", [2, 512], F32)

    assert cur[0] <= 228000, f"SBUF over budget: {cur[0]} bytes/partition"

    psB = [nc.alloc_psum_tensor(f"psb{i}", [128, 1024], BF16).ap() for i in range(2)]
    psF = [nc.alloc_psum_tensor(f"psf{i}", [128, 512], F32).ap() for i in range(6)]

    CONSTS = {"inv2048": (0, 1.0 / 2048.0), "eps": (1, EPS), "mhalf": (2, -0.5),
              "invs2048": (3, 1.0 / math.sqrt(2048.0)), "one": (4, 1.0),
              "invs8192": (5, 1.0 / math.sqrt(8192.0)), "neg1": (6, -1.0),
              "inv4096": (7, 1.0 / 4096.0), "ln05": (8, -0.6931471805599453)}

    def ccol(name, lo=0, hi=128):
        return consts[lo:hi, CONSTS[name][0]:CONSTS[name][0] + 1]

    def proj_gate(t):
        return proj[:, t * 512:(t + 1) * 512]

    def proj_xbc(t, lo=0, span=512):
        base = GATE_COLS + t * 515 + 3
        return proj[:, base + lo:base + lo + span]

    def proj_xbc_halo(t):
        base = GATE_COLS + t * 515
        return proj[:, base:base + 515]

    def proj_dtile():
        base = GATE_COLS + XBC_COLS
        return proj[0:64, base:base + 512]

    def xs_tile(t):
        return xsy[:, t * 512:(t + 1) * 512]

    s_io = b.sem("io")
    s_w = b.sem("w")
    s_d2 = b.sem("d2")
    s_pe = b.sem("pe")
    s_v = b.sem("v")
    s_sc = b.sem("sc")
    s_g = b.sem("g")
    s_out = b.sem("out")

    psf_free = [[] for _ in range(6)]
    psb_free = [[] for _ in range(2)]

    for dst_, src_ in [(convw, d_convw), (convb, d_convb), (lnw, d_lnw), (normw, d_normw),
                       (su, d_su), (dcol, d_dcols), (wsdp, d_wsdp), (abias, d_dtb),
                       (acol, d_alog), (cu, d_wsu_u), (cg, d_wsu_g), (sudp_t, d_sudp)]:
        b.dma("sync", dst_, src_, s_w)
    b.dma("gpsimd", h128b, d_h128, s_io)
    b.dma("gpsimd", h64b, d_h64, s_io)
    b.dma("gpsimd", causal, d_causal, s_io)
    W0 = s_w.n
    IO0 = s_io.n

    b.op("vector", lambda e: e.memset(onesb, 1.0))
    b.op("vector", lambda e: e.memset(onesf, 1.0))
    b.op("vector", lambda e: e.memset(ones64, 1.0))
    for nm, (idx, val) in CONSTS.items():
        b.op("vector", (lambda i_, v_: (lambda e: e.memset(consts[:, i_:i_ + 1], v_)))(idx, val))
    b.op("gpsimd", lambda e: e.memset(idb, 0.0))
    b.op("gpsimd", lambda e: e.affine_select(out=idb, in_=idb, compare_op=ALU.not_equal,
                                             fill=1.0, base=0, pattern=[[-1, 128]], channel_multiplier=1))
    b.op("gpsimd", lambda e: e.memset(idf, 0.0))
    b.op("gpsimd", lambda e: e.affine_select(out=idf, in_=idf, compare_op=ALU.not_equal,
                                             fill=1.0, base=0, pattern=[[-1, 128]], channel_multiplier=1))
    b.op("gpsimd", lambda e: e.memset(halo, 0.0))
    b.op("gpsimd", lambda e: e.memset(carry, 0.0), incs=[(s_g, 1)])
    G_INIT = s_g.n

    b.op("scalar", lambda e: e.activation(out=acol, in_=acol, func=AF.Exp),
         waits=[(s_w, W0)], incs=[(s_sc, 1)])
    SC_A = s_sc.n
    b.op("vector", lambda e: e.tensor_scalar(out=acol, in0=acol, scalar1=-1.0,
                                             scalar2=None, op0=ALU.mult),
         waits=[(s_sc, SC_A), (s_w, W0)])
    b.op("vector", lambda e: e.tensor_tensor(out=cu, in0=cu, in1=sudp_t, op=ALU.mult),
         waits=[(s_w, W0)])
    b.op("vector", lambda e: e.tensor_scalar(out=cu, in0=cu, scalar1=ccol("invs2048"),
                                             scalar2=None, op0=ALU.mult))
    b.op("vector", lambda e: e.tensor_scalar(out=cg, in0=cg, scalar1=ccol("invs2048"),
                                             scalar2=None, op0=ALU.mult))
    b.op("vector", lambda e: e.tensor_scalar(out=wsdp, in0=wsdp, scalar1=ccol("invs8192", 0, 64),
                                             scalar2=None, op0=ALU.mult), incs=[(s_v, 1)])
    V_CONST = s_v.n

    skip_pre = cfg.get("skip_prepass", False)
    if not skip_pre:
        swin_v = s_win.rearrange("(pb kh) kp (ktl p) -> pb kh kp ktl p", kh=2, p=256)
        for i in range(16):
            p1dst = swin_v[:, i // 8, :, i % 8, :].transpose([1, 0, 2])
            srcv2 = d_win[i * 128:(i + 1) * 128, :].rearrange("kp (pb p) -> kp pb p", p=256)
            b.dma("gpsimd", p1dst, srcv2, s_io)

        mk = {}
        for i in range(32):
            par = i % 2
            fin = pp_inf[par][:, 0:2048]
            fout = pp_out[par][:, 0:2048]
            waits = [(s_v, V_CONST)]
            if i >= 2:
                waits.append((s_w, mk[("p2o", i - 2)]))
            b.dma("sync", fin, d_wop[i * 128:(i + 1) * 128, :], s_w, waits=waits)
            W_IN = s_w.n
            b.op("vector", (lambda fin_, fout_, i_: (lambda e: e.tensor_scalar(
                out=fout_, in0=fin_, scalar1=normw[:, i_:i_ + 1], scalar2=None, op0=ALU.mult)))(fin, fout, i),
                waits=[(s_w, W_IN)], incs=[(s_v, 1)])
            swop_v = s_wop.rearrange("(pb kq) kp (ktl p) -> pb kq kp ktl p", kq=4, p=256)
            b.dma("sync", swop_v[:, i // 8, :, i % 8, :].transpose([1, 0, 2]),
                  fout.rearrange("kp (pb p) -> kp pb p", p=256),
                  s_w, waits=[(s_v, s_v.n)])
            mk[("p2o", i)] = s_w.n

        P2_DONE = mk[("p2o", 31)]
        for wi, (dsrc, ddst) in enumerate([(d_wup, s_wup), (d_wgt, s_wgt)]):
            for kt in range(16):
                it = wi * 16 + kt
                par = it % 2
                waits = [(s_w, P2_DONE)]
                if it >= 2:
                    waits.append((s_w, mk[("p3o", it - 2)]))
                iow = s_io.n
                b.dma("gpsimd", pp_in[par][:, 0:4096], dsrc[kt * 128:(kt + 1) * 128, 0:4096], s_io, waits=waits)
                b.dma("gpsimd", pp_in[par][:, 4096:8192], dsrc[kt * 128:(kt + 1) * 128, 4096:8192], s_io, waits=waits)
                IOW = s_io.n
                for j in range(16):
                    bk = j % 2
                    b.op("tensor", (lambda par_, j_, bk_: (lambda e: e.matmul(
                        psF[bk_], h128b, pp_in[par_][:, j_ * 512:(j_ + 1) * 512],
                        start=True, stop=True)))(par, j, bk),
                        waits=[(s_io, IOW), (s_io, IO0)] + psf_free[bk], incs=[(s_pe, 1)])
                    b.op("vector", (lambda par_, j_, bk_: (lambda e: e.tensor_copy(
                        out=pp_out[par_][:, j_ * 512:(j_ + 1) * 512], in_=psF[bk_])))(par, j, bk),
                        waits=[(s_pe, s_pe.n)], incs=[(s_v, 1)])
                    psf_free[bk] = [(s_v, s_v.n)]
                ddst_v = ddst.rearrange("(pb kh) kp (ktl p) -> pb kh kp ktl p", kh=2, p=256)
                b.dma("sync", ddst_v[:, kt // 8, :, kt % 8, :].transpose([1, 0, 2]),
                      pp_out[par].rearrange("kp (pb p) -> kp pb p", p=256),
                      s_w, waits=[(s_v, s_v.n)])
                mk[("p3o", it)] = s_w.n

        dwdn_v = d_wdn.rearrange("(i jr) r -> i (jr r)", i=64)
        swdn1_v = s_wdn1.rearrange("(i jr) r -> i (jr r)", i=64)
        P3_DONE = mk[("p3o", 31)]
        for c in range(32):
            par = c % 2
            fin = pp_in[par][0:64, 0:8192]
            fout = pp_out[par][0:64, 0:8192]
            waits = [(s_w, P3_DONE)]
            if c >= 2:
                waits.append((s_w, mk[("p4o", c - 2)]))
            b.dma("gpsimd", fin, dwdn_v[:, c * 8192:(c + 1) * 8192], s_io, waits=waits)
            IOW = s_io.n
            for j in range(16):
                bk = j % 2
                rstart = (c * 8192 + j * 512) % 2048
                b.op("tensor", (lambda fin_, j_, bk_: (lambda e: e.matmul(
                    psF[bk_][0:64, :], h64b, fin_[:, j_ * 512:(j_ + 1) * 512],
                    start=True, stop=True)))(fin, j, bk),
                    waits=[(s_io, IOW), (s_v, V_CONST)] + psf_free[bk], incs=[(s_pe, 1)])
                b.op("vector", (lambda fout_, j_, rs_, bk_: (lambda e: e.tensor_tensor(
                    out=fout_[:, j_ * 512:(j_ + 1) * 512], in0=psF[bk_][0:64, :],
                    in1=wsdp[:, rs_:rs_ + 512], op=ALU.mult)))(fout, j, rstart, bk),
                    waits=[(s_pe, s_pe.n)], incs=[(s_v, 1)])
                psf_free[bk] = [(s_v, s_v.n)]
            b.dma("sync", swdn1_v[:, c * 8192:(c + 1) * 8192], fout, s_w, waits=[(s_v, s_v.n)])
            mk[("p4o", c)] = s_w.n

        for kt in range(64):
            par = kt % 2
            fin = pp_in[par][:, 0:2048]
            fout = pp_out[par][:, 0:2048]
            waits = [(s_w, mk[("p4o", 31)])]
            if kt >= 2:
                waits.append((s_w, mk[("p5o", kt - 2)]))
            b.dma("sync", fin, s_wdn1[kt * 128:(kt + 1) * 128, :], s_w, waits=waits)
            W_IN = s_w.n
            for j in range(4):
                bk = j % 2
                b.op("tensor", (lambda fin_, j_, bk_: (lambda e: e.matmul(
                    psF[bk_], h128b, fin_[:, j_ * 512:(j_ + 1) * 512],
                    start=True, stop=True)))(fin, j, bk),
                    waits=[(s_w, W_IN)] + psf_free[bk], incs=[(s_pe, 1)])
                b.op("vector", (lambda fout_, j_, bk_: (lambda e: e.tensor_copy(
                    out=fout_[:, j_ * 512:(j_ + 1) * 512], in_=psF[bk_])))(fout, j, bk),
                    waits=[(s_pe, s_pe.n)], incs=[(s_v, 1)])
                psf_free[bk] = [(s_v, s_v.n)]
            swdn_v = s_wdn.rearrange("(pb kq) kp (ktl p) -> pb kq kp ktl p", kq=8, p=256)
            b.dma("sync", swdn_v[:, kt // 8, :, kt % 8, :].transpose([1, 0, 2]),
                  fout.rearrange("kp (pb p) -> kp pb p", p=256),
                  s_w, waits=[(s_v, s_v.n)])
            mk[("p5o", kt)] = s_w.n

    PRE_DONE = [(s_w, s_w.n), (s_v, s_v.n), (s_io, s_io.n), (s_g, G_INIT), (s_sc, s_sc.n)]

    tb_par = [0]

    def tpose_rounds(blocks, first_waits, evac_emit, group=4):
        nrounds = (len(blocks) + group - 1) // group
        for r in range(nrounds):
            bank = tb_par[0] % 2
            tb_par[0] += 1
            chunk = blocks[r * group:(r + 1) * group]
            for si, (in_ap, key) in enumerate(chunk):
                b.op("tensor", (lambda a_, bk_, s_: (lambda e: e.transpose(
                    psB[bk_][:, s_ * 128:(s_ + 1) * 128], a_, idb)))(in_ap, bank, si),
                    waits=list(first_waits) + psb_free[bank], incs=[(s_pe, 1)])
            PE_M = s_pe.n
            last = None
            for si, (in_ap, key) in enumerate(chunk):
                last = evac_emit(key, psB[bank][:, si * 128:(si + 1) * 128], (s_pe, PE_M))
            psb_free[bank] = [last]

    M_out_exit = 0
    M_xtm_free = [(s_v, 0)]
    M_proj_free = list(PRE_DONE)
    M_a5_free = list(PRE_DONE)
    M_xsy_free = [(s_g, G_INIT)]
    M_carry = [(s_g, G_INIT)]
    M_exitst = [(s_out, 0)]

    STOP = cfg.get("stop_after")

    def stop_at(name, _done=[False]):
        order = ["entry", "in_proj", "conv", "dt", "ssd", "op", "ln1", "f1", "f2", "ffn3", "ffn", None]
        if STOP is None:
            return False
        return order.index(STOP) < order.index(name)

    for g in range(NGRP):
        seq = g // (NGRP_FULL // SEQ)
        tok0 = (g % (NGRP_FULL // SEQ)) * G
        first_of_seq = (tok0 == 0)

        src = d_x[seq, tok0:tok0 + G, :].rearrange("(a p) h -> p a h", p=128)
        b.dma("gpsimd", x_tm.rearrange("p (a h) -> p a h", h=HD), src, s_io,
              waits=list(M_xtm_free))
        IO_X = s_io.n

        for a in range(4):
            b.op("scalar", (lambda a_: (lambda e: e.activation(
                out=exitst, in_=x_tm[:, a_ * HD:(a_ + 1) * HD], func=AF.Square,
                accum_out=cols[:, a_:a_ + 1])))(a),
                waits=[(s_io, IO_X)] + list(M_exitst), incs=[(s_sc, 1)])
        b.op("scalar", lambda e: e.activation(out=cols[:, 4:8], in_=cols[:, 0:4], func=AF.Ln,
                                              scale=ccol("inv2048"), bias=ccol("eps")),
             waits=[(s_sc, s_sc.n)])
        b.op("scalar", lambda e: e.activation(out=cols[:, 4:8], in_=cols[:, 4:8], func=AF.Exp,
                                              scale=ccol("mhalf")), incs=[(s_sc, 1)])
        SC_RSTD1 = s_sc.n
        M_exitst = [(s_sc, SC_RSTD1)]

        for a in range(4):
            b.op("vector", (lambda a_: (lambda e: e.tensor_scalar(
                out=xn_stage, in0=x_tm[:, a_ * HD:(a_ + 1) * HD],
                scalar1=cols[:, 4 + a_:5 + a_], scalar2=None, op0=ALU.mult)))(a),
                waits=[(s_sc, SC_RSTD1)] + psb_free[0] + psb_free[1] + list(M_exitst),
                incs=[(s_v, 1)])
            V_STAGE = s_v.n
            M_exitst = [(s_v, V_STAGE)]

            def evac_xn(key, slot_ap, pe_mark, a_=a):
                i_ = key
                b.op("vector", (lambda i2, a2, sl: (lambda e: e.tensor_scalar(
                    out=a5[:, i2 * 512 + a2 * 128: i2 * 512 + (a2 + 1) * 128],
                    in0=sl, scalar1=lnw[:, i2:i2 + 1], scalar2=None, op0=ALU.mult)))(i_, a_, slot_ap),
                    waits=[pe_mark] + list(M_a5_free), incs=[(s_v, 1)])
                return (s_v, s_v.n)

            blocks = [(xn_stage[:, i * 128:(i + 1) * 128], i) for i in range(16)]
            tpose_rounds(blocks, [(s_v, V_STAGE)], evac_xn, group=8)
            M_exitst = [(s_pe, s_pe.n)]
        V_XN_DONE = s_v.n
        M_a5_free = []

        if stop_at("in_proj"):
            continue
        ip_ev = {}
        ip_strip = {}
        for pb in range(33):
            for kh in range(2):
                gstrip = pb * 2 + kh
                par = gstrip % 4
                waits = list(M_proj_free) + [(s_v, V_XN_DONE)]
                if gstrip >= 4:
                    waits.append(ip_strip[gstrip - 4])
                b.dma("sync", wbuf4[par], s_win[gstrip], s_w, waits=waits)
                W_IP = s_w.n
                for j in range(2):
                    bankF = 2 + (pb % 2) * 2 + j
                    for k8 in range(8):
                        kt = kh * 8 + k8
                        mmw = [(s_w, W_IP)]
                        if kt == 0:
                            mmw += psf_free[bankF]
                        b.op("tensor", (lambda par_, j_, k8_, kt_, bankF_: (lambda e: e.matmul(
                            psF[bankF_], wbuf4[par_][:, k8_ * 256 + j_ * 128: k8_ * 256 + j_ * 128 + 128],
                            a5[:, kt_ * 512:(kt_ + 1) * 512],
                            start=(kt_ == 0), stop=(kt_ == 15))))(par, j, k8, kt, bankF),
                            waits=mmw, incs=([(s_pe, 1)] if (k8 == 7 and j == 1) else []))
                ip_strip[gstrip] = (s_pe, s_pe.n)
            PE_MM = s_pe.n
            for j in range(2):
                tI = pb * 2 + j
                bankF = 2 + (pb % 2) * 2 + j
                if tI < NT_GATE:
                    dstap = proj_gate(tI)
                elif tI < NT_GATE + NT_XBC:
                    dstap = proj_xbc(tI - NT_GATE)
                else:
                    dstap = proj[:, GATE_COLS + XBC_COLS: GATE_COLS + XBC_COLS + 512]
                if tI % 2 == 0:
                    b.op("vector", (lambda d_, bf_: (lambda e: e.tensor_copy(out=d_, in_=psF[bf_])))(dstap, bankF),
                         waits=[(s_pe, PE_MM)], incs=[(s_v, 1)])
                    psf_free[bankF] = [(s_v, s_v.n)]
                    ip_ev[tI] = (s_v, s_v.n)
                else:
                    b.op("scalar", (lambda d_, bf_: (lambda e: e.copy(out=d_, in_=psF[bf_])))(dstap, bankF),
                         waits=[(s_pe, PE_MM)], incs=[(s_sc, 1)])
                    psf_free[bankF] = [(s_sc, s_sc.n)]
                    ip_ev[tI] = (s_sc, s_sc.n)
        IP_ALL = [(s_v, s_v.n), (s_sc, s_sc.n)]
        IP_PE_END = (s_pe, s_pe.n)

        if stop_at("conv"):
            continue
        if first_of_seq and g > 0:
            b.op("gpsimd", lambda e: e.memset(halo, 0.0), waits=list(M_xsy_free), incs=[(s_g, 1)])
        HALO_RDY = (s_g, s_g.n)
        cv_marks = {}
        for t in range(NT_XBC):
            eng = "vector"
            sm = s_v
            w0 = GATE_COLS + t * 515
            b.op(eng, (lambda t_, w0_: (lambda e: e.tensor_copy(
                out=proj[:, w0_:w0_ + 3], in_=halo[:, t_ * 3:t_ * 3 + 3])))(t, w0),
                waits=[ip_ev[NT_GATE + t], HALO_RDY], incs=[(sm, 1)])
            b.op(eng, (lambda t_: (lambda e: e.tensor_copy(
                out=halo[:, t_ * 3:t_ * 3 + 3], in_=proj_xbc(t_, 509, 3))))(t),
                incs=[(sm, 1)])
            b.op(eng, (lambda t_: (lambda e: e.tensor_scalar(
                out=xs_tile(t_), in0=proj_xbc_halo(t_)[:, 0:512],
                scalar1=convw[:, t_ * KC:t_ * KC + 1], scalar2=None, op0=ALU.mult)))(t),
                waits=list(M_xsy_free) + [IP_PE_END], incs=[(sm, 1)])
            for k in range(1, KC):
                b.op(eng, (lambda t_, k_: (lambda e: e.scalar_tensor_tensor(
                    out=xs_tile(t_), in0=proj_xbc_halo(t_)[:, k_:k_ + 512],
                    scalar=convw[:, t_ * KC + k_:t_ * KC + k_ + 1], in1=xs_tile(t_),
                    op0=ALU.mult, op1=ALU.add)))(t, k),
                    incs=[(sm, 1)])
            b.op("scalar", (lambda t_: (lambda e: e.activation(
                out=xs_tile(t_), in_=xs_tile(t_), func=AF.Silu, bias=convb[:, t_:t_ + 1])))(t),
                waits=[(sm, sm.n)], incs=[(s_sc, 1)])
            cv_marks[t] = (s_sc, s_sc.n)
        CV_ALL = (s_sc, s_sc.n)
        b.dma("sync", cc_buf, xs_tile(32)[64:128, :], s_d2, waits=[CV_ALL])
        D2_CC = s_d2.n

        if stop_at("dt"):
            continue
        b.op("vector", lambda e: e.tensor_scalar(out=dt_sp, in0=proj_dtile(),
                                                 scalar1=abias, scalar2=None, op0=ALU.add),
             waits=[ip_ev[65]] + IP_ALL, incs=[(s_v, 1)])
        V_DT0 = s_v.n
        b.op("scalar", lambda e: e.activation(out=dA_t, in_=proj_dtile(), func=AF.Abs,
                                              bias=abias),
             waits=[ip_ev[65]] + IP_ALL, incs=[(s_sc, 1)])
        SC_ABS = s_sc.n
        b.op("scalar", lambda e: e.activation(out=dA_t, in_=dA_t, func=AF.Exp,
                                              scale=ccol("neg1", 0, 64)))
        b.op("scalar", lambda e: e.activation(out=dA_t, in_=dA_t, func=AF.Ln,
                                              bias=ccol("one", 0, 64)), incs=[(s_sc, 1)])
        SC_DT = s_sc.n
        b.op("vector", lambda e: e.tensor_scalar(out=dt_sp, in0=dt_sp, scalar1=0.0,
                                                 scalar2=None, op0=ALU.max))
        b.op("vector", lambda e: e.tensor_tensor(out=dt_sp, in0=dt_sp, in1=dA_t, op=ALU.add),
             waits=[(s_sc, SC_DT)])
        b.op("vector", lambda e: e.tensor_copy(out=dt_spb, in_=dt_sp))
        b.op("vector", lambda e: e.tensor_scalar(out=dA_t, in0=dt_sp, scalar1=acol,
                                                 scalar2=None, op0=ALU.mult), incs=[(s_v, 1)])
        for c in range(NCHUNK):
            b.op("vector", (lambda c_: (lambda e: e.tensor_tensor_scan(
                out=acum[:, c_ * CH:(c_ + 1) * CH], data0=ones64,
                data1=dA_t[:, c_ * CH:(c_ + 1) * CH], initial=0.0,
                op0=ALU.mult, op1=ALU.add)))(c), incs=[(s_v, 1)])
        V_ACUM = s_v.n

        if stop_at("ssd"):
            continue
        prev_repl_reads = None
        gate_done = []
        oct_marks = []
        for c in range(NCHUNK):
            csl = slice(c * CH, (c + 1) * CH)
            fw = [(s_v, V_ACUM)]
            if prev_repl_reads is not None:
                fw += prev_repl_reads
            b.dma("sync", s_acum, acum[:, csl], s_d2, waits=fw)
            D2_FLAT = s_d2.n

            b.op("vector", (lambda c_: (lambda e: e.tensor_scalar(
                out=dssub, in0=acum[:, c_ * CH:(c_ + 1) * CH],
                scalar1=acum[:, c_ * CH + CH - 1: c_ * CH + CH],
                scalar2=None, op0=ALU.subtract)))(c), waits=[(s_v, V_ACUM)], incs=[(s_v, 1)])
            b.op("scalar", lambda e: e.activation(out=dshm, in_=dssub, func=AF.Exp,
                                                  scale=ccol("neg1", 0, 64)),
                 waits=[(s_v, s_v.n)], incs=[(s_sc, 1)])
            SC_DS = s_sc.n

            bank = tb_par[0] % 2
            tb_par[0] += 1
            tw = psb_free[bank] + [(s_v, s_v.n), (s_sc, SC_DS)]
            b.op("tensor", (lambda c_, bk_: (lambda e: e.transpose(
                psB[bk_][:, 0:64], dt_spb[:, c_ * CH:(c_ + 1) * CH], idb[0:64, 0:64])))(c, bank),
                waits=tw, incs=[(s_pe, 1)])
            b.op("tensor", (lambda bk_: (lambda e: e.transpose(
                psB[bk_][:, 64:128], dshm, idb[0:64, 0:64])))(bank), incs=[(s_pe, 1)])
            b.op("tensor", (lambda c_, bk_: (lambda e: e.transpose(
                psB[bk_][:, 128:192], xs_tile(32)[0:64, c_ * CH:(c_ + 1) * CH], idb[0:64, 0:64])))(c, bank),
                waits=[CV_ALL], incs=[(s_pe, 1)])
            b.op("tensor", (lambda c_: (lambda e: e.transpose(
                psF[0][:, 0:64], acum[:, c_ * CH:(c_ + 1) * CH], idf[0:64, 0:64])))(c),
                waits=psf_free[0], incs=[(s_pe, 1)])
            b.op("tensor", (lambda c_: (lambda e: e.transpose(
                psF[0][0:1, 64:128], acum[:, c_ * CH + CH - 1:c_ * CH + CH], idf[0:64, 0:64])))(c),
                incs=[(s_pe, 1)])
            PE_TR = s_pe.n
            b.op("vector", (lambda bk_: (lambda e: e.tensor_copy(out=dtt, in_=psB[bk_][:, 0:64])))(bank),
                 waits=[(s_pe, PE_TR)], incs=[(s_v, 1)])
            b.op("vector", (lambda bk_: (lambda e: e.tensor_copy(out=dst, in_=psB[bk_][:, 64:128])))(bank),
                 incs=[(s_v, 1)])
            b.op("vector", (lambda bk_: (lambda e: e.tensor_copy(out=bct, in_=psB[bk_][:, 128:192])))(bank),
                 incs=[(s_v, 1)])
            b.op("vector", lambda e: e.tensor_copy(out=act_t, in_=psF[0][:, 0:64]), incs=[(s_v, 1)])
            b.op("vector", lambda e: e.tensor_copy(out=rowst[0:1, 0:64], in_=psF[0][0:1, 64:128]),
                 incs=[(s_v, 1)])
            V_TR = s_v.n
            psb_free[bank] = [(s_v, V_TR)]
            psf_free[0] = [(s_v, V_TR)]
            b.op("tensor", lambda e: e.matmul(psF[1][0:64, 0:64], onesf[:, 0:64], rowst[0:1, 0:64],
                                              start=True, stop=True),
                 waits=[(s_v, V_TR)] + psf_free[1], incs=[(s_pe, 1)])
            b.op("scalar", lambda e: e.activation(out=cdecr, in_=psF[1][0:64, 0:64], func=AF.Exp),
                 waits=[(s_pe, s_pe.n)], incs=[(s_sc, 1)])
            SC_CDECR = s_sc.n
            b.op("tensor", (lambda c_: (lambda e: e.matmul(
                psF[1][:, 128:256], xs_tile(32)[0:64, c_ * CH:(c_ + 1) * CH],
                cc_buf[:, c_ * CH:(c_ + 1) * CH], start=True, stop=True)))(c),
                waits=[(s_d2, D2_CC), CV_ALL, (s_sc, SC_CDECR)], incs=[(s_pe, 1)])
            PE_CB = s_pe.n
            b.op("vector", lambda e: e.tensor_tensor(out=cbm, in0=psF[1][:, 128:256],
                                                     in1=causal, op=ALU.mult),
                 waits=[(s_pe, PE_CB)], incs=[(s_v, 1)])
            V_CBM = s_v.n
            psf_free[1] = [(s_v, V_CBM)]

            xct_ev = []
            for half in range(4):
                bank = tb_par[0] % 2
                tb_par[0] += 1
                for q in range(8):
                    t = half * 8 + q
                    b.op("tensor", (lambda t_, c_, bk_, q_: (lambda e: e.transpose(
                        psB[bk_][:, q_ * 128:(q_ + 1) * 128],
                        xs_tile(t_)[:, c_ * CH:(c_ + 1) * CH], idb)))(t, c, bank, q),
                        waits=[CV_ALL] + psb_free[bank], incs=[(s_pe, 1)])
                PE_T8 = s_pe.n
                b.op("vector", (lambda half_, bk_: (lambda e: e.tensor_tensor(
                    out=xct.rearrange("p (h f) -> p h f", f=64)[:, half_ * 16:(half_ + 1) * 16, :],
                    in0=psB[bk_].rearrange("p (h f) -> p h f", f=64),
                    in1=dtt[:, half_ * 16:(half_ + 1) * 16].unsqueeze(2).broadcast_to([128, 16, 64]),
                    op=ALU.mult)))(half, bank),
                    waits=[(s_pe, PE_T8), (s_v, V_TR)], incs=[(s_v, 1)])
                psb_free[bank] = [(s_v, s_v.n)]
                xct_ev.append((s_v, s_v.n))
            V_XCT = s_v.n

            psY = [psF[2], psF[3]]
            psS = [psF[4], psF[5]]
            fr = {}

            def front(o):
                par = o % 2
                rw = [(s_d2, D2_FLAT)]
                if len(oct_marks) >= 1:
                    rw += oct_marks[-1]["repl_reads"]
                b.dma("sync", replo[par],
                      s_acum.rearrange("h s -> (h s)").rearrange("(o2 f) -> o2 f", o2=1)[:, o * 1024:(o + 1) * 1024].broadcast_to([128, 1024]),
                      s_d2, waits=rw)
                D2_R = s_d2.n
                msk_w = [(s_d2, D2_R), (s_v, V_TR)]
                if len(oct_marks) >= 1:
                    msk_w.append(oct_marks[-1]["y_pe"])
                b.op("vector", (lambda par_, o_: (lambda e: e.tensor_tensor(
                    out=scoreso[par_].rearrange("p (h l) -> p h l", l=128),
                    in0=replo[par_].rearrange("p (h l) -> p h l", l=128),
                    in1=act_t[:, o_ * 8:(o_ + 1) * 8].unsqueeze(2).broadcast_to([128, 8, 128]),
                    op=ALU.subtract)))(par, o), waits=msk_w, incs=[(s_v, 1)])
                V_SUB = s_v.n
                b.op("vector", (lambda par_: (lambda e: e.tensor_scalar(
                    out=scoreso[par_], in0=scoreso[par_], scalar1=0.0, scalar2=None,
                    op0=ALU.min)))(par), incs=[(s_v, 1)])
                V_MS = s_v.n
                b.op("scalar", (lambda par_: (lambda e: e.activation(
                    out=scoreso[par_], in_=scoreso[par_], func=AF.Exp)))(par),
                    waits=[(s_v, V_MS)], incs=[(s_sc, 1)])
                SC_EXP = s_sc.n
                b.op("vector", (lambda par_: (lambda e: e.tensor_tensor(
                    out=scoreso[par_].rearrange("p (h l) -> p h l", l=128),
                    in0=scoreso[par_].rearrange("p (h l) -> p h l", l=128),
                    in1=cbm.rearrange("p (h2 l) -> p h2 l", h2=1).broadcast_to([128, 8, 128]),
                    op=ALU.mult)))(par),
                    waits=[(s_sc, SC_EXP), (s_v, V_CBM)], incs=[(s_v, 1)])
                V_SCORE = s_v.n
                cd_w = [(s_d2, D2_R)]
                if len(oct_marks) >= 1:
                    cd_w.append(oct_marks[-1]["y_pe"])
                b.op("scalar", (lambda par_: (lambda e: e.activation(
                    out=cdeco[par_], in_=replo[par_][0:64, :], func=AF.Exp)))(par),
                    waits=cd_w, incs=[(s_sc, 1)])
                SC_CD = s_sc.n
                b.op("gpsimd", (lambda par_, c_: (lambda e: e.tensor_tensor(
                    out=cdeco[par_].rearrange("p (h l) -> p h l", l=128),
                    in0=cdeco[par_].rearrange("p (h l) -> p h l", l=128),
                    in1=cc_buf[:, c_ * CH:(c_ + 1) * CH].rearrange("p (h2 l) -> p h2 l", h2=1).broadcast_to([64, 8, 128]),
                    op=ALU.mult)))(par, c),
                    waits=[(s_sc, SC_CD), (s_d2, D2_CC)], incs=[(s_g, 1)])
                G_CD = s_g.n
                bd_w = [(s_v, V_TR)]
                if len(oct_marks) >= 1:
                    bd_w.append(oct_marks[-1]["st_pe"])
                b.op("gpsimd", (lambda par_, o_: (lambda e: e.tensor_tensor(
                    out=bdeco[par_].rearrange("p (h n) -> p h n", n=64),
                    in0=bct.rearrange("p (h2 n) -> p h2 n", h2=1).broadcast_to([128, 8, 64]),
                    in1=dst[:, o_ * 8:(o_ + 1) * 8].unsqueeze(2).broadcast_to([128, 8, 64]),
                    op=ALU.mult)))(par, o), waits=bd_w, incs=[(s_g, 1)])
                G_BD = s_g.n
                cb_w = list(M_carry)
                if len(oct_marks) >= 1:
                    cb_w.append(oct_marks[-1]["y_pe"])
                b.op("vector", (lambda par_, o_: (lambda e: e.tensor_copy(
                    out=carryb[par_], in_=carry[:, o_ * 512:(o_ + 1) * 512])))(par, o),
                    waits=cb_w, incs=[(s_v, 1)])
                V_CBo = s_v.n
                fr[o] = dict(V_SUB=V_SUB, SC_CD=SC_CD, V_SCORE=V_SCORE, G_CD=G_CD,
                             G_BD=G_BD, V_CBo=V_CBo)

            def back(o):
                par = o % 2
                f = fr[o]
                ymm_w = [(s_v, f["V_SCORE"]), (s_v, V_XCT), (s_v, f["V_CBo"]), (s_g, f["G_CD"])]
                if len(gate_done) >= 2:
                    ymm_w.append(gate_done[-2])
                for j2 in range(4):
                    h = 2 * j2
                    habs = o * 8 + h
                    b.op("tensor", (lambda par_, j2_, habs_, h_, f_: (lambda e: e.matmul(
                        psY[par_][0:64, j2_ * 128:(j2_ + 1) * 128],
                        xct[:, habs_ * 64:(habs_ + 1) * 64],
                        scoreso[par_][:, h_ * 128:(h_ + 1) * 128],
                        start=f_, stop=False, skip_group_check=True)))(par, j2, habs, h, j2 == 0),
                        waits=ymm_w, incs=[(s_pe, 1)])
                    b.op("tensor", (lambda par_, j2_, h_: (lambda e: e.matmul(
                        psY[par_][0:64, j2_ * 128:(j2_ + 1) * 128],
                        carryb[par_][:, h_ * 64:(h_ + 1) * 64],
                        cdeco[par_][:, h_ * 128:(h_ + 1) * 128],
                        start=False, stop=False, skip_group_check=True)))(par, j2, h),
                        incs=[(s_pe, 1)])
                    b.op("tensor", (lambda par_, j2_, habs_, h_, f_: (lambda e: e.matmul(
                        psY[par_][64:128, j2_ * 128:(j2_ + 1) * 128],
                        xct[:, (habs_ + 1) * 64:(habs_ + 2) * 64],
                        scoreso[par_][:, (h_ + 1) * 128:(h_ + 2) * 128],
                        start=f_, stop=False, skip_group_check=True,
                        tile_position=(0, 64))))(par, j2, habs, h, j2 == 0),
                        incs=[(s_pe, 1)])
                    b.op("tensor", (lambda par_, j2_, h_, l_: (lambda e: e.matmul(
                        psY[par_][64:128, j2_ * 128:(j2_ + 1) * 128],
                        carryb[par_][:, (h_ + 1) * 64:(h_ + 2) * 64],
                        cdeco[par_][:, (h_ + 1) * 128:(h_ + 2) * 128],
                        start=False, stop=l_, skip_group_check=True,
                        tile_position=(0, 64))))(par, j2, h, j2 == 3),
                        incs=[(s_pe, 1)])
                PE_Y = s_pe.n
                st_w = [(s_g, f["G_BD"]), (s_v, V_XCT)]
                if len(oct_marks) >= 2:
                    st_w.append(oct_marks[-2]["st_rd"])
                for hh in range(8):
                    habs = o * 8 + hh
                    b.op("tensor", (lambda par_, hh_, habs_: (lambda e: e.matmul(
                        psS[par_][0:64, hh_ * 64:(hh_ + 1) * 64],
                        bdeco[par_][:, hh_ * 64:(hh_ + 1) * 64],
                        xct[:, habs_ * 64:(habs_ + 1) * 64],
                        start=(hh_ == 0), stop=(hh_ == 7), skip_group_check=True)))(par, hh, habs),
                        waits=st_w, incs=([(s_pe, 1)] if hh == 7 else []))
                PE_S = s_pe.n
                sg_w = [(s_pe, PE_Y), CV_ALL] + IP_ALL
                if len(gate_done) >= 2:
                    sg_w.append(gate_done[-2])
                b.op("scalar", (lambda o_, c_, par_: (lambda e: e.activation(
                    out=sg[par_].rearrange("p (t l) -> p t l", l=128),
                    in_=proj[:, 0:GATE_COLS].rearrange("p (t f) -> p t f", f=512)[:, o_ * 4:(o_ + 1) * 4, c_ * CH:(c_ + 1) * CH],
                    func=AF.Silu)))(o, c, par), waits=sg_w, incs=[(s_sc, 1)])
                SC_SG = s_sc.n
                b.op("vector", (lambda o_, c_: (lambda e: e.tensor_tensor(
                    out=tmpg.rearrange("p (t l) -> p t l", l=128),
                    in0=xsy.rearrange("p (t f) -> p t f", f=512)[:, o_ * 4:(o_ + 1) * 4, c_ * CH:(c_ + 1) * CH],
                    in1=dcol[:, o_ * 4:(o_ + 1) * 4].unsqueeze(2).broadcast_to([128, 4, 128]),
                    op=ALU.mult)))(o, c),
                    waits=[(s_v, V_XCT)], incs=[(s_v, 1)])
                b.op("vector", (lambda par_: (lambda e: e.tensor_tensor(
                    out=tmpg, in0=tmpg, in1=psY[par_], op=ALU.add)))(par),
                    waits=[(s_pe, PE_Y)], incs=[(s_v, 1)])
                b.op("vector", (lambda o_, c_, par_: (lambda e: e.tensor_tensor(
                    out=xsy.rearrange("p (t f) -> p t f", f=512)[:, o_ * 4:(o_ + 1) * 4, c_ * CH:(c_ + 1) * CH],
                    in0=tmpg.rearrange("p (t l) -> p t l", l=128),
                    in1=sg[par_].rearrange("p (t l) -> p t l", l=128),
                    op=ALU.mult)))(o, c, par),
                    waits=[(s_sc, SC_SG)], incs=[(s_v, 1)])
                gate_done.append((s_v, s_v.n))
                b.op("gpsimd", (lambda o_: (lambda e: e.tensor_tensor(
                    out=carry.rearrange("p (h f) -> p h f", f=64)[:, o_ * 8:(o_ + 1) * 8, :],
                    in0=carry.rearrange("p (h f) -> p h f", f=64)[:, o_ * 8:(o_ + 1) * 8, :],
                    in1=cdecr[:, o_ * 8:(o_ + 1) * 8].unsqueeze(2).broadcast_to([64, 8, 64]),
                    op=ALU.mult)))(o),
                    waits=[(s_v, f["V_CBo"]), (s_sc, SC_CDECR)], incs=[(s_g, 1)])
                G_MUL = s_g.n
                b.op("vector", (lambda o_, par_: (lambda e: e.tensor_tensor(
                    out=carry[:, o_ * 512:(o_ + 1) * 512], in0=carry[:, o_ * 512:(o_ + 1) * 512],
                    in1=psS[par_][0:64, :], op=ALU.add)))(o, par),
                    waits=[(s_g, G_MUL), (s_pe, PE_S)], incs=[(s_v, 1)])
                V_ST = s_v.n
                oct_marks.append({"repl_reads": [(s_v, f["V_SUB"]), (s_sc, f["SC_CD"])],
                                  "y_pe": (s_pe, PE_Y), "st_pe": (s_pe, PE_S),
                                  "st_rd": (s_v, V_ST)})

            for o in range(8):
                front(o)
                if o >= 1:
                    back(o - 1)
            back(7)
            M_carry = [(s_v, s_v.n)]
            psf_free[2] = [gate_done[-2]]
            psf_free[3] = [gate_done[-1]]
            psf_free[4] = [oct_marks[-2]["st_rd"]]
            psf_free[5] = [oct_marks[-1]["st_rd"]]
            prev_repl_reads = oct_marks[-1]["repl_reads"] + oct_marks[-2]["repl_reads"]
        GATE_ALL = [(s_v, s_v.n)]

        if stop_at("op"):
            continue
        for t in range(NT_GATE):
            b.op("scalar", (lambda t_: (lambda e: e.activation(
                out=y2win[:, t_ * 512:(t_ + 1) * 512], in_=xs_tile(t_), func=AF.Square)))(t),
                waits=GATE_ALL + list(M_carry), incs=[(s_sc, 1)])
        SC_Y2 = s_sc.n
        for t in range(NT_GATE):
            mmw = [(s_sc, SC_Y2)]
            if t == 0:
                mmw += psf_free[0]
            b.op("tensor", (lambda t_: (lambda e: e.matmul(
                psF[0][0:1, :], onesb, y2win[:, t_ * 512:(t_ + 1) * 512],
                start=(t_ == 0), stop=(t_ == NT_GATE - 1))))(t),
                waits=mmw, incs=([(s_pe, 1)] if t == NT_GATE - 1 else []))
        PE_YS = s_pe.n
        b.op("scalar", lambda e: e.activation(out=rowst[0:1, :], in_=psF[0][0:1, :],
                                              func=AF.Ln, scale=ccol("inv4096", 0, 1), bias=ccol("eps", 0, 1)),
             waits=[(s_pe, PE_YS)])
        b.op("scalar", lambda e: e.activation(out=rowst[0:1, :], in_=rowst[0:1, :], func=AF.Exp,
                                              scale=ccol("mhalf", 0, 1), bias=ccol("ln05", 0, 1)),
             incs=[(s_sc, 1)])
        SC_RY = s_sc.n
        psf_free[0] = [(s_sc, SC_RY)]
        for i4 in range(4):
            b.op("tensor", (lambda i_: (lambda e: e.transpose(
                psF[0][:, i_:i_ + 1], rowst[0:1, i_ * 128:(i_ + 1) * 128], idf[0:1, 0:1])))(i4),
                waits=[(s_sc, SC_RY)] + (psf_free[0] if i4 == 0 else []), incs=[(s_pe, 1)])
        b.op("vector", lambda e: e.tensor_copy(out=cols[:, 8:12], in_=psF[0][:, 0:4]),
             waits=[(s_pe, s_pe.n)], incs=[(s_v, 1)])
        psf_free[0] = [(s_v, s_v.n)]
        RY_COLS = (s_v, s_v.n)

        op_mk = {}
        for pb in range(8):
            for kq in range(4):
                gstrip = pb * 4 + kq
                par = gstrip % 2
                waits = [(s_sc, SC_Y2)]
                if gstrip >= 2:
                    waits.append(op_mk[gstrip - 2])
                b.dma("sync", wbuf[par], s_wop[gstrip], s_w, waits=waits)
                W_OP = s_w.n
                for j in range(2):
                    bankF = 2 + (pb % 2) * 2 + j
                    for k8 in range(8):
                        kt = kq * 8 + k8
                        mmw = [(s_w, W_OP)]
                        if kt == 0:
                            mmw += psf_free[bankF]
                        b.op("tensor", (lambda par_, j_, k8_, kt_, bankF_: (lambda e: e.matmul(
                            psF[bankF_], wbuf[par_][:, k8_ * 256 + j_ * 128: k8_ * 256 + j_ * 128 + 128],
                            xs_tile(kt_), start=(kt_ == 0), stop=(kt_ == 31))))(par, j, k8, kt, bankF),
                            waits=mmw, incs=([(s_pe, 1)] if (k8 == 7 and j == 1) else []))
                op_mk[gstrip] = (s_pe, s_pe.n)
            PE_OP2 = s_pe.n
            for j in range(2):
                tI = pb * 2 + j
                bankF = 2 + (pb % 2) * 2 + j
                stg = a6[:, j * 512:(j + 1) * 512]
                b.op("vector", (lambda stg_, bf_: (lambda e: e.tensor_copy(out=stg_, in_=psF[bf_])))(stg, bankF),
                     waits=[(s_pe, PE_OP2)], incs=[(s_v, 1)])
                V_STG = s_v.n
                psf_free[bankF] = [(s_v, V_STG)]

                def evac_z1(key, slot_ap, pe_mark, tI_=tI):
                    a_ = key
                    b.op("vector", (lambda a2, t2, sl: (lambda e: e.scalar_tensor_tensor(
                        out=x_tm[:, a2 * HD + t2 * 128: a2 * HD + (t2 + 1) * 128],
                        in0=sl, scalar=cols[:, 8 + a2:9 + a2],
                        in1=x_tm[:, a2 * HD + t2 * 128: a2 * HD + (t2 + 1) * 128],
                        op0=ALU.mult, op1=ALU.add)))(a_, tI_, slot_ap),
                        waits=[pe_mark, RY_COLS], incs=[(s_v, 1)])
                    return (s_v, s_v.n)

                blocks = [(stg[:, a * 128:(a + 1) * 128], a) for a in range(4)]
                tpose_rounds(blocks, [(s_v, V_STG)], evac_z1, group=4)
        Z1_DONE = s_v.n

        if stop_at("ln1"):
            continue
        for a in range(4):
            b.op("scalar", (lambda a_: (lambda e: e.activation(
                out=exitst, in_=x_tm[:, a_ * HD:(a_ + 1) * HD], func=AF.Square,
                accum_out=cols[:, 28 + a_:29 + a_])))(a),
                waits=[(s_v, Z1_DONE)] + list(M_exitst), incs=[(s_sc, 1)])
            b.op("scalar", (lambda a_: (lambda e: e.activation(
                out=exitst, in_=x_tm[:, a_ * HD:(a_ + 1) * HD], func=AF.Copy,
                accum_out=cols[:, 16 + a_:17 + a_])))(a), incs=[(s_sc, 1)])
        SC_LN1S = s_sc.n
        M_exitst = [(s_sc, SC_LN1S)]
        b.op("scalar", lambda e: e.mul(out=cols[:, 16:20], in_=cols[:, 16:20], mul=ccol("inv2048")))
        b.op("scalar", lambda e: e.mul(out=cols[:, 28:32], in_=cols[:, 28:32], mul=ccol("inv2048")))
        b.op("scalar", lambda e: e.activation(out=cols[:, 20:24], in_=cols[:, 16:20], func=AF.Square))
        b.op("scalar", lambda e: e.mul(out=cols[:, 20:24], in_=cols[:, 20:24], mul=-1.0))
        for a4 in range(4):
            b.op("scalar", (lambda a_: (lambda e: e.add(
                out=cols[:, 12 + a_:13 + a_], in_=cols[:, 28 + a_:29 + a_],
                add=cols[:, 20 + a_:21 + a_])))(a4))
        b.op("scalar", lambda e: e.activation(out=cols[:, 12:16], in_=cols[:, 12:16], func=AF.Ln,
                                              bias=ccol("eps")))
        b.op("scalar", lambda e: e.activation(out=cols[:, 12:16], in_=cols[:, 12:16], func=AF.Exp,
                                              scale=ccol("mhalf")), incs=[(s_sc, 1)])
        SC_R1 = s_sc.n
        for a in range(4):
            b.op("vector", (lambda a_: (lambda e: e.tensor_scalar(
                out=x1_tm[:, a_ * HD:(a_ + 1) * HD], in0=x_tm[:, a_ * HD:(a_ + 1) * HD],
                scalar1=cols[:, 16 + a_:17 + a_], scalar2=cols[:, 12 + a_:13 + a_],
                op0=ALU.subtract, op1=ALU.mult)))(a),
                waits=[(s_sc, SC_R1)], incs=[(s_v, 1)])
        X1_DONE = s_v.n
        M_xtm_free = [(s_v, X1_DONE)]

        if stop_at("f1"):
            continue
        f1_first = GATE_ALL + [(s_v, X1_DONE), (s_sc, SC_Y2)]
        for a in range(4):
            def evac_ub(key, slot_ap, pe_mark, a_=a):
                i_ = key
                b.op("vector", (lambda i2, a2, sl: (lambda e: e.tensor_scalar(
                    out=a5[:, i2 * 512 + a2 * 128: i2 * 512 + (a2 + 1) * 128],
                    in0=sl, scalar1=su[:, i2:i2 + 1], scalar2=None, op0=ALU.mult)))(i_, a_, slot_ap),
                    waits=[pe_mark], incs=[(s_v, 1)])
                return (s_v, s_v.n)

            blocks = [(x1_tm[:, a * HD + i * 128: a * HD + (i + 1) * 128], i) for i in range(16)]
            tpose_rounds(blocks, f1_first, evac_ub, group=8)
        UB_RAW = s_v.n

        if stop_at("f2"):
            continue
        for st in range(4):
            bit = 1 << st
            srcbuf = a5 if st % 2 == 0 else ub1
            dstbuf = ub1 if st % 2 == 0 else a5
            stm = s_v.n
            gm = s_g.n
            extra = [(s_v, UB_RAW)] if st == 0 else []
            for i in range(16):
                if i & bit:
                    continue
                jp = i | bit
                for (oi, sgn) in ((i, ALU.add), (jp, ALU.subtract)):
                    eng = "vector" if (oi % 2 == 0) else "gpsimd"
                    sm = s_v if eng == "vector" else s_g
                    b.op(eng, (lambda sb_, db_, i_, jp_, oi_, sg2: (lambda e: e.tensor_tensor(
                        out=db_[:, oi_ * 512:(oi_ + 1) * 512],
                        in0=sb_[:, i_ * 512:(i_ + 1) * 512],
                        in1=sb_[:, jp_ * 512:(jp_ + 1) * 512],
                        op=sg2)))(srcbuf, dstbuf, i, jp, oi, sgn),
                        waits=[(s_v, stm), (s_g, gm)] + extra, incs=[(sm, 1)])
        b.op("vector", lambda e: e.engine_nop(), waits=[(s_v, s_v.n), (s_g, s_g.n)], incs=[(s_v, 1)])
        UB_DONE = s_v.n

        if stop_at("ffn3"):
            continue
        f3_mk = {}
        sidx = [0]
        for pb in range(32):
            for wsel, wsrc in enumerate([s_wup, s_wgt]):
                for kh in range(2):
                    gstrip = sidx[0]
                    sidx[0] += 1
                    par = gstrip % 4
                    waits = [(s_v, UB_DONE)]
                    if gstrip >= 4:
                        waits.append(f3_mk[gstrip - 4])
                    b.dma("sync", wbuf4[par], wsrc[pb * 2 + kh], s_w, waits=waits)
                    W_F3 = s_w.n
                    for j in range(2):
                        if wsel == 0:
                            bankF = 2 + (pb % 2) * 2 + j
                        else:
                            bankF = 0 + j
                        for k8 in range(8):
                            kt = kh * 8 + k8
                            mmw = [(s_w, W_F3)]
                            if kt == 0:
                                mmw += psf_free[bankF]
                            b.op("tensor", (lambda par_, j_, k8_, kt_, bankF_: (lambda e: e.matmul(
                                psF[bankF_], wbuf4[par_][:, k8_ * 256 + j_ * 128: k8_ * 256 + j_ * 128 + 128],
                                a5[:, kt_ * 512:(kt_ + 1) * 512],
                                start=(kt_ == 0), stop=(kt_ == 15))))(par, j, k8, kt, bankF),
                                waits=mmw, incs=([(s_pe, 1)] if (k8 == 7 and j == 1) else []))
                    f3_mk[gstrip] = (s_pe, s_pe.n)
            PE_UG = s_pe.n
            for j in range(2):
                tI = pb * 2 + j
                bank_u = 2 + (pb % 2) * 2 + j
                bank_g = 0 + j
                b.op("vector", (lambda tI_, bu_, j_: (lambda e: e.tensor_scalar(
                    out=usg[:, j_ * 1024:j_ * 1024 + 512], in0=psF[bu_],
                    scalar1=cu[:, tI_:tI_ + 1], scalar2=None, op0=ALU.mult)))(tI, bank_u, j),
                    waits=[(s_pe, PE_UG)], incs=[(s_v, 1)])
                psf_free[bank_u] = [(s_v, s_v.n)]
                b.op("scalar", (lambda tI_, bg_, j_: (lambda e: e.activation(
                    out=usg[:, j_ * 1024 + 512:j_ * 1024 + 1024], in_=psF[bg_], func=AF.Silu,
                    scale=cg[:, tI_:tI_ + 1])))(tI, bank_g, j),
                    waits=[(s_pe, PE_UG)], incs=[(s_sc, 1)])
                psf_free[bank_g] = [(s_sc, s_sc.n)]
                b.op("vector", (lambda tI_, j_: (lambda e: e.tensor_tensor(
                    out=proj[:, tI_ * 512:(tI_ + 1) * 512], in0=usg[:, j_ * 1024:j_ * 1024 + 512],
                    in1=usg[:, j_ * 1024 + 512:j_ * 1024 + 1024], op=ALU.mult)))(tI, j),
                    waits=[(s_v, s_v.n), (s_sc, s_sc.n)], incs=[(s_v, 1)])
        V_FFNV = s_v.n

        if stop_at("ffn"):
            continue
        f4_mk = {}
        for pb in range(8):
            for kq in range(8):
                gstrip = pb * 8 + kq
                par = gstrip % 4
                waits = [(s_v, V_FFNV)]
                if gstrip >= 4:
                    waits.append(f4_mk[gstrip - 4])
                b.dma("sync", wbuf4[par], s_wdn[pb * 8 + kq], s_w, waits=waits)
                W_D = s_w.n
                for j in range(2):
                    bankF = 2 + (pb % 2) * 2 + j
                    for k8 in range(8):
                        kt = kq * 8 + k8
                        mmw = [(s_w, W_D)]
                        if kt == 0:
                            mmw += psf_free[bankF]
                        b.op("tensor", (lambda par_, j_, k8_, kt_, bankF_: (lambda e: e.matmul(
                            psF[bankF_], wbuf4[par_][:, k8_ * 256 + j_ * 128: k8_ * 256 + j_ * 128 + 128],
                            proj[:, kt_ * 512:(kt_ + 1) * 512],
                            start=(kt_ == 0), stop=(kt_ == 63))))(par, j, k8, kt, bankF),
                            waits=mmw, incs=([(s_pe, 1)] if (k8 == 7 and j == 1) else []))
                f4_mk[gstrip] = (s_pe, s_pe.n)
            PE_DN = s_pe.n
            for j in range(2):
                tI = pb * 2 + j
                bankF = 2 + (pb % 2) * 2 + j
                stg = a6[:, j * 512:(j + 1) * 512]
                b.op("vector", (lambda stg_, bf_: (lambda e: e.tensor_copy(out=stg_, in_=psF[bf_])))(stg, bankF),
                     waits=[(s_pe, PE_DN)], incs=[(s_v, 1)])
                V_STG = s_v.n
                psf_free[bankF] = [(s_v, V_STG)]

                def evac_z(key, slot_ap, pe_mark, tI_=tI):
                    a_ = key
                    b.op("vector", (lambda a2, t2, sl: (lambda e: e.tensor_tensor(
                        out=zfull[:, a2 * HD + t2 * 128: a2 * HD + (t2 + 1) * 128],
                        in0=sl, in1=x1_tm[:, a2 * HD + t2 * 128: a2 * HD + (t2 + 1) * 128],
                        op=ALU.add)))(a_, tI_, slot_ap),
                        waits=[pe_mark], incs=[(s_v, 1)])
                    return (s_v, s_v.n)

                blocks = [(stg[:, a * 128:(a + 1) * 128], a) for a in range(4)]
                tpose_rounds(blocks, [(s_v, V_STG)], evac_z, group=4)
        Z_DONE = s_v.n

        for a in range(4):
            zt = zfull[:, a * HD:(a + 1) * HD]
            ew = [(s_v, Z_DONE), (s_out, M_out_exit)] + list(M_exitst)
            b.op("scalar", (lambda zt_: (lambda e: e.activation(
                out=exitst, in_=zt_, func=AF.Square, accum_out=cols[:, 24:25])))(zt),
                waits=ew, incs=[(s_sc, 1)])
            b.op("scalar", (lambda zt_: (lambda e: e.activation(
                out=exitst, in_=zt_, func=AF.Copy, accum_out=cols[:, 25:26])))(zt),
                incs=[(s_sc, 1)])
            b.op("scalar", lambda e: e.mul(out=cols[:, 24:26], in_=cols[:, 24:26], mul=ccol("inv2048")))
            b.op("scalar", lambda e: e.activation(out=cols[:, 26:27], in_=cols[:, 25:26], func=AF.Square))
            b.op("scalar", lambda e: e.mul(out=cols[:, 26:27], in_=cols[:, 26:27], mul=-1.0))
            b.op("scalar", lambda e: e.add(out=cols[:, 24:25], in_=cols[:, 24:25], add=cols[:, 26:27]))
            b.op("scalar", lambda e: e.activation(out=cols[:, 24:25], in_=cols[:, 24:25], func=AF.Ln,
                                                  bias=ccol("eps")))
            b.op("scalar", lambda e: e.activation(out=cols[:, 24:25], in_=cols[:, 24:25], func=AF.Exp,
                                                  scale=ccol("mhalf")), incs=[(s_sc, 1)])
            SC_LN2 = s_sc.n
            b.op("vector", (lambda zt_: (lambda e: e.tensor_scalar(
                out=exitst, in0=zt_, scalar1=cols[:, 25:26], scalar2=cols[:, 24:25],
                op0=ALU.subtract, op1=ALU.mult)))(zt),
                waits=[(s_sc, SC_LN2)], incs=[(s_v, 1)])
            V_OUT = s_v.n
            b.dma("sync", d_out[seq, tok0 + a * 128: tok0 + (a + 1) * 128, :], exitst, s_out,
                  waits=[(s_v, V_OUT)])
            M_out_exit = s_out.n
            M_exitst = [(s_out, M_out_exit)]

        M_proj_free = [(s_pe, s_pe.n)]
        M_a5_free = [(s_pe, s_pe.n)]
        M_xsy_free = [(s_v, Z_DONE), (s_pe, s_pe.n)]
        if first_of_seq is False and g + 1 < NGRP and (g + 1) % (NGRP_FULL // SEQ) == 0:
            pass
        if g + 1 < NGRP and (g + 1) % (NGRP_FULL // SEQ) == 0:
            b.op("gpsimd", lambda e: e.memset(carry, 0.0),
                 waits=list(M_carry) + [(s_pe, s_pe.n)], incs=[(s_g, 1)])
            M_carry = [(s_g, s_g.n)]

    local_map = dict(locals())
    for nm in cfg.get("dumps", []):
        ap = local_map[nm]
        dd = nc.dram_tensor("dump_" + nm, list(ap.shape), ap.dtype, kind="ExternalOutput").ap()
        b.dma("sync", dd, ap, s_out,
              waits=[(s_v, s_v.n), (s_sc, s_sc.n), (s_pe, s_pe.n), (s_g, s_g.n),
                     (s_w, s_w.n), (s_io, s_io.n), (s_d2, s_d2.n)])

    b.op("sync", lambda e: e.nop(), waits=[(s_out, s_out.n)])

    import contextlib
    with contextlib.ExitStack() as ctx:
        for s in b.sems:
            s.h = ctx.enter_context(nc.semaphore(s.name))
        block = ctx.enter_context(nc.Block())
        b.emit(block)
    return nc



def _stage(inputs):
    ip = {k: np.asarray(v) for k, v in inputs.items()}
    st = {}
    win = np.ascontiguousarray(ip["L0_attn_in_proj_w"].T).astype(np.float32, copy=False)
    win_pad = np.zeros((HD, PROJ_PAD), np.float32)
    win_pad[:, :PROJ] = win
    st["w_in_T"] = win_pad
    st["w_op_T"] = np.ascontiguousarray(ip["L0_attn_out_proj_w"].T)
    st["w_up_T"] = np.ascontiguousarray(ip["L0_ffn_W_up"].T)
    st["w_gt_T"] = np.ascontiguousarray(ip["L0_ffn_W_gate"].T)
    st["w_dn_T"] = np.ascontiguousarray(ip["L0_ffn_W_down"].T)
    st["convw_in"] = np.ascontiguousarray(
        ip["L0_attn_conv_w"].reshape(NT_XBC, 128, KC).transpose(1, 0, 2).reshape(128, NT_XBC * KC))
    st["convb_in"] = np.ascontiguousarray(ip["L0_attn_conv_b"].reshape(NT_XBC, 128).T)
    st["lnw_cols"] = np.ascontiguousarray(ip["L0_attn_ln_weight"].reshape(16, 128).T)
    st["normw_cols"] = np.ascontiguousarray(ip["L0_attn_norm_weight"].reshape(32, 128).T)
    st["su_cols"] = np.ascontiguousarray(ip["L0_ffn_SU_ug"].reshape(16, 128).T)
    st["sudp_cols"] = np.ascontiguousarray(ip["L0_ffn_SU_dp"].reshape(64, 128).T)
    st["wsu_up"] = np.ascontiguousarray(ip["L0_ffn_Wscale_ug"][:INTER].reshape(64, 128).T)
    st["wsu_gt"] = np.ascontiguousarray(ip["L0_ffn_Wscale_ug"][INTER:].reshape(64, 128).T)
    st["wsdp_rep"] = np.ascontiguousarray(np.broadcast_to(ip["L0_ffn_Wscale_dp"][None, :], (64, HD)))
    st["d_cols_in"] = np.ascontiguousarray(np.repeat(ip["L0_attn_D"], P).reshape(32, 128).T)
    st["dt_bias"] = np.ascontiguousarray(ip["L0_attn_dt_bias"].reshape(NH, 1))
    st["a_log"] = np.ascontiguousarray(ip["L0_attn_A_log"].reshape(NH, 1))
    st["h128"] = _hadamard(128)
    st["h64"] = _hadamard(64)
    st["causalT"] = np.ascontiguousarray(np.triu(np.ones((128, 128), np.float32)))
    return st


def kernel(**inputs):
    cfg = {}
    nc = build_program(cfg)
    shared = _stage(inputs)
    x_full = np.asarray(inputs["L0_attn_hidden_states"])
    in_maps = []
    for c in range(8):
        m = dict(shared)
        m["x"] = np.ascontiguousarray(x_full[2 * c:2 * c + 2])
        in_maps.append(m)
    res = run_bass_kernel_spmd(nc, in_maps, core_ids=list(range(8)))
    out = np.concatenate([np.asarray(r["out"]) for r in res.results], axis=0)
    return out
